# version 02
# baseline (speedup 1.0000x reference)
import numpy as np
import ml_dtypes
import concourse.bass as bass
import concourse.mybir as mybir
from concourse.bass_utils import run_bass_kernel_spmd
from concourse.aot_env import allow_ici_all_to_all
from contextlib import ExitStack

F32 = mybir.dt.float32
BF16 = mybir.dt.bfloat16
ALU = mybir.AluOpType
ACT = mybir.ActivationFunctionType
AXX = mybir.AxisListType.X

NCORE = 8
D = 1024
T = 16384
TL = T // NCORE
NCH = 8
TT1 = 256
NT1 = TL // TT1
LN_EPS = 1e-5

_UPTO = 4
_DBG = None


class Bld:
    ENGS = ("sync", "gpsimd", "tensor", "scalar", "vector")

    def __init__(self, nc, block, sems):
        self.nc, self.block = nc, block
        self.sems = sems
        self.cnt = {k: 0 for k in sems}
        self.pend = {e: [] for e in self.ENGS}

    def flush(self):
        for e in self.ENGS:
            lst = self.pend[e]
            if not lst:
                continue
            self.pend[e] = []
            getattr(self.block, e)(lambda eng, lst=lst: [f(eng) for f in lst])

    def _p(self, eng, fn):
        self.pend[eng].append(fn)

    def _wrap_inc(self, eng, sem, n=1):
        self.cnt[sem] += n
        h = self.sems[sem]
        f = self.pend[eng][-1]
        self.pend[eng][-1] = lambda e, f=f, h=h, n=n: f(e).then_inc(h, n)
        return self.cnt[sem]

    def wait(self, eng, sem, val=None):
        if val is None:
            val = self.cnt[sem]
        if val <= 0:
            return
        assert val <= self.cnt[sem], (sem, val, self.cnt[sem])
        h = self.sems[sem]
        self._p(eng, lambda e, h=h, v=val: e.wait_ge(h, v))

    def dma(self, q, out, in_):
        eng = "sync" if q == "ql" else "gpsimd"
        self._p(eng, lambda e, o=out, i=in_: e.dma_start(out=o, in_=i))
        return self._wrap_inc(eng, q, 16)

    def mm(self, out, lhsT, rhs, start, stop, inc=None):
        self._p("tensor", lambda e, o=out, l=lhsT, r=rhs, s0=start, s1=stop:
                e.matmul(o, l, r, start=s0, stop=s1))
        if inc:
            return self._wrap_inc("tensor", inc)

    def act(self, out, in_, func, bias=0.0, scale=1.0, inc=None):
        self._p("scalar", lambda e, o=out, i=in_, f=func, b=bias, s=scale:
                e.activation(out=o, in_=i, func=f, bias=b, scale=s))
        if inc:
            return self._wrap_inc("scalar", inc)

    def v(self, meth, *args, inc=None, **kw):
        self._p("vector", lambda e, m=meth, a=args, k=kw: getattr(e, m)(*a, **k))
        try:
            fs = args[0].free_size() if hasattr(args[0], "free_size") else 999
        except Exception:
            fs = 999
        r = None
        if inc:
            r = self._wrap_inc("vector", inc)
        if fs <= 64 and meth != "memset":
            for _ in range(4):
                self._p("vector", lambda e: e.engine_nop())
        return r

    def g(self, meth, *args, inc=None, **kw):
        self._p("gpsimd", lambda e, m=meth, a=args, k=kw: getattr(e, m)(*a, **k))
        if inc:
            return self._wrap_inc("gpsimd", inc)

    def barrier(self):
        engs = self.ENGS
        vals = {}
        for e in engs:
            sem = "b_" + e
            h = self.sems[sem]
            self.cnt[sem] += 1
            vals[e] = self.cnt[sem]
            self._p(e, lambda eng, h=h: eng.nop().then_inc(h, 1))
        for e in engs:
            for o in engs:
                if o != e:
                    self.wait(e, "b_" + o, vals[o])


def fm_pack(a):
    tn = a.shape[0]
    return np.ascontiguousarray(
        a.T.reshape(NCH, 128, tn).transpose(1, 0, 2).reshape(128, NCH * tn))


def fm_unpack(a, tn):
    return np.ascontiguousarray(
        a.reshape(128, NCH, tn).transpose(1, 0, 2).reshape(D, tn).T)


def build(upto=4, dbg=None):
    nc = bass.Bass("TRN2", target_bir_lowering=False, debug=False)
    DT = nc.dram_tensor

    x0d = DT("x0", [128, NCH * TL], F32, kind="ExternalInput")
    w1d = DT("w1", [4, NCH, 128, D], F32, kind="ExternalInput")
    b1d = DT("b1", [1, 4 * D], F32, kind="ExternalInput")
    gamd = DT("gam", [128, NCH], F32, kind="ExternalInput")
    betd = DT("bet", [128, NCH], F32, kind="ExternalInput")
    wfd = DT("wf", [3, NCH, 128, D], F32, kind="ExternalInput")
    murd = DT("mur", [128, NCH], F32, kind="ExternalInput")
    mukd = DT("muk", [128, NCH], F32, kind="ExternalInput")
    wqkvd = DT("wqkv", [3, NCH, 128, 128], F32, kind="ExternalInput")
    bqkd = DT("bqk", [128, 2], F32, kind="ExternalInput")
    bvrowd = DT("bvrow", [1, 128], F32, kind="ExternalInput")
    wod = DT("wo", [NCH, 128, D], F32, kind="ExternalInput")
    bod = DT("bo", [128, NCH], F32, kind="ExternalInput")
    maskd = DT("mask", [128, 4 * 512], F32, kind="ExternalInput")
    seld = DT("sel", [128, 8], F32, kind="ExternalInput")
    outd = DT("out", [128, NCH * TL], F32, kind="ExternalOutput")

    zscr = DT("zscr", [128, NCH * TL], F32)
    gscr = DT("gscr", [128, NCH * TL], BF16)
    ag1i = DT("ag1i", [128, 16], F32)
    ag1o = DT("ag1o", [NCORE * 128, 16], F32)
    ag2i = DT("ag2i", [128, 8], F32)
    ag2o = DT("ag2o", [NCORE * 128, 8], F32)
    ag3i = DT("ag3i", [128, 8], F32)
    ag3o = DT("ag3o", [NCORE * 128, 8], F32)
    ag4i = DT("ag4i", [128, NCH * TL], BF16)
    ag4o = DT("ag4o", [NCORE * 128, NCH * TL], BF16)
    qscr = DT("qscr", [128, T], BF16)
    a2ai = DT("a2ai", [NCORE, 128, TL], BF16)
    a2ao = DT("a2ao", [NCORE, 128, TL], BF16)
    rg = [list(range(NCORE))]

    ctx0 = ExitStack()
    sems = {}
    for s in ("ql", "qg", "cc", "mT", "mS", "mV",
              "b_sync", "b_gpsimd", "b_tensor", "b_scalar", "b_vector"):
        sems[s] = ctx0.enter_context(nc.semaphore(s))

    with ctx0, nc.Block() as block:
        bld = Bld(nc, block, sems)
        perm = ExitStack()
        sb = lambda name, shape, dt: perm.enter_context(nc.sbuf_tensor("s_" + name, shape, dt))
        xres = sb("xres", [128, NCH * TL], F32)
        onescol = sb("onescol", [128, 1], BF16)
        onesrow = sb("onesrow", [1, 512], BF16)
        ones2k = sb("ones2k", [128, TT1], F32)
        cst = sb("cst", [128, 8], F32)
        b1row = sb("b1row", [1, 4 * D], BF16)
        gam = sb("gam", [128, NCH], F32)
        bet = sb("bet", [128, NCH], F32)
        mur = sb("mur", [128, NCH], F32)
        muk = sb("muk", [128, NCH], F32)
        selm = sb("selm", [128, 8], F32)
        hcar = sb("hcar", [128, 8], F32)
        fcar = sb("fcar", [128, 8], F32)
        h_in = sb("h_in", [128, 8], F32)
        halocar = sb("halocar", [128, 8], F32)
        haloc = sb("haloc", [128, 8], F32)
        carrs = sb("carrs", [128, 8, 16], F32)
        ctmp = sb("ctmp", [128, 24], F32)
        srow = sb("srow", [1, 3072], F32)
        browbf = sb("browbf", [1, 2048], BF16)
        psA = perm.enter_context(nc.psum_tensor("psA", [128, 2048], F32))
        psB = perm.enter_context(nc.psum_tensor("psB", [128, 2048], F32))

        bld.dma("ql", xres[:, :], x0d[:, :])
        bld.dma("ql", gam[:, :], gamd[:, :])
        bld.dma("ql", bet[:, :], betd[:, :])
        bld.dma("ql", mur[:, :], murd[:, :])
        bld.dma("ql", muk[:, :], mukd[:, :])
        bld.dma("ql", selm[:, :], seld[:, :])
        b1v = bld.dma("qg", b1row[:, :], b1d[:, :])
        ld0 = bld.cnt["ql"]
        bld.v("memset", onescol[:, :], 1.0)
        bld.v("memset", onesrow[:, :], 1.0)
        bld.v("memset", ones2k[:, :], 1.0)
        bld.v("memset", hcar[:, :], 0.0)
        bld.v("memset", fcar[:, :], 1.0)
        bld.v("memset", cst[:, 0:1], LN_EPS)
        bld.v("memset", cst[:, 1:2], 1.0 / D)
        bld.v("memset", cst[:, 2:3], -1.0)
        bld.v("memset", cst[:, 3:4], -4.0)
        bld.v("memset", cst[:, 4:5], 0.125)
        bld.v("memset", cst[:, 5:6], 0.0)
        bld.v("memset", cst[:, 6:7], -0.5)
        bld.wait("vector", "ql", ld0)
        bld.wait("tensor", "qg", b1v)

        def ap3(buf, pstep, cstep, n, off=0, nc_=NCH):
            return bass.AP(buf, off, [[pstep, 128], [cstep, nc_], [1, n]])

        def xres_t(tt, n=TT1):
            return ap3(xres, NCH * TL, TL, n, off=tt * n)

        def bc3(col_off, n=TT1):
            return bass.AP(psB, col_off, [[2048, 128], [0, NCH], [1, n]])

        def v3(buf, n=TT1, pstep=2048):
            return ap3(buf, pstep, n, n)

        Abc_off, Bbc_off = 1024, 1280

        def ln_stats(src_of_ci, xb, sq, n=TT1):
            for ci in range(NCH):
                bld.v("tensor_copy", xb[:, ci * n:(ci + 1) * n], src_of_ci(ci))
            for ci in range(NCH):
                vv = bld.v("tensor_mul", sq[:, ci * n:(ci + 1) * n],
                           xb[:, ci * n:(ci + 1) * n], xb[:, ci * n:(ci + 1) * n],
                           inc=("mV" if ci == NCH - 1 else None))
            bld.wait("tensor", "mV", vv)
            for ci in range(NCH):
                bld.mm(psB[0:1, 1536:1536 + n], onescol[:, :], xb[:, ci * n:(ci + 1) * n],
                       ci == 0, ci == NCH - 1)
            for ci in range(NCH):
                tv = bld.mm(psB[0:1, 1792:1792 + n], onescol[:, :], sq[:, ci * n:(ci + 1) * n],
                            ci == 0, ci == NCH - 1,
                            inc=("mT" if ci == NCH - 1 else None))
            bld.wait("vector", "mT", tv)
            m_ = srow[0:1, 2048:2048 + n]
            msq = srow[0:1, 2304:2304 + n]
            bld.v("tensor_scalar", m_, psB[0:1, 1536:1536 + n], cst[0:1, 1:2], None, ALU.mult)
            bld.v("tensor_scalar", msq, psB[0:1, 1792:1792 + n], cst[0:1, 1:2], None, ALU.mult)
            var = srow[0:1, 2560:2560 + n]
            bld.v("tensor_mul", var, m_, m_)
            bld.v("tensor_sub", var, msq, var)
            bld.v("tensor_scalar", var, var, cst[0:1, 0:1], None, ALU.add)
            inv = srow[0:1, 2816:2816 + n]
            sv = bld.v("reciprocal", inv, var, inc="mV")
            bld.wait("scalar", "mV", sv)
            arow = srow[0:1, 0:n]
            av = bld.act(arow, inv, ACT.Sqrt, inc="mS")
            bld.wait("vector", "mS", av)
            t1 = srow[0:1, 2304:2304 + n]
            bld.v("tensor_mul", t1, arow, arow)
            bld.v("tensor_mul", t1, t1, var)
            bld.v("tensor_scalar", t1, t1, cst[0:1, 6:7], 1.5, ALU.mult, ALU.add)
            bld.v("tensor_mul", arow, arow, t1)
            brow = srow[0:1, 256:256 + n]
            bld.v("tensor_mul", brow, m_, arow)
            bld.v("tensor_scalar", brow, brow, cst[0:1, 2:3], None, ALU.mult)
            bcast_rows(arow, brow, n)

        def bcast_rows(arow, brow, n=TT1):
            bld.v("tensor_copy", browbf[0:1, 0:n], arow)
            ahif = srow[0:1, 2560:2560 + n]
            bld.v("tensor_copy", ahif, browbf[0:1, 0:n])
            alo = srow[0:1, 2816:2816 + n]
            bld.v("tensor_sub", alo, arow, ahif)
            bld.v("tensor_copy", browbf[0:1, 256:256 + n], alo)
            vv = bld.v("tensor_copy", browbf[0:1, 512:512 + n], brow, inc="mV")
            bld.wait("tensor", "mV", vv)
            bld.mm(psB[:, Abc_off:Abc_off + n], onesrow[0:1, 0:128], browbf[0:1, 0:n],
                   True, False)
            bld.mm(psB[:, Abc_off:Abc_off + n], onesrow[0:1, 0:128], browbf[0:1, 256:256 + n],
                   False, True)
            tv = bld.mm(psB[:, Bbc_off:Bbc_off + n], onesrow[0:1, 0:128],
                        browbf[0:1, 512:512 + n], True, True, inc="mT")
            bld.wait("vector", "mT", tv)

        st1 = ExitStack()
        sb1 = lambda name, shape, dt: st1.enter_context(nc.sbuf_tensor("s_" + name, shape, dt))
        bF = sb1("bF", [128, 2048], F32)
        bI = sb1("bI", [128, 2048], F32)
        bS = sb1("bS", [128, 2048], F32)
        bT2 = sb1("bT2", [128, 2080], F32)
        xb_ = sb1("xb", [128, 2048], BF16)
        sqn = sb1("sqn", [128, 2048], BF16)
        gbf = sb1("gbf", [128, 2048], BF16)
        a3row = sb1("a3row", [1, TL], F32)
        b3row = sb1("b3row", [1, TL], F32)
        st1b = ExitStack()
        wbuf = st1b.enter_context(nc.sbuf_tensor("s_wbuf", [128, 2, D], BF16))
        bO = st1b.enter_context(nc.sbuf_tensor("s_bO", [128, 2048], F32))

        gate_dst = [psA, psB, psA, psB]
        gate_fun = [ACT.Sigmoid, ACT.Sigmoid, ACT.Sigmoid, ACT.Tanh]
        gate_buf = [bI, bF, bO, bS]

        worder = [(tt, g, ci) for tt in range(NT1) for g in range(4) for ci in range(NCH)]
        wld_v = {}
        wuse_v = {}
        wload_ptr = [0]

        def w1_issue_next():
            i = wload_ptr[0]
            if i >= len(worder):
                return
            wload_ptr[0] += 1
            tt, g, ci = worder[i]
            slot = i % 2
            if i >= 2:
                bld.wait("gpsimd", "mT", wuse_v[i - 2])
            wld_v[i] = bld.dma("qg", wbuf[:, slot, :], w1d[g, ci, :, :])

        w1_issue_next()
        w1_issue_next()

        for tt in range(NT1):
            c0 = tt * TT1
            ln_stats(lambda ci: xres[:, ci * TL + c0: ci * TL + c0 + TT1], xb_, sqn)
            bld.v("tensor_tensor", v3(bT2, pstep=2080), xres_t(tt), bc3(Abc_off), ALU.mult)
            bld.v("tensor_tensor", v3(bT2, pstep=2080), v3(bT2, pstep=2080),
                  bc3(Bbc_off), ALU.add)
            xnv = bld.v("tensor_copy", xb_[:, :], bT2[:, 0:2048], inc="mV")
            bld.wait("tensor", "mV", xnv)
            if dbg == "xn":
                bld.wait("gpsimd", "mV", xnv)
                bld.dma("qg", zscr[:, tt * 2048:(tt + 1) * 2048], bT2[:, 0:2048])

            evac_v = {}
            for g in range(4):
                dst = gate_dst[g]
                for ci in range(NCH):
                    i = worder.index((tt, g, ci))
                    bld.wait("tensor", "qg", wld_v[i])
                    if ci == 0 and g >= 2:
                        bld.wait("tensor", "mS", evac_v[g - 2])
                    slot = i % 2
                    for e in range(NCH):
                        tv = bld.mm(dst[:, e * TT1:(e + 1) * TT1],
                                    wbuf[:, slot, e * 128:(e + 1) * 128],
                                    xb_[:, ci * TT1:(ci + 1) * TT1],
                                    ci == 0 and e % 2 == 0, False,
                                    inc=("mT" if e == NCH - 1 else None))
                    wuse_v[i] = tv
                    w1_issue_next()
                for e in range(NCH):
                    tv = bld.mm(dst[:, e * TT1:(e + 1) * TT1],
                                b1row[0:1, g * D + e * 128: g * D + (e + 1) * 128],
                                onesrow[0:1, 0:TT1], False, True,
                                inc=("mT" if e == NCH - 1 else None))
                bld.wait("scalar", "mT", tv)
                evac_v[g] = bld.act(gate_buf[g][:, :], dst[:, :], gate_fun[g], inc="mS")
            bld.wait("vector", "mS", evac_v[3])

            if dbg == "gatef":
                fdv = bld.v("tensor_copy", bT2[:, 0:2048], bF[:, :], inc="mV")
                bld.wait("gpsimd", "mV", fdv)
                bld.dma("qg", zscr[:, tt * 2048:(tt + 1) * 2048], bT2[:, 0:2048])
            bld.v("tensor_mul", bI[:, :], bI[:, :], bS[:, :])
            for ci in range(NCH):
                bld.v("tensor_tensor_scan", bS[:, ci * TT1:(ci + 1) * TT1],
                      bF[:, ci * TT1:(ci + 1) * TT1], bI[:, ci * TT1:(ci + 1) * TT1],
                      hcar[:, ci:ci + 1], ALU.mult, ALU.add)
            for ci in range(NCH):
                bld.v("tensor_tensor_scan", bI[:, ci * TT1:(ci + 1) * TT1],
                      bF[:, ci * TT1:(ci + 1) * TT1], ones2k[:, 0:TT1],
                      fcar[:, ci:ci + 1], ALU.mult, ALU.mult)
            hS = bass.AP(bS, TT1 - 1, [[2048, 128], [TT1, NCH]])
            hI = bass.AP(bI, TT1 - 1, [[2048, 128], [TT1, NCH]])
            bld.v("tensor_copy", hcar[:, :], hS)
            bld.v("tensor_copy", fcar[:, :], hI)
            if dbg == "scanh":
                sdv = bld.v("tensor_copy", bT2[:, 0:2048], bS[:, :], inc="mV")
                bld.wait("gpsimd", "mV", sdv)
                bld.dma("qg", zscr[:, tt * 2048:(tt + 1) * 2048], bT2[:, 0:2048])
            bld.v("tensor_mul", gbf[:, :], bO[:, :], bI[:, :])
            bld.v("tensor_mul", bO[:, :], bO[:, :], bS[:, :])
            zv = bld.v("tensor_add", bT2[:, 0:2048], bT2[:, 0:2048], bO[:, :], inc="mV")
            bld.wait("gpsimd", "mV", zv)
            bld.dma("qg", zscr[:, tt * 2048:(tt + 1) * 2048], bT2[:, 0:2048])
            stv = bld.dma("qg", gscr[:, tt * 2048:(tt + 1) * 2048], gbf[:, :])
            bld.wait("vector", "qg", stv)
            bld.barrier()

        bld.v("tensor_copy", ctmp[:, 0:8], hcar[:, :])
        cv = bld.v("tensor_copy", ctmp[:, 8:16], fcar[:, :], inc="mV")
        bld.wait("gpsimd", "mV", cv)
        bld.dma("qg", ag1i[:, :], ctmp[:, 0:16])
        bld.wait("gpsimd", "qg")
        bld.g("collective_compute", "AllGather", ALU.bypass,
              replica_groups=rg, ins=[ag1i.ap().opt()], outs=[ag1o.ap().opt()], inc="cc")
        bld.wait("sync", "cc")
        bld.dma("ql", carrs[:, :, :], bass.AP(ag1o, 0, [[16, 128], [128 * 16, 8], [1, 16]]))
        bld.wait("vector", "ql")
        for ci in range(NCH):
            bld.v("tensor_copy", ctmp[:, 0:8], carrs[:, :, 8 + ci])
            bld.v("tensor_copy", ctmp[:, 8:16], carrs[:, :, ci])
            bld.v("tensor_tensor_scan", ctmp[:, 16:24], ctmp[:, 0:8], ctmp[:, 8:16],
                  0.0, ALU.mult, ALU.add)
            bld.v("tensor_mul", ctmp[:, 16:24], ctmp[:, 16:24], selm[:, :])
            bld.v("reduce_sum", h_in[:, ci:ci + 1], ctmp[:, 16:24], axis=AXX)
        bld.barrier()
        if dbg == "hin":
            bld.v("tensor_copy", bT2[:, 0:8], h_in[:, :])
            bld.v("tensor_copy", bT2[:, 8:16], hcar[:, :])
            bld.v("tensor_copy", bT2[:, 16:24], fcar[:, :])
            hv_ = bld.v("tensor_copy", bT2[:, 24:152], carrs[:, 0:8, :], inc="mV")
            bld.wait("gpsimd", "mV", hv_)
            dvv = bld.dma("qg", zscr[:, 0:2048], bT2[:, 0:2048])
            bld.wait("sync", "qg", dvv)
            bld.barrier()

        for tt in range(NT1):
            c0 = tt * TT1
            bld.dma("ql", bT2[:, 0:2048], zscr[:, tt * 2048:(tt + 1) * 2048])
            gv = bld.dma("ql", gbf[:, :], gscr[:, tt * 2048:(tt + 1) * 2048])
            bld.wait("vector", "ql", gv)
            for ci in range(NCH):
                bld.v("scalar_tensor_tensor", bT2[:, ci * TT1:(ci + 1) * TT1],
                      gbf[:, ci * TT1:(ci + 1) * TT1], h_in[:, ci:ci + 1],
                      bT2[:, ci * TT1:(ci + 1) * TT1], ALU.mult, ALU.add)
            ln_stats(lambda ci: bT2[:, ci * TT1:(ci + 1) * TT1], xb_, sqn)
            bld.v("tensor_tensor", v3(bF), v3(bT2, pstep=2080), bc3(Abc_off), ALU.mult)
            bld.v("tensor_tensor", v3(bF), v3(bF), bc3(Bbc_off), ALU.add)
            for ci in range(NCH):
                bld.v("tensor_scalar", bF[:, ci * TT1:(ci + 1) * TT1],
                      bF[:, ci * TT1:(ci + 1) * TT1], gam[:, ci:ci + 1], bet[:, ci:ci + 1],
                      ALU.mult, ALU.add)
            bld.v("tensor_tensor", xres_t(tt), xres_t(tt), v3(bF), ALU.add)
            ln_stats(lambda ci: xres[:, ci * TL + c0: ci * TL + c0 + TT1], xb_, sqn)
            bld.v("tensor_copy", a3row[0:1, c0:c0 + TT1], srow[0:1, 0:TT1])
            bld.v("tensor_copy", b3row[0:1, c0:c0 + TT1], srow[0:1, 256:256 + TT1])
            bld.barrier()

        def halo_ag(arow_, brow_, agi, ago, dest):
            vv = bld.v("tensor_copy", browbf[0:1, 0:1], arow_[0:1, TL - 1:TL], inc=None)
            vv = bld.v("tensor_copy", browbf[0:1, 1:2], brow_[0:1, TL - 1:TL], inc="mV")
            bld.wait("tensor", "mV", vv)
            tv = bld.mm(psB[:, 1024:1026], onesrow[0:1, 0:128], browbf[0:1, 0:2],
                        True, True, inc="mT")
            bld.wait("vector", "mT", tv)
            xlast = bass.AP(xres, TL - 1, [[NCH * TL, 128], [TL, NCH]])
            bld.v("tensor_tensor", ctmp[:, 0:8], xlast,
                  bass.AP(psB, 1025, [[2048, 128], [0, NCH]]), ALU.add)
            vv = bld.v("tensor_tensor", ctmp[:, 0:8], ctmp[:, 0:8],
                       bass.AP(psB, 1024, [[2048, 128], [0, NCH]]), ALU.mult, inc="mV")
            bld.wait("gpsimd", "mV", vv)
            bld.dma("qg", agi[:, :], ctmp[:, 0:8])
            bld.wait("gpsimd", "qg")
            bld.g("collective_compute", "AllGather", ALU.bypass,
                  replica_groups=rg, ins=[agi.ap().opt()], outs=[ago.ap().opt()], inc="cc")
            bld.wait("sync", "cc")
            bld.dma("ql", carrs[:, :, 0:8], bass.AP(ago, 0, [[8, 128], [128 * 8, 8], [1, 8]]))
            bld.wait("vector", "ql")
            for ci in range(NCH):
                bld.v("tensor_mul", ctmp[:, 0:8], carrs[:, :, ci], selm[:, :])
                bld.v("reduce_sum", dest[:, ci:ci + 1], ctmp[:, 0:8], axis=AXX)
            bld.barrier()

        if dbg == "a3row":
            bld.v("tensor_copy", bF[0:1, 0:TL], a3row[0:1, :])
            a3v = bld.v("tensor_copy", bF[0:1, TL:2 * TL] if False else bI[0:1, 0:TL],
                        b3row[0:1, :], inc="mV")
            bld.wait("gpsimd", "mV", a3v)
            dv1 = bld.dma("qg", zscr[:, 0:TL], bF[:, 0:TL])
            dv2 = bld.dma("qg", zscr[:, TL:2 * TL], bI[:, 0:TL])
            bld.wait("vector", "qg", dv2)
            bld.barrier()
        halo_ag(a3row, b3row, ag2i, ag2o, haloc)

        def ffn_dbg(buf, tt, n=2048):
            bld.barrier()
            dv = bld.dma("qg", zscr[:, tt * 2048:tt * 2048 + n], buf)
            bld.wait("vector", "qg", dv)
            bld.barrier()

        def ffn_stage(arow_, brow_, halosel, ffw, out_final=False):
            for tt in range(NT1):
                c0 = tt * TT1
                av = srow[0:1, 512:512 + TT1]
                bv = srow[0:1, 768:768 + TT1]
                bld.v("tensor_copy", av, arow_[0:1, c0:c0 + TT1])
                bld.v("tensor_copy", bv, brow_[0:1, c0:c0 + TT1])
                bcast_rows(av, bv)
                if dbg == "ffnbc":
                    bld.v("tensor_copy", bF[:, 0:256], psB[:, 1024:1280])
                    bcv = bld.v("tensor_copy", bF[:, 256:512], psB[:, 1280:1536], inc="mV")
                    bld.wait("gpsimd", "mV", bcv)
                    dv_ = bld.dma("qg", zscr[:, tt * 512:(tt + 1) * 512], bF[:, 0:512])
                    bld.wait("vector", "qg", dv_)
                    bld.barrier()
                xnp = ap3(bT2, 2080, 257, TT1, off=1)
                bld.v("tensor_tensor", xnp, xres_t(tt), bc3(Abc_off), ALU.mult)
                apv = bld.v("tensor_tensor", xnp, xnp, bc3(Bbc_off), ALU.add, inc="mV")
                if dbg == "ffnxn0":
                    bld.wait("gpsimd", "mV", apv)
                    dv0 = bld.dma("qg", zscr[:, tt * 2048:(tt + 1) * 2048], bT2[:, 0:2048])
                    bld.wait("vector", "qg", dv0)
                    bld.barrier()
                hc = bass.AP(bT2, 0, [[2080, 128], [257, NCH]])
                if tt == 0:
                    bld.v("tensor_copy", hc, halosel[:, :])
                else:
                    bld.v("tensor_copy", hc, halocar[:, :])
                lc = bass.AP(bT2, TT1, [[2080, 128], [257, NCH]])
                bld.v("tensor_copy", halocar[:, :], lc)
                xall = ap3(bT2, 2080, 257, TT1, off=1)
                xprev = ap3(bT2, 2080, 257, TT1, off=0)
                if dbg in ("ffnxn", "ffnxn0"):
                    ffn_dbg(bT2[:, 0:2048], tt)
                bld.v("tensor_tensor", v3(bF), xall, xprev, ALU.subtract)
                for ci in range(NCH):
                    xp_ci = bass.AP(bT2, ci * 257, [[2080, 128], [1, TT1]])
                    bld.v("scalar_tensor_tensor", xb_[:, ci * TT1:(ci + 1) * TT1],
                          bF[:, ci * TT1:(ci + 1) * TT1], mur[:, ci:ci + 1], xp_ci,
                          ALU.mult, ALU.add)
                for ci in range(NCH):
                    xp_ci = bass.AP(bT2, ci * 257, [[2080, 128], [1, TT1]])
                    vv = bld.v("scalar_tensor_tensor", sqn[:, ci * TT1:(ci + 1) * TT1],
                               bF[:, ci * TT1:(ci + 1) * TT1], muk[:, ci:ci + 1], xp_ci,
                               ALU.mult, ALU.add, inc=("mV" if ci == NCH - 1 else None))
                bld.wait("tensor", "mV", vv)
                for ci in range(NCH):
                    for e in range(NCH):
                        tv = bld.mm(psA[:, e * TT1:(e + 1) * TT1],
                                    ffw[:, 0 * NCH * D + ci * D + e * 128: 0 * NCH * D + ci * D + (e + 1) * 128],
                                    xb_[:, ci * TT1:(ci + 1) * TT1],
                                    ci == 0 and e % 2 == 0, ci == NCH - 1,
                                    inc=("mT" if (ci == NCH - 1 and e == NCH - 1) else None))
                bld.wait("scalar", "mT", tv)
                rv = bld.act(bI[:, :], psA[:, :], ACT.Sigmoid, inc="mS")
                if dbg == "ffnr":
                    bld.wait("vector", "mS", rv)
                    ffn_dbg(bI[:, :], tt)
                for ci in range(NCH):
                    for e in range(NCH):
                        tv = bld.mm(psB[:, e * TT1:(e + 1) * TT1],
                                    ffw[:, 1 * NCH * D + ci * D + e * 128: 1 * NCH * D + ci * D + (e + 1) * 128],
                                    sqn[:, ci * TT1:(ci + 1) * TT1],
                                    ci == 0 and e % 2 == 0, ci == NCH - 1,
                                    inc=("mT" if (ci == NCH - 1 and e == NCH - 1) else None))
                bld.wait("vector", "mT", tv)
                bld.v("tensor_scalar", bS[:, :], psB[:, 0:2048], cst[:, 5:6], None, ALU.max)
                if dbg == "ffnk":
                    ffn_dbg(bS[:, :], tt)
                kav = bld.v("tensor_mul", sqn[:, :], bS[:, :], bS[:, :], inc="mV")
                bld.wait("tensor", "mV", kav)
                bld.wait("tensor", "mS", rv)
                for ci in range(NCH):
                    for e in range(NCH):
                        tv = bld.mm(psA[:, e * TT1:(e + 1) * TT1],
                                    ffw[:, 2 * NCH * D + ci * D + e * 128: 2 * NCH * D + ci * D + (e + 1) * 128],
                                    sqn[:, ci * TT1:(ci + 1) * TT1],
                                    ci == 0 and e % 2 == 0, ci == NCH - 1,
                                    inc=("mT" if (ci == NCH - 1 and e == NCH - 1) else None))
                bld.wait("vector", "mT", tv)
                bld.wait("vector", "mS", rv)
                bld.v("tensor_tensor", bF[:, :], bI[:, :], psA[:, :], ALU.mult, inc="mV")
                if dbg == "ffnout":
                    ffn_dbg(bF[:, :], tt)
                fv = bld.v("tensor_tensor", xres_t(tt), xres_t(tt), v3(bF), ALU.add, inc="mV")
                if out_final:
                    bld.wait("gpsimd", "mV", fv)
                    xt = bass.AP(outd, tt * TT1, [[NCH * TL, 128], [TL, NCH], [1, TT1]])
                    bld.dma("qg", xt, xres_t(tt))
                bld.barrier()

        bld.flush()
        st1b.close()
        if upto >= 2:
            ffw = sb1("ffw", [128, 3 * NCH * D], BF16)
            for m in range(3):
                for ci in range(NCH):
                    wv = bld.dma("qg", ffw[:, m * NCH * D + ci * D:m * NCH * D + (ci + 1) * D], wfd[m, ci, :, :])
            bld.wait("tensor", "qg", wv)
            bld.wait("vector", "qg", wv)
            ffn_stage(a3row, b3row, haloc, ffw, out_final=False)

        if upto >= 3:
            bld.flush()
            st1.close()
            s3 = ExitStack()
            sb3 = lambda name, shape, dt: s3.enter_context(
                nc.sbuf_tensor("s3_" + name, shape, dt))
            kT = sb3("kT", [128, T], BF16)
            vaug = sb3("vaug", [128, 130 * 128], BF16)
            wqkv = sb3("wqkv", [128, 3 * NCH * 128], BF16)
            masks = sb3("masks", [128, 4 * 512], BF16)
            bqk = sb3("bqk", [128, 2], F32)
            x2nb = sb3("x2nb", [128, 2048], BF16)
            s3pre = ExitStack()
            xb3 = s3pre.enter_context(nc.sbuf_tensor("s3_xb3", [128, 2048], BF16))
            sq3 = s3pre.enter_context(nc.sbuf_tensor("s3_sq3", [128, 2048], BF16))
            t3 = s3pre.enter_context(nc.sbuf_tensor("s3_t3", [128, 2048], F32))

            mv_ = bld.dma("ql", t3[:, :], maskd[:, :])
            bld.dma("ql", bqk[:, :], bqkd[:, :])
            lv = bld.dma("ql", srow[0:1, 0:128], bvrowd[:, :])
            bld.wait("vector", "ql", lv)
            bld.v("tensor_copy", masks[:, :], t3[:, :])
            bvv = bld.v("tensor_copy", browbf[0:1, 1920:2048], srow[0:1, 0:128], inc="mV")
            for m in range(3):
                for ci in range(NCH):
                    wv3 = bld.dma("qg", wqkv[:, m * 1024 + ci * 128:m * 1024 + (ci + 1) * 128],
                                  wqkvd[m, ci, :, :])
            bld.v("memset", bass.AP(vaug, 64, [[130 * 128, 128], [130, 128], [65, 2]]), 1.0)

            for tt in range(NT1):
                c0 = tt * TT1
                ln_stats(lambda ci: xres[:, ci * TL + c0: ci * TL + c0 + TT1], xb3, sq3)
                bld.v("tensor_tensor", v3(t3), xres_t(tt), bc3(Abc_off), ALU.mult)
                xnv3 = bld.v("tensor_tensor", v3(x2nb), v3(t3), bc3(Bbc_off), ALU.add,
                             inc="mV")
                bld.wait("gpsimd", "mV", xnv3)
                agp = bass.AP(ag4i, tt * TT1, [[NCH * TL, 128], [TL, NCH], [1, TT1]])
                agv = bld.dma("qg", agp, v3(x2nb))
                bld.wait("vector", "qg", agv)
            bld.wait("gpsimd", "qg")
            bld.g("collective_compute", "AllGather", ALU.bypass,
                  replica_groups=rg, ins=[ag4i.ap().opt()], outs=[ag4o.ap().opt()],
                  inc="cc")
            bld.wait("sync", "cc")
            bld.wait("tensor", "qg", wv3)
            bld.wait("tensor", "mV", bvv)
            bld.flush()
            s3pre.close()
            xg = sb3("xg", [128, 2 * NCH * 512], BF16)
            pTA = sb3("pTA", [128, 2048], BF16)
            qt2 = sb3("qt2", [128, 2 * 512], BF16)
            osb = sb3("osb", [64, 1024], F32)
            oTb = sb3("oTb", [64, 2 * 1024], BF16)

            xg_v = {}

            def xg_load(gt):
                r_, off = gt // 4, (gt % 4) * 512
                slot = gt % 2
                inap = bass.AP(ag4o, (r_ * 128) * (NCH * TL) + off,
                               [[NCH * TL, 128], [TL, NCH], [1, 512]])
                bld.wait("sync", "mT")
                xg_v[gt] = bld.dma("ql", ap3(xg, 2 * NCH * 512, 512, 512,
                                             off=slot * NCH * 512), inap)

            xg_load(0)
            xg_load(1)
            ev_q = ev_k = ev_v = 0
            for gt in range(32):
                slot = gt % 2
                xgs = lambda ci: xg[:, slot * NCH * 512 + ci * 512:
                                    slot * NCH * 512 + (ci + 1) * 512]
                bld.wait("tensor", "ql", xg_v[gt])
                if gt >= 1:
                    bld.wait("tensor", "mS", ev_k)
                    bld.wait("tensor", "mV", ev_v)
                for ci in range(NCH):
                    bld.mm(psA[:, 0:512], wqkv[:, 0 * 1024 + ci * 128:0 * 1024 + (ci + 1) * 128],
                           xgs(ci), ci == 0, ci == NCH - 1)
                for ci in range(NCH):
                    qkv_t = bld.mm(psA[:, 512:1024],
                                   wqkv[:, 1 * 1024 + ci * 128:1 * 1024 + (ci + 1) * 128],
                                   xgs(ci), ci == 0, ci == NCH - 1,
                                   inc=("mT" if ci == NCH - 1 else None))
                for sub in range(4):
                    for ci in range(NCH):
                        bld.mm(psB[:, sub * 128:(sub + 1) * 128],
                               bass.AP(xg, slot * NCH * 512 + ci * 512 + sub * 128,
                                       [[2 * NCH * 512, 128], [1, 128]]),
                               wqkv[:, 2 * 1024 + ci * 128:2 * 1024 + (ci + 1) * 128],
                               sub == 0 and ci == 0, False)
                    vt = bld.mm(psB[:, sub * 128:(sub + 1) * 128],
                                onesrow[0:1, 0:128], browbf[0:1, 1920:2048],
                                False, True, inc=("mT" if sub == 3 else None))
                bld.wait("scalar", "mT", qkv_t)
                qslot = gt % 2
                if gt >= 2:
                    bld.wait("scalar", "qg", qst_v[gt - 2])
                bld.act(x2nb[:, qslot * 512:(qslot + 1) * 512], psA[:, 0:512],
                        ACT.Identity, bias=bqk[:, 0:1])
                ev_k = bld.act(kT[:, gt * 512:(gt + 1) * 512], psA[:, 512:1024],
                               ACT.Identity, bias=bqk[:, 1:2], inc="mS")
                bld.wait("gpsimd", "mS", ev_k)
                qst_v = qst_v if gt else {}
                qst_v[gt] = bld.dma("qg", qscr[:, gt * 512:(gt + 1) * 512],
                                    x2nb[:, qslot * 512:(qslot + 1) * 512])
                bld.wait("vector", "mT", vt)
                for sub in range(4):
                    tb = (gt * 4 + sub) * 130
                    ev_v = bld.v("tensor_copy",
                                 bass.AP(vaug, tb, [[130 * 128, 128], [65, 2], [1, 64]]),
                                 bass.AP(psB, sub * 128, [[2048, 128], [64, 2], [1, 64]]),
                                 inc=("mV" if sub == 3 else None))
                if gt < 30:
                    xg_load(gt + 2)

            if dbg == "att_xg":
                bld.barrier()
                dvx = bld.dma("qg", zscr[:, :],
                              bass.AP(ag4o, 0, [[NCH * TL, 128], [1, NCH * TL]]))
                bld.wait("vector", "qg", dvx)
                bld.barrier()
            if dbg in ("att_k", "att_q", "att_v"):
                bld.barrier()
                src = {"att_k": kT[:, :], "att_q": None, "att_v": None}[dbg] \
                    if dbg == "att_k" else None
                if dbg == "att_k":
                    dvk = bld.dma("qg", zscr[:, :], kT[:, :])
                elif dbg == "att_q":
                    dvk = bld.dma("qg", zscr[:, :], bass.AP(qscr, 0, [[T, 128], [1, T]]))
                else:
                    dvk = bld.dma("qg", zscr[:, :], vaug[:, 0:T])
                bld.wait("vector", "qg", dvk)
                bld.barrier()

            qld = {}

            def q_load(qt):
                qld[qt] = bld.dma("ql", qt2[:, (qt % 2) * 512:(qt % 2 + 1) * 512],
                                  qscr[:, qt * 512:(qt + 1) * 512])

            osb_v = bld.cnt["mV"]
            bld.wait("sync", "qg", qst_v[0])
            q_load(0)
            bld.wait("sync", "qg", qst_v[1])
            q_load(1)
            exp_v = {}
            qk_done = {}
            ost_v = {}
            for qt in range(32):
                nkt = 4 * qt + 4
                qs = qt % 2
                bld.wait("tensor", "ql", qld[qt])
                for kt in range(nkt):
                    reg0 = 0 if kt % 2 == 0 else 1024
                    if (qt, kt - 2) in exp_v:
                        bld.wait("tensor", "mS", exp_v[(qt, kt - 2)])
                    elif kt < 2 and qt > 0:
                        bld.wait("tensor", "mS", exp_v[(qt - 1, 4 * qt - 2 + kt)])
                        bld.wait("tensor", "mV", osb_v)
                    bld.mm(psA[:, reg0:reg0 + 512], kT[0:64, kt * 128:(kt + 1) * 128],
                           qt2[0:64, qs * 512:(qs + 1) * 512], True, True)
                    tvq = bld.mm(psA[:, reg0 + 512:reg0 + 1024],
                                 kT[64:128, kt * 128:(kt + 1) * 128],
                                 qt2[64:128, qs * 512:(qs + 1) * 512], True, True, inc="mT")
                    bld.wait("scalar", "mT", tvq)
                    exp_v[(qt, kt)] = bld.act(pTA[:, reg0:reg0 + 1024],
                                              psA[:, reg0:reg0 + 1024], ACT.Exp,
                                              bias=cst[:, 3:4], scale=0.125, inc="mS")
                    waits_pv = [("mS", exp_v[(qt, kt)])]
                    if kt >= 4 * qt:
                        a_ = kt - 4 * qt
                        bld.wait("vector", "mS", exp_v[(qt, kt)])
                        mk_v = bld.v("tensor_tensor",
                                     ap3(pTA, 2048, 512, 512, off=reg0, nc_=2),
                                     ap3(pTA, 2048, 512, 512, off=reg0, nc_=2),
                                     bass.AP(masks, a_ * 512, [[4 * 512, 128], [0, 2], [1, 512]]),
                                     ALU.mult, inc="mV")
                        waits_pv.append(("mV", mk_v))
                    if kt == 0:
                        waits_pv.append(("mV", osb_v))
                    for s_, v_ in waits_pv:
                        bld.wait("tensor", s_, v_)
                    bld.mm(psB[0:65, 0:512], vaug[:, kt * 130:kt * 130 + 65],
                           pTA[:, reg0:reg0 + 512], kt == 0, kt == nkt - 1)
                    pv_t = bld.mm(psB[0:65, 512:1024],
                                  vaug[:, kt * 130 + 65:kt * 130 + 130],
                                  pTA[:, reg0 + 512:reg0 + 1024], kt == 0, kt == nkt - 1,
                                  inc=("mT" if kt == nkt - 1 else None))
                qk_done[qt] = bld.cnt["mT"]
                if qt < 31:
                    if qt >= 1:
                        bld.wait("sync", "mT", qk_done[qt])
                    bld.wait("sync", "qg", qst_v[qt + 1])
                    q_load(qt + 1)
                bld.wait("vector", "mT", pv_t)
                bld.v("reciprocal", srow[0:1, 0:512], psB[64:65, 0:512])
                bld.v("reciprocal", srow[0:1, 512:1024], psB[64:65, 512:1024])
                bld.v("tensor_copy", browbf[0:1, 0:1024], srow[0:1, 0:1024])
                bld.v("tensor_copy", srow[0:1, 1024:2048], browbf[0:1, 0:1024])
                bld.v("tensor_sub", srow[0:1, 1024:2048], srow[0:1, 0:1024],
                      srow[0:1, 1024:2048])
                rc_v = bld.v("tensor_copy", browbf[0:1, 1024:2048], srow[0:1, 1024:2048],
                             inc="mV")
                bld.wait("tensor", "mV", rc_v)
                bld.wait("tensor", "mS", exp_v[(qt, nkt - 1)])
                bld.wait("tensor", "mS", exp_v[(qt, nkt - 2)])
                bld.mm(psA[0:64, 0:512], onesrow[0:1, 0:64], browbf[0:1, 0:512],
                       True, False)
                bld.mm(psA[0:64, 512:1024], onesrow[0:1, 0:64], browbf[0:1, 512:1024],
                       True, False)
                bld.mm(psA[0:64, 0:512], onesrow[0:1, 0:64], browbf[0:1, 1024:1536],
                       False, True)
                bc_t = bld.mm(psA[0:64, 512:1024], onesrow[0:1, 0:64],
                              browbf[0:1, 1536:2048], False, True, inc="mT")
                bld.wait("vector", "mT", bc_t)
                if qt >= 2:
                    bld.wait("vector", "qg", ost_v[qt - 2])
                bld.v("tensor_copy", osb[0:64, :], psB[0:64, 0:1024])
                osb_v = bld.v("tensor_mul", oTb[0:64, qs * 1024:(qs + 1) * 1024],
                              osb[0:64, :], psA[0:64, 0:1024], inc="mV")
                bld.wait("gpsimd", "mV", osb_v)
                blk, off = qt // 4, (qt % 4) * 512
                bld.dma("qg", a2ai[blk, 0:64, off:off + 512],
                        oTb[0:64, qs * 1024:qs * 1024 + 512])
                ost_v[qt] = bld.dma("qg", a2ai[blk, 64:128, off:off + 512],
                                    oTb[0:64, qs * 1024 + 512:qs * 1024 + 1024])

            bld.wait("gpsimd", "qg")
            if dbg == "att_o":
                bld.barrier()
                dvo = bld.dma("qg", zscr[:, :],
                              bass.AP(a2ai, 0, [[T, 128], [1, T]]))
                bld.wait("vector", "qg", dvo)
                bld.barrier()

            def _a2a(e, i_=a2ai, o_=a2ao, rg_=rg):
                with allow_ici_all_to_all():
                    return e.collective_compute(
                        "AllToAll", ALU.bypass, replica_groups=rg_,
                        ins=[i_.ap().opt()], outs=[o_.ap().opt()])

            bld.pend["gpsimd"].append(_a2a)
            bld._wrap_inc("gpsimd", "cc")
            bld.wait("sync", "cc")
            bld.flush()
            s3.close()

            s3b = ExitStack()
            woT = s3b.enter_context(nc.sbuf_tensor("s3_woT", [128, NCH * D], BF16))
            obin = s3b.enter_context(nc.sbuf_tensor("s3_obin", [128, NCH * 512], BF16))
            obuf = s3b.enter_context(nc.sbuf_tensor("s3_obuf", [128, 2048], F32))
            bo3b = s3b.enter_context(nc.sbuf_tensor("s3_bo3b", [128, NCH], F32))
            bld.dma("ql", bo3b[:, :], bod[:, :])
            for ci in range(NCH):
                wo_v = bld.dma("qg", woT[:, ci * D:(ci + 1) * D], wod[ci, :, :])
            bld.wait("tensor", "qg", wo_v)
            ob_v = 0
            for tt4 in range(4):
                inap = bass.AP(a2ao, tt4 * 512, [[TL, 128], [128 * TL, NCH], [1, 512]])
                bld.wait("sync", "mT")
                ov = bld.dma("ql", ap3(obin, NCH * 512, 512, 512), inap)
                bld.wait("tensor", "ql", ov)
                if tt4 >= 1:
                    bld.wait("tensor", "mV", ob_v)
                for do in range(NCH):
                    ps = psA if do < 4 else psB
                    col = (do % 4) * 512
                    for oc in range(NCH):
                        op_t = bld.mm(ps[:, col:col + 512],
                                      woT[:, oc * D + do * 128:oc * D + (do + 1) * 128],
                                      obin[:, oc * 512:(oc + 1) * 512], oc == 0,
                                      oc == NCH - 1,
                                      inc=("mT" if oc == NCH - 1 else None))
                bld.wait("scalar", "mT", op_t)
                for half in range(2):
                    bld.wait("scalar", "mV", ob_v)
                    for d4 in range(4):
                        do = half * 4 + d4
                        ps = psA if half == 0 else psB
                        oev = bld.act(obuf[:, d4 * 512:(d4 + 1) * 512],
                                      ps[:, d4 * 512:(d4 + 1) * 512], ACT.Identity,
                                      bias=bo3b[:, do:do + 1],
                                      inc=("mS" if d4 == 3 else None))
                    bld.wait("vector", "mS", oev)
                    xh = bass.AP(xres, half * 4 * TL + tt4 * 512,
                                 [[NCH * TL, 128], [TL, 4], [1, 512]])
                    ob_v = bld.v("tensor_tensor", xh, xh,
                                 ap3(obuf, 2048, 512, 512, nc_=4), ALU.add, inc="mV")
            bld.flush()
            s3b.close()

        if upto >= 4:
            s4 = ExitStack()
            sb4 = lambda name, shape, dt: s4.enter_context(
                nc.sbuf_tensor("s4_" + name, shape, dt))
            bF = sb4("bF", [128, 2048], F32)
            bI = sb4("bI", [128, 2048], F32)
            bS = sb4("bS", [128, 2048], F32)
            bT2 = sb4("bT2", [128, 2080], F32)
            xb_ = sb4("xb", [128, 2048], BF16)
            sqn = sb4("sqn", [128, 2048], BF16)
            gbf = sb4("gbf", [128, 2048], BF16)
            ffw = sb4("ffw", [128, 3 * NCH * D], BF16)
            a3row = sb4("a3row", [1, TL], F32)
            b3row = sb4("b3row", [1, TL], F32)
            for m in range(3):
                for ci in range(NCH):
                    wv4 = bld.dma("qg", ffw[:, m * NCH * D + ci * D:m * NCH * D + (ci + 1) * D],
                                  wfd[m, ci, :, :])
            bld.wait("tensor", "qg", wv4)
            bld.wait("vector", "qg", wv4)
            for tt in range(NT1):
                c0 = tt * TT1
                ln_stats(lambda ci: xres[:, ci * TL + c0: ci * TL + c0 + TT1], xb_, sqn)
                bld.v("tensor_copy", a3row[0:1, c0:c0 + TT1], srow[0:1, 0:TT1])
                bld.v("tensor_copy", b3row[0:1, c0:c0 + TT1], srow[0:1, 256:256 + TT1])
                bld.barrier()
            halo_ag(a3row, b3row, ag3i, ag3o, haloc)
            ffn_stage(a3row, b3row, haloc, ffw, out_final=True)
            bld.wait("gpsimd", "qg")
            bld.flush()
            s4.close()

        if upto < 4:
            bld.barrier()
            if dbg is not None:
                bld.dma("qg", outd[:, :], zscr[:, :])
            else:
                for tt in range(NT1):
                    xt = bass.AP(outd, tt * TT1, [[NCH * TL, 128], [TL, NCH], [1, TT1]])
                    bld.dma("qg", xt, xres_t(tt))
            bld.wait("gpsimd", "qg")
            bld.flush()
            if upto < 3:
                st1.close()
        perm.close()
    return nc


def prep_inputs(core, inputs):
    c = core
    x = np.asarray(inputs["L0_attn_a_x"]).reshape(T, D)
    d = {}
    d["x0"] = fm_pack(np.asarray(x[c * TL:(c + 1) * TL], np.float32))
    w1 = np.stack([np.asarray(inputs["L0_attn_a_Wi"]), np.asarray(inputs["L0_attn_a_Wf"]),
                   np.asarray(inputs["L0_attn_a_Wo"]), np.asarray(inputs["L0_attn_a_Ws"])])
    d["w1"] = np.ascontiguousarray(w1.reshape(4, NCH, 128, D), np.float32)
    d["b1"] = np.ascontiguousarray(np.concatenate(
        [np.asarray(inputs["L0_attn_a_bi"]), np.asarray(inputs["L0_attn_a_bf"]),
         np.asarray(inputs["L0_attn_a_bo"]), np.asarray(inputs["L0_attn_a_bs"])])[None, :],
        np.float32)
    pp = lambda v: np.ascontiguousarray(np.asarray(v).reshape(NCH, 128).T, np.float32)
    d["gam"] = pp(inputs["L0_attn_a_gamma"])
    d["bet"] = pp(inputs["L0_attn_a_beta"])
    d["mur"] = pp(inputs["L0_ffn_mu_r"])
    d["muk"] = pp(inputs["L0_ffn_mu_k"])
    wf = np.stack([np.asarray(inputs["L0_ffn_W_r"]).T,
                   np.asarray(inputs["L0_ffn_W_k"]).T,
                   np.asarray(inputs["L0_ffn_W_v"]).T])
    d["wf"] = np.ascontiguousarray(wf.reshape(3, NCH, 128, D), np.float32)
    r0, r1 = 128 * c, 128 * (c + 1)
    wq = np.asarray(inputs["L0_attn_b_Wq"]).T[:, r0:r1]
    wk = np.asarray(inputs["L0_attn_b_Wk"]).T[:, r0:r1]
    wv = np.asarray(inputs["L0_attn_b_Wv"]).T[:, r0:r1]
    d["wqkv"] = np.ascontiguousarray(np.stack([wq, wk, wv]).reshape(3, NCH, 128, 128),
                                     np.float32)
    d["bqk"] = np.ascontiguousarray(
        np.stack([np.asarray(inputs["L0_attn_b_bq"])[r0:r1],
                  np.asarray(inputs["L0_attn_b_bk"])[r0:r1]], axis=1), np.float32)
    d["bvrow"] = np.ascontiguousarray(np.asarray(inputs["L0_attn_b_bv"])[None, r0:r1],
                                      np.float32)
    d["wo"] = np.ascontiguousarray(
        np.asarray(inputs["L0_attn_b_Wo"]).T.reshape(NCH, 128, D), np.float32)
    d["bo"] = pp(inputs["L0_attn_b_bo"])
    mk = np.zeros((128, 4, 512), np.float32)
    q = np.arange(512)
    for a in range(4):
        for k in range(128):
            mk[k, a] = (q >= 128 * a + k)
    d["mask"] = np.ascontiguousarray(mk.reshape(128, 4 * 512))
    sel = np.zeros((128, 8), np.float32)
    if c > 0:
        sel[:, c - 1] = 1.0
    d["sel"] = sel
    return d


def kernel(**inputs):
    nc = build(_UPTO, _DBG)
    in_maps = [prep_inputs(c, inputs) for c in range(NCORE)]
    res = run_bass_kernel_spmd(nc, in_maps, core_ids=list(range(NCORE)))
    if _DBG is not None:
        return [res.results[c]["out"] for c in range(NCORE)]
    shards = [fm_unpack(res.results[c]["out"], TL) for c in range(NCORE)]
    return np.concatenate(shards, axis=0).reshape(1, T, D).astype(np.float32)


# version 03
# speedup vs baseline: 1.4271x; 1.4271x over previous
import numpy as np
import ml_dtypes
import concourse.bass as bass
import concourse.mybir as mybir
from concourse.bass_utils import run_bass_kernel_spmd
from concourse.aot_env import allow_ici_all_to_all
from contextlib import ExitStack

F32 = mybir.dt.float32
BF16 = mybir.dt.bfloat16
ALU = mybir.AluOpType
ACT = mybir.ActivationFunctionType
AXX = mybir.AxisListType.X

NCORE = 8
D = 1024
T = 16384
TL = T // NCORE
NCH = 8
TT1 = 256
NT1 = TL // TT1
LN_EPS = 1e-5

_UPTO = 4
_DBG = None


class Bld:
    ENGS = ("sync", "gpsimd", "tensor", "scalar", "vector")

    def __init__(self, nc, block, sems):
        self.nc, self.block = nc, block
        self.sems = sems
        self.cnt = {k: 0 for k in sems}
        self.pend = {e: [] for e in self.ENGS}

    def flush(self):
        for e in self.ENGS:
            lst = self.pend[e]
            if not lst:
                continue
            self.pend[e] = []
            getattr(self.block, e)(lambda eng, lst=lst: [f(eng) for f in lst])

    def _p(self, eng, fn):
        self.pend[eng].append(fn)

    def _wrap_inc(self, eng, sem, n=1):
        self.cnt[sem] += n
        h = self.sems[sem]
        f = self.pend[eng][-1]
        self.pend[eng][-1] = lambda e, f=f, h=h, n=n: f(e).then_inc(h, n)
        return self.cnt[sem]

    def wait(self, eng, sem, val=None):
        if val is None:
            val = self.cnt[sem]
        if val <= 0:
            return
        assert val <= self.cnt[sem], (sem, val, self.cnt[sem])
        h = self.sems[sem]
        self._p(eng, lambda e, h=h, v=val: e.wait_ge(h, v))

    def dma(self, q, out, in_):
        eng = "sync" if q == "ql" else "gpsimd"
        self._p(eng, lambda e, o=out, i=in_: e.dma_start(out=o, in_=i))
        return self._wrap_inc(eng, q, 16)

    def mm(self, out, lhsT, rhs, start, stop, inc=None):
        self._p("tensor", lambda e, o=out, l=lhsT, r=rhs, s0=start, s1=stop:
                e.matmul(o, l, r, start=s0, stop=s1))
        if inc:
            return self._wrap_inc("tensor", inc)

    def act(self, out, in_, func, bias=0.0, scale=1.0, inc=None):
        self._p("scalar", lambda e, o=out, i=in_, f=func, b=bias, s=scale:
                e.activation(out=o, in_=i, func=f, bias=b, scale=s))
        if inc:
            return self._wrap_inc("scalar", inc)

    def v(self, meth, *args, inc=None, **kw):
        self._p("vector", lambda e, m=meth, a=args, k=kw: getattr(e, m)(*a, **k))
        try:
            fs = args[0].free_size() if hasattr(args[0], "free_size") else 999
        except Exception:
            fs = 999
        r = None
        if inc:
            r = self._wrap_inc("vector", inc)
        if fs <= 64 and meth != "memset":
            for _ in range(4):
                self._p("vector", lambda e: e.engine_nop())
        return r

    def g(self, meth, *args, inc=None, **kw):
        self._p("gpsimd", lambda e, m=meth, a=args, k=kw: getattr(e, m)(*a, **k))
        if inc:
            return self._wrap_inc("gpsimd", inc)

    def barrier(self):
        engs = self.ENGS
        vals = {}
        for e in engs:
            sem = "b_" + e
            h = self.sems[sem]
            self.cnt[sem] += 1
            vals[e] = self.cnt[sem]
            self._p(e, lambda eng, h=h: eng.nop().then_inc(h, 1))
        for e in engs:
            for o in engs:
                if o != e:
                    self.wait(e, "b_" + o, vals[o])


def fm_pack(a):
    tn = a.shape[0]
    return np.ascontiguousarray(
        a.T.reshape(NCH, 128, tn).transpose(1, 0, 2).reshape(128, NCH * tn))


def fm_unpack(a, tn):
    return np.ascontiguousarray(
        a.reshape(128, NCH, tn).transpose(1, 0, 2).reshape(D, tn).T)


def build(upto=4, dbg=None):
    nc = bass.Bass("TRN2", target_bir_lowering=False, debug=False)
    DT = nc.dram_tensor

    x0d = DT("x0", [128, NCH * TL], F32, kind="ExternalInput")
    w1d = DT("w1", [4, NCH, 128, D], F32, kind="ExternalInput")
    b1d = DT("b1", [1, 4 * D], F32, kind="ExternalInput")
    gamd = DT("gam", [128, NCH], F32, kind="ExternalInput")
    betd = DT("bet", [128, NCH], F32, kind="ExternalInput")
    wfd = DT("wf", [3, NCH, 128, D], F32, kind="ExternalInput")
    murd = DT("mur", [128, NCH], F32, kind="ExternalInput")
    mukd = DT("muk", [128, NCH], F32, kind="ExternalInput")
    wqkvd = DT("wqkv", [3, NCH, 128, 128], F32, kind="ExternalInput")
    bqkd = DT("bqk", [128, 2], F32, kind="ExternalInput")
    bvrowd = DT("bvrow", [1, 128], F32, kind="ExternalInput")
    wod = DT("wo", [NCH, 128, D], F32, kind="ExternalInput")
    bod = DT("bo", [128, NCH], F32, kind="ExternalInput")
    maskd = DT("mask", [128, 4 * 512], F32, kind="ExternalInput")
    seld = DT("sel", [128, 8], F32, kind="ExternalInput")
    outd = DT("out", [128, NCH * TL], F32, kind="ExternalOutput")

    zscr = DT("zscr", [128, NCH * TL], F32)
    gscr = DT("gscr", [128, NCH * TL], BF16)
    ag1i = DT("ag1i", [128, 16], F32)
    ag1o = DT("ag1o", [NCORE * 128, 16], F32)
    ag2i = DT("ag2i", [128, 8], F32)
    ag2o = DT("ag2o", [NCORE * 128, 8], F32)
    ag3i = DT("ag3i", [128, 8], F32)
    ag3o = DT("ag3o", [NCORE * 128, 8], F32)
    ag4i = DT("ag4i", [128, NCH * TL], BF16)
    ag4o = DT("ag4o", [NCORE * 128, NCH * TL], BF16)
    qscr = DT("qscr", [128, T], BF16)
    a2ai = DT("a2ai", [NCORE, 128, TL], BF16)
    a2ao = DT("a2ao", [NCORE, 128, TL], BF16)
    rg = [list(range(NCORE))]

    ctx0 = ExitStack()
    sems = {}
    for s in ("ql", "qg", "cc", "mT", "mS", "mV",
              "b_sync", "b_gpsimd", "b_tensor", "b_scalar", "b_vector"):
        sems[s] = ctx0.enter_context(nc.semaphore(s))

    with ctx0, nc.Block() as block:
        bld = Bld(nc, block, sems)
        perm = ExitStack()
        sb = lambda name, shape, dt: perm.enter_context(nc.sbuf_tensor("s_" + name, shape, dt))
        xres = sb("xres", [128, NCH * TL], F32)
        onescol = sb("onescol", [128, 1], BF16)
        onesrow = sb("onesrow", [1, 512], BF16)
        ones2k = sb("ones2k", [128, TT1], F32)
        cst = sb("cst", [128, 8], F32)
        b1row = sb("b1row", [1, 4 * D], BF16)
        gam = sb("gam", [128, NCH], F32)
        bet = sb("bet", [128, NCH], F32)
        mur = sb("mur", [128, NCH], F32)
        muk = sb("muk", [128, NCH], F32)
        selm = sb("selm", [128, 8], F32)
        hcar = sb("hcar", [128, 8], F32)
        fcar = sb("fcar", [128, 8], F32)
        h_in = sb("h_in", [128, 8], F32)
        halocar = sb("halocar", [128, 8], F32)
        haloc = sb("haloc", [128, 8], F32)
        carrs = sb("carrs", [128, 8, 16], F32)
        ctmp = sb("ctmp", [128, 24], F32)
        srow = sb("srow", [1, 3072], F32)
        browbf = sb("browbf", [1, 2048], BF16)
        psA = perm.enter_context(nc.psum_tensor("psA", [128, 2048], F32))
        psB = perm.enter_context(nc.psum_tensor("psB", [128, 2048], F32))

        bld.dma("ql", xres[:, :], x0d[:, :])
        bld.dma("ql", gam[:, :], gamd[:, :])
        bld.dma("ql", bet[:, :], betd[:, :])
        bld.dma("ql", mur[:, :], murd[:, :])
        bld.dma("ql", muk[:, :], mukd[:, :])
        bld.dma("ql", selm[:, :], seld[:, :])
        b1v = bld.dma("qg", b1row[:, :], b1d[:, :])
        ld0 = bld.cnt["ql"]
        bld.v("memset", onescol[:, :], 1.0)
        bld.v("memset", onesrow[:, :], 1.0)
        bld.v("memset", ones2k[:, :], 1.0)
        bld.v("memset", hcar[:, :], 0.0)
        bld.v("memset", fcar[:, :], 1.0)
        bld.v("memset", cst[:, 0:1], LN_EPS)
        bld.v("memset", cst[:, 1:2], 1.0 / D)
        bld.v("memset", cst[:, 2:3], -1.0)
        bld.v("memset", cst[:, 3:4], -4.0)
        bld.v("memset", cst[:, 4:5], 0.125)
        bld.v("memset", cst[:, 5:6], 0.0)
        bld.v("memset", cst[:, 6:7], -0.5)
        bld.wait("vector", "ql", ld0)
        bld.wait("tensor", "qg", b1v)

        def ap3(buf, pstep, cstep, n, off=0, nc_=NCH):
            return bass.AP(buf, off, [[pstep, 128], [cstep, nc_], [1, n]])

        def xres_t(tt, n=TT1):
            return ap3(xres, NCH * TL, TL, n, off=tt * n)

        def bc3(col_off, n=TT1):
            return bass.AP(psB, col_off, [[2048, 128], [0, NCH], [1, n]])

        def v3(buf, n=TT1, pstep=2048):
            return ap3(buf, pstep, n, n)

        Abc_off, Bbc_off = 1024, 1280

        def ln_stats(src_of_ci, xb, sq, n=TT1):
            for ci in range(NCH):
                bld.v("tensor_copy", xb[:, ci * n:(ci + 1) * n], src_of_ci(ci))
            for ci in range(NCH):
                vv = bld.v("tensor_mul", sq[:, ci * n:(ci + 1) * n],
                           xb[:, ci * n:(ci + 1) * n], xb[:, ci * n:(ci + 1) * n],
                           inc=("mV" if ci == NCH - 1 else None))
            bld.wait("tensor", "mV", vv)
            for ci in range(NCH):
                bld.mm(psB[0:1, 1536:1536 + n], onescol[:, :], xb[:, ci * n:(ci + 1) * n],
                       ci == 0, ci == NCH - 1)
            for ci in range(NCH):
                tv = bld.mm(psB[0:1, 1792:1792 + n], onescol[:, :], sq[:, ci * n:(ci + 1) * n],
                            ci == 0, ci == NCH - 1,
                            inc=("mT" if ci == NCH - 1 else None))
            bld.wait("vector", "mT", tv)
            m_ = srow[0:1, 2048:2048 + n]
            msq = srow[0:1, 2304:2304 + n]
            bld.v("tensor_scalar", m_, psB[0:1, 1536:1536 + n], cst[0:1, 1:2], None, ALU.mult)
            bld.v("tensor_scalar", msq, psB[0:1, 1792:1792 + n], cst[0:1, 1:2], None, ALU.mult)
            var = srow[0:1, 2560:2560 + n]
            bld.v("tensor_mul", var, m_, m_)
            bld.v("tensor_sub", var, msq, var)
            bld.v("tensor_scalar", var, var, cst[0:1, 0:1], None, ALU.add)
            inv = srow[0:1, 2816:2816 + n]
            sv = bld.v("reciprocal", inv, var, inc="mV")
            bld.wait("scalar", "mV", sv)
            arow = srow[0:1, 0:n]
            av = bld.act(arow, inv, ACT.Sqrt, inc="mS")
            bld.wait("vector", "mS", av)
            t1 = srow[0:1, 2304:2304 + n]
            bld.v("tensor_mul", t1, arow, arow)
            bld.v("tensor_mul", t1, t1, var)
            bld.v("tensor_scalar", t1, t1, cst[0:1, 6:7], 1.5, ALU.mult, ALU.add)
            bld.v("tensor_mul", arow, arow, t1)
            brow = srow[0:1, 256:256 + n]
            bld.v("tensor_mul", brow, m_, arow)
            bld.v("tensor_scalar", brow, brow, cst[0:1, 2:3], None, ALU.mult)
            bcast_rows(arow, brow, n)

        def bcast_rows(arow, brow, n=TT1):
            bld.v("tensor_copy", browbf[0:1, 0:n], arow)
            ahif = srow[0:1, 2560:2560 + n]
            bld.v("tensor_copy", ahif, browbf[0:1, 0:n])
            alo = srow[0:1, 2816:2816 + n]
            bld.v("tensor_sub", alo, arow, ahif)
            bld.v("tensor_copy", browbf[0:1, 256:256 + n], alo)
            vv = bld.v("tensor_copy", browbf[0:1, 512:512 + n], brow, inc="mV")
            bld.wait("tensor", "mV", vv)
            bld.mm(psB[:, Abc_off:Abc_off + n], onesrow[0:1, 0:128], browbf[0:1, 0:n],
                   True, False)
            bld.mm(psB[:, Abc_off:Abc_off + n], onesrow[0:1, 0:128], browbf[0:1, 256:256 + n],
                   False, True)
            tv = bld.mm(psB[:, Bbc_off:Bbc_off + n], onesrow[0:1, 0:128],
                        browbf[0:1, 512:512 + n], True, True, inc="mT")
            bld.wait("vector", "mT", tv)

        st1 = ExitStack()
        sb1 = lambda name, shape, dt: st1.enter_context(nc.sbuf_tensor("s_" + name, shape, dt))
        bF = sb1("bF", [128, 2048], F32)
        bI = sb1("bI", [128, 2048], F32)
        bS = sb1("bS", [128, 2048], F32)
        bT2 = sb1("bT2", [128, 2080], F32)
        xb_ = sb1("xb", [128, 2048], BF16)
        sqn = sb1("sqn", [128, 2048], BF16)
        gbf = sb1("gbf", [128, 2048], BF16)
        a3row = sb1("a3row", [1, TL], F32)
        b3row = sb1("b3row", [1, TL], F32)
        st1b = ExitStack()
        wbuf = st1b.enter_context(nc.sbuf_tensor("s_wbuf", [128, 2, D], BF16))
        bO = st1b.enter_context(nc.sbuf_tensor("s_bO", [128, 2048], F32))

        gate_dst = [psA, psB, psA, psB]
        gate_fun = [ACT.Sigmoid, ACT.Sigmoid, ACT.Sigmoid, ACT.Tanh]
        gate_buf = [bI, bF, bO, bS]

        worder = [(tt, g, ci) for tt in range(NT1) for g in range(4) for ci in range(NCH)]
        wld_v = {}
        wuse_v = {}
        wload_ptr = [0]

        def w1_issue_next():
            i = wload_ptr[0]
            if i >= len(worder):
                return
            wload_ptr[0] += 1
            tt, g, ci = worder[i]
            slot = i % 2
            if i >= 2:
                bld.wait("gpsimd", "mT", wuse_v[i - 2])
            wld_v[i] = bld.dma("qg", wbuf[:, slot, :], w1d[g, ci, :, :])

        w1_issue_next()
        w1_issue_next()

        for tt in range(NT1):
            c0 = tt * TT1
            ln_stats(lambda ci: xres[:, ci * TL + c0: ci * TL + c0 + TT1], xb_, sqn)
            bld.v("tensor_tensor", v3(bT2, pstep=2080), xres_t(tt), bc3(Abc_off), ALU.mult)
            bld.v("tensor_tensor", v3(bT2, pstep=2080), v3(bT2, pstep=2080),
                  bc3(Bbc_off), ALU.add)
            xnv = bld.v("tensor_copy", xb_[:, :], bT2[:, 0:2048], inc="mV")
            bld.wait("tensor", "mV", xnv)
            if dbg == "xn":
                bld.wait("gpsimd", "mV", xnv)
                bld.dma("qg", zscr[:, tt * 2048:(tt + 1) * 2048], bT2[:, 0:2048])

            evac_v = {}
            for g in range(4):
                dst = gate_dst[g]
                for ci in range(NCH):
                    i = worder.index((tt, g, ci))
                    bld.wait("tensor", "qg", wld_v[i])
                    if ci == 0 and g >= 2:
                        bld.wait("tensor", "mS", evac_v[g - 2])
                    slot = i % 2
                    for e in range(NCH):
                        tv = bld.mm(dst[:, e * TT1:(e + 1) * TT1],
                                    wbuf[:, slot, e * 128:(e + 1) * 128],
                                    xb_[:, ci * TT1:(ci + 1) * TT1],
                                    ci == 0 and e % 2 == 0, False,
                                    inc=("mT" if e == NCH - 1 else None))
                    wuse_v[i] = tv
                    w1_issue_next()
                for e in range(NCH):
                    tv = bld.mm(dst[:, e * TT1:(e + 1) * TT1],
                                b1row[0:1, g * D + e * 128: g * D + (e + 1) * 128],
                                onesrow[0:1, 0:TT1], False, True,
                                inc=("mT" if e == NCH - 1 else None))
                bld.wait("scalar", "mT", tv)
                evac_v[g] = bld.act(gate_buf[g][:, :], dst[:, :], gate_fun[g], inc="mS")
            bld.wait("vector", "mS", evac_v[3])

            if dbg == "gatef":
                fdv = bld.v("tensor_copy", bT2[:, 0:2048], bF[:, :], inc="mV")
                bld.wait("gpsimd", "mV", fdv)
                bld.dma("qg", zscr[:, tt * 2048:(tt + 1) * 2048], bT2[:, 0:2048])
            bld.v("tensor_mul", bI[:, :], bI[:, :], bS[:, :])
            for ci in range(NCH):
                bld.v("tensor_tensor_scan", bS[:, ci * TT1:(ci + 1) * TT1],
                      bF[:, ci * TT1:(ci + 1) * TT1], bI[:, ci * TT1:(ci + 1) * TT1],
                      hcar[:, ci:ci + 1], ALU.mult, ALU.add)
            for ci in range(NCH):
                bld.v("tensor_tensor_scan", bI[:, ci * TT1:(ci + 1) * TT1],
                      bF[:, ci * TT1:(ci + 1) * TT1], ones2k[:, 0:TT1],
                      fcar[:, ci:ci + 1], ALU.mult, ALU.mult)
            hS = bass.AP(bS, TT1 - 1, [[2048, 128], [TT1, NCH]])
            hI = bass.AP(bI, TT1 - 1, [[2048, 128], [TT1, NCH]])
            bld.v("tensor_copy", hcar[:, :], hS)
            bld.v("tensor_copy", fcar[:, :], hI)
            if dbg == "scanh":
                sdv = bld.v("tensor_copy", bT2[:, 0:2048], bS[:, :], inc="mV")
                bld.wait("gpsimd", "mV", sdv)
                bld.dma("qg", zscr[:, tt * 2048:(tt + 1) * 2048], bT2[:, 0:2048])
            bld.v("tensor_mul", gbf[:, :], bO[:, :], bI[:, :])
            bld.v("tensor_mul", bO[:, :], bO[:, :], bS[:, :])
            zv = bld.v("tensor_add", bT2[:, 0:2048], bT2[:, 0:2048], bO[:, :], inc="mV")
            bld.wait("gpsimd", "mV", zv)
            bld.dma("qg", zscr[:, tt * 2048:(tt + 1) * 2048], bT2[:, 0:2048])
            stv = bld.dma("qg", gscr[:, tt * 2048:(tt + 1) * 2048], gbf[:, :])
            bld.wait("vector", "qg", stv)
            bld.barrier()

        bld.v("tensor_copy", ctmp[:, 0:8], hcar[:, :])
        cv = bld.v("tensor_copy", ctmp[:, 8:16], fcar[:, :], inc="mV")
        bld.wait("gpsimd", "mV", cv)
        bld.dma("qg", ag1i[:, :], ctmp[:, 0:16])
        bld.wait("gpsimd", "qg")
        bld.g("collective_compute", "AllGather", ALU.bypass,
              replica_groups=rg, ins=[ag1i.ap().opt()], outs=[ag1o.ap().opt()], inc="cc")
        bld.wait("sync", "cc")
        bld.dma("ql", carrs[:, :, :], bass.AP(ag1o, 0, [[16, 128], [128 * 16, 8], [1, 16]]))
        bld.wait("vector", "ql")
        for ci in range(NCH):
            bld.v("tensor_copy", ctmp[:, 0:8], carrs[:, :, 8 + ci])
            bld.v("tensor_copy", ctmp[:, 8:16], carrs[:, :, ci])
            bld.v("tensor_tensor_scan", ctmp[:, 16:24], ctmp[:, 0:8], ctmp[:, 8:16],
                  0.0, ALU.mult, ALU.add)
            bld.v("tensor_mul", ctmp[:, 16:24], ctmp[:, 16:24], selm[:, :])
            bld.v("reduce_sum", h_in[:, ci:ci + 1], ctmp[:, 16:24], axis=AXX)
        bld.barrier()
        if dbg == "hin":
            bld.v("tensor_copy", bT2[:, 0:8], h_in[:, :])
            bld.v("tensor_copy", bT2[:, 8:16], hcar[:, :])
            bld.v("tensor_copy", bT2[:, 16:24], fcar[:, :])
            hv_ = bld.v("tensor_copy", bT2[:, 24:152], carrs[:, 0:8, :], inc="mV")
            bld.wait("gpsimd", "mV", hv_)
            dvv = bld.dma("qg", zscr[:, 0:2048], bT2[:, 0:2048])
            bld.wait("sync", "qg", dvv)
            bld.barrier()

        for tt in range(NT1):
            c0 = tt * TT1
            bld.dma("ql", bT2[:, 0:2048], zscr[:, tt * 2048:(tt + 1) * 2048])
            gv = bld.dma("ql", gbf[:, :], gscr[:, tt * 2048:(tt + 1) * 2048])
            bld.wait("vector", "ql", gv)
            for ci in range(NCH):
                bld.v("scalar_tensor_tensor", bT2[:, ci * TT1:(ci + 1) * TT1],
                      gbf[:, ci * TT1:(ci + 1) * TT1], h_in[:, ci:ci + 1],
                      bT2[:, ci * TT1:(ci + 1) * TT1], ALU.mult, ALU.add)
            ln_stats(lambda ci: bT2[:, ci * TT1:(ci + 1) * TT1], xb_, sqn)
            bld.v("tensor_tensor", v3(bF), v3(bT2, pstep=2080), bc3(Abc_off), ALU.mult)
            bld.v("tensor_tensor", v3(bF), v3(bF), bc3(Bbc_off), ALU.add)
            for ci in range(NCH):
                bld.v("tensor_scalar", bF[:, ci * TT1:(ci + 1) * TT1],
                      bF[:, ci * TT1:(ci + 1) * TT1], gam[:, ci:ci + 1], bet[:, ci:ci + 1],
                      ALU.mult, ALU.add)
            bld.v("tensor_tensor", xres_t(tt), xres_t(tt), v3(bF), ALU.add)
            ln_stats(lambda ci: xres[:, ci * TL + c0: ci * TL + c0 + TT1], xb_, sqn)
            bld.v("tensor_copy", a3row[0:1, c0:c0 + TT1], srow[0:1, 0:TT1])
            bld.v("tensor_copy", b3row[0:1, c0:c0 + TT1], srow[0:1, 256:256 + TT1])
            bld.barrier()

        def halo_ag(arow_, brow_, agi, ago, dest):
            vv = bld.v("tensor_copy", browbf[0:1, 0:1], arow_[0:1, TL - 1:TL], inc=None)
            vv = bld.v("tensor_copy", browbf[0:1, 1:2], brow_[0:1, TL - 1:TL], inc="mV")
            bld.wait("tensor", "mV", vv)
            tv = bld.mm(psB[:, 1024:1026], onesrow[0:1, 0:128], browbf[0:1, 0:2],
                        True, True, inc="mT")
            bld.wait("vector", "mT", tv)
            xlast = bass.AP(xres, TL - 1, [[NCH * TL, 128], [TL, NCH]])
            bld.v("tensor_tensor", ctmp[:, 0:8], xlast,
                  bass.AP(psB, 1025, [[2048, 128], [0, NCH]]), ALU.add)
            vv = bld.v("tensor_tensor", ctmp[:, 0:8], ctmp[:, 0:8],
                       bass.AP(psB, 1024, [[2048, 128], [0, NCH]]), ALU.mult, inc="mV")
            bld.wait("gpsimd", "mV", vv)
            bld.dma("qg", agi[:, :], ctmp[:, 0:8])
            bld.wait("gpsimd", "qg")
            bld.g("collective_compute", "AllGather", ALU.bypass,
                  replica_groups=rg, ins=[agi.ap().opt()], outs=[ago.ap().opt()], inc="cc")
            bld.wait("sync", "cc")
            bld.dma("ql", carrs[:, :, 0:8], bass.AP(ago, 0, [[8, 128], [128 * 8, 8], [1, 8]]))
            bld.wait("vector", "ql")
            for ci in range(NCH):
                bld.v("tensor_mul", ctmp[:, 0:8], carrs[:, :, ci], selm[:, :])
                bld.v("reduce_sum", dest[:, ci:ci + 1], ctmp[:, 0:8], axis=AXX)
            bld.barrier()

        if dbg == "a3row":
            bld.v("tensor_copy", bF[0:1, 0:TL], a3row[0:1, :])
            a3v = bld.v("tensor_copy", bF[0:1, TL:2 * TL] if False else bI[0:1, 0:TL],
                        b3row[0:1, :], inc="mV")
            bld.wait("gpsimd", "mV", a3v)
            dv1 = bld.dma("qg", zscr[:, 0:TL], bF[:, 0:TL])
            dv2 = bld.dma("qg", zscr[:, TL:2 * TL], bI[:, 0:TL])
            bld.wait("vector", "qg", dv2)
            bld.barrier()
        halo_ag(a3row, b3row, ag2i, ag2o, haloc)

        def ffn_dbg(buf, tt, n=2048):
            bld.barrier()
            dv = bld.dma("qg", zscr[:, tt * 2048:tt * 2048 + n], buf)
            bld.wait("vector", "qg", dv)
            bld.barrier()

        def ffn_stage(arow_, brow_, halosel, ffw, out_final=False):
            for tt in range(NT1):
                c0 = tt * TT1
                av = srow[0:1, 512:512 + TT1]
                bv = srow[0:1, 768:768 + TT1]
                bld.v("tensor_copy", av, arow_[0:1, c0:c0 + TT1])
                bld.v("tensor_copy", bv, brow_[0:1, c0:c0 + TT1])
                bcast_rows(av, bv)
                if dbg == "ffnbc":
                    bld.v("tensor_copy", bF[:, 0:256], psB[:, 1024:1280])
                    bcv = bld.v("tensor_copy", bF[:, 256:512], psB[:, 1280:1536], inc="mV")
                    bld.wait("gpsimd", "mV", bcv)
                    dv_ = bld.dma("qg", zscr[:, tt * 512:(tt + 1) * 512], bF[:, 0:512])
                    bld.wait("vector", "qg", dv_)
                    bld.barrier()
                xnp = ap3(bT2, 2080, 257, TT1, off=1)
                bld.v("tensor_tensor", xnp, xres_t(tt), bc3(Abc_off), ALU.mult)
                apv = bld.v("tensor_tensor", xnp, xnp, bc3(Bbc_off), ALU.add, inc="mV")
                if dbg == "ffnxn0":
                    bld.wait("gpsimd", "mV", apv)
                    dv0 = bld.dma("qg", zscr[:, tt * 2048:(tt + 1) * 2048], bT2[:, 0:2048])
                    bld.wait("vector", "qg", dv0)
                    bld.barrier()
                hc = bass.AP(bT2, 0, [[2080, 128], [257, NCH]])
                if tt == 0:
                    bld.v("tensor_copy", hc, halosel[:, :])
                else:
                    bld.v("tensor_copy", hc, halocar[:, :])
                lc = bass.AP(bT2, TT1, [[2080, 128], [257, NCH]])
                bld.v("tensor_copy", halocar[:, :], lc)
                xall = ap3(bT2, 2080, 257, TT1, off=1)
                xprev = ap3(bT2, 2080, 257, TT1, off=0)
                if dbg in ("ffnxn", "ffnxn0"):
                    ffn_dbg(bT2[:, 0:2048], tt)
                bld.v("tensor_tensor", v3(bF), xall, xprev, ALU.subtract)
                for ci in range(NCH):
                    xp_ci = bass.AP(bT2, ci * 257, [[2080, 128], [1, TT1]])
                    bld.v("scalar_tensor_tensor", xb_[:, ci * TT1:(ci + 1) * TT1],
                          bF[:, ci * TT1:(ci + 1) * TT1], mur[:, ci:ci + 1], xp_ci,
                          ALU.mult, ALU.add)
                for ci in range(NCH):
                    xp_ci = bass.AP(bT2, ci * 257, [[2080, 128], [1, TT1]])
                    vv = bld.v("scalar_tensor_tensor", sqn[:, ci * TT1:(ci + 1) * TT1],
                               bF[:, ci * TT1:(ci + 1) * TT1], muk[:, ci:ci + 1], xp_ci,
                               ALU.mult, ALU.add, inc=("mV" if ci == NCH - 1 else None))
                bld.wait("tensor", "mV", vv)
                for ci in range(NCH):
                    for e in range(NCH):
                        tv = bld.mm(psA[:, e * TT1:(e + 1) * TT1],
                                    ffw[:, 0 * NCH * D + ci * D + e * 128: 0 * NCH * D + ci * D + (e + 1) * 128],
                                    xb_[:, ci * TT1:(ci + 1) * TT1],
                                    ci == 0 and e % 2 == 0, ci == NCH - 1,
                                    inc=("mT" if (ci == NCH - 1 and e == NCH - 1) else None))
                bld.wait("scalar", "mT", tv)
                rv = bld.act(bI[:, :], psA[:, :], ACT.Sigmoid, inc="mS")
                if dbg == "ffnr":
                    bld.wait("vector", "mS", rv)
                    ffn_dbg(bI[:, :], tt)
                for ci in range(NCH):
                    for e in range(NCH):
                        tv = bld.mm(psB[:, e * TT1:(e + 1) * TT1],
                                    ffw[:, 1 * NCH * D + ci * D + e * 128: 1 * NCH * D + ci * D + (e + 1) * 128],
                                    sqn[:, ci * TT1:(ci + 1) * TT1],
                                    ci == 0 and e % 2 == 0, ci == NCH - 1,
                                    inc=("mT" if (ci == NCH - 1 and e == NCH - 1) else None))
                bld.wait("vector", "mT", tv)
                bld.v("tensor_scalar", bS[:, :], psB[:, 0:2048], cst[:, 5:6], None, ALU.max)
                if dbg == "ffnk":
                    ffn_dbg(bS[:, :], tt)
                kav = bld.v("tensor_mul", sqn[:, :], bS[:, :], bS[:, :], inc="mV")
                bld.wait("tensor", "mV", kav)
                bld.wait("tensor", "mS", rv)
                for ci in range(NCH):
                    for e in range(NCH):
                        tv = bld.mm(psA[:, e * TT1:(e + 1) * TT1],
                                    ffw[:, 2 * NCH * D + ci * D + e * 128: 2 * NCH * D + ci * D + (e + 1) * 128],
                                    sqn[:, ci * TT1:(ci + 1) * TT1],
                                    ci == 0 and e % 2 == 0, ci == NCH - 1,
                                    inc=("mT" if (ci == NCH - 1 and e == NCH - 1) else None))
                bld.wait("vector", "mT", tv)
                bld.wait("vector", "mS", rv)
                bld.v("tensor_tensor", bF[:, :], bI[:, :], psA[:, :], ALU.mult, inc="mV")
                if dbg == "ffnout":
                    ffn_dbg(bF[:, :], tt)
                fv = bld.v("tensor_tensor", xres_t(tt), xres_t(tt), v3(bF), ALU.add, inc="mV")
                if out_final:
                    bld.wait("gpsimd", "mV", fv)
                    xt = bass.AP(outd, tt * TT1, [[NCH * TL, 128], [TL, NCH], [1, TT1]])
                    bld.dma("qg", xt, xres_t(tt))
                bld.barrier()

        bld.flush()
        st1b.close()
        if upto >= 2:
            ffw = sb1("ffw", [128, 3 * NCH * D], BF16)
            for m in range(3):
                for ci in range(NCH):
                    wv = bld.dma("qg", ffw[:, m * NCH * D + ci * D:m * NCH * D + (ci + 1) * D], wfd[m, ci, :, :])
            bld.wait("tensor", "qg", wv)
            bld.wait("vector", "qg", wv)
            ffn_stage(a3row, b3row, haloc, ffw, out_final=False)

        if upto >= 3:
            bld.flush()
            st1.close()
            s3 = ExitStack()
            sb3 = lambda name, shape, dt: s3.enter_context(
                nc.sbuf_tensor("s3_" + name, shape, dt))
            kT = sb3("kT", [128, T], BF16)
            vaug = sb3("vaug", [128, 130 * 128], BF16)
            wqkv = sb3("wqkv", [128, 3 * NCH * 128], BF16)
            masks = sb3("masks", [128, 4 * 512], BF16)
            bqk = sb3("bqk", [128, 2], F32)
            x2nb = sb3("x2nb", [128, 2048], BF16)
            s3pre = ExitStack()
            xb3 = s3pre.enter_context(nc.sbuf_tensor("s3_xb3", [128, 2048], BF16))
            sq3 = s3pre.enter_context(nc.sbuf_tensor("s3_sq3", [128, 2048], BF16))
            t3 = s3pre.enter_context(nc.sbuf_tensor("s3_t3", [128, 2048], F32))

            mv_ = bld.dma("ql", t3[:, :], maskd[:, :])
            bld.dma("ql", bqk[:, :], bqkd[:, :])
            lv = bld.dma("ql", srow[0:1, 0:128], bvrowd[:, :])
            bld.wait("vector", "ql", lv)
            bld.v("tensor_copy", masks[:, :], t3[:, :])
            bvv = bld.v("tensor_copy", browbf[0:1, 1920:2048], srow[0:1, 0:128], inc="mV")
            for m in range(3):
                for ci in range(NCH):
                    wv3 = bld.dma("qg", wqkv[:, m * 1024 + ci * 128:m * 1024 + (ci + 1) * 128],
                                  wqkvd[m, ci, :, :])
            bld.v("memset", bass.AP(vaug, 64, [[130 * 128, 128], [130, 128], [65, 2]]), 1.0)

            for tt in range(NT1):
                c0 = tt * TT1
                ln_stats(lambda ci: xres[:, ci * TL + c0: ci * TL + c0 + TT1], xb3, sq3)
                bld.v("tensor_tensor", v3(t3), xres_t(tt), bc3(Abc_off), ALU.mult)
                xnv3 = bld.v("tensor_tensor", v3(x2nb), v3(t3), bc3(Bbc_off), ALU.add,
                             inc="mV")
                bld.wait("gpsimd", "mV", xnv3)
                agp = bass.AP(ag4i, tt * TT1, [[NCH * TL, 128], [TL, NCH], [1, TT1]])
                agv = bld.dma("qg", agp, v3(x2nb))
                bld.wait("vector", "qg", agv)
            bld.wait("gpsimd", "qg")
            bld.g("collective_compute", "AllGather", ALU.bypass,
                  replica_groups=rg, ins=[ag4i.ap().opt()], outs=[ag4o.ap().opt()],
                  inc="cc")
            bld.wait("sync", "cc")
            bld.wait("tensor", "qg", wv3)
            bld.wait("tensor", "mV", bvv)
            bld.flush()
            s3pre.close()
            xg = sb3("xg", [128, 2 * NCH * 512], BF16)
            pTA = sb3("pTA", [128, 2 * 2048], BF16)
            pTB = sb3("pTB", [128, 2 * 1024], BF16)
            qt2 = sb3("qt2", [128, 2 * 512], BF16)
            osb = sb3("osb", [64, 1024], F32)
            oTb = sb3("oTb", [64, 2 * 1024], BF16)

            xg_v = {}

            def xg_load(gt):
                r_, off = gt // 4, (gt % 4) * 512
                slot = gt % 2
                inap = bass.AP(ag4o, (r_ * 128) * (NCH * TL) + off,
                               [[NCH * TL, 128], [TL, NCH], [1, 512]])
                bld.wait("sync", "mT")
                xg_v[gt] = bld.dma("ql", ap3(xg, 2 * NCH * 512, 512, 512,
                                             off=slot * NCH * 512), inap)

            xg_load(0)
            xg_load(1)
            ev_q = ev_k = ev_v = 0
            for gt in range(32):
                slot = gt % 2
                xgs = lambda ci: xg[:, slot * NCH * 512 + ci * 512:
                                    slot * NCH * 512 + (ci + 1) * 512]
                bld.wait("tensor", "ql", xg_v[gt])
                if gt >= 1:
                    bld.wait("tensor", "mS", ev_k)
                    bld.wait("tensor", "mV", ev_v)
                for ci in range(NCH):
                    bld.mm(psA[:, 0:512], wqkv[:, 0 * 1024 + ci * 128:0 * 1024 + (ci + 1) * 128],
                           xgs(ci), ci == 0, ci == NCH - 1)
                for ci in range(NCH):
                    qkv_t = bld.mm(psA[:, 512:1024],
                                   wqkv[:, 1 * 1024 + ci * 128:1 * 1024 + (ci + 1) * 128],
                                   xgs(ci), ci == 0, ci == NCH - 1,
                                   inc=("mT" if ci == NCH - 1 else None))
                for sub in range(4):
                    for ci in range(NCH):
                        bld.mm(psB[:, sub * 128:(sub + 1) * 128],
                               bass.AP(xg, slot * NCH * 512 + ci * 512 + sub * 128,
                                       [[2 * NCH * 512, 128], [1, 128]]),
                               wqkv[:, 2 * 1024 + ci * 128:2 * 1024 + (ci + 1) * 128],
                               sub == 0 and ci == 0, False)
                    vt = bld.mm(psB[:, sub * 128:(sub + 1) * 128],
                                onesrow[0:1, 0:128], browbf[0:1, 1920:2048],
                                False, True, inc=("mT" if sub == 3 else None))
                bld.wait("scalar", "mT", qkv_t)
                qslot = gt % 2
                if gt >= 2:
                    bld.wait("scalar", "qg", qst_v[gt - 2])
                bld.act(x2nb[:, qslot * 512:(qslot + 1) * 512], psA[:, 0:512],
                        ACT.Identity, bias=bqk[:, 0:1])
                ev_k = bld.act(kT[:, gt * 512:(gt + 1) * 512], psA[:, 512:1024],
                               ACT.Identity, bias=bqk[:, 1:2], inc="mS")
                bld.wait("gpsimd", "mS", ev_k)
                qst_v = qst_v if gt else {}
                qst_v[gt] = bld.dma("qg", qscr[:, gt * 512:(gt + 1) * 512],
                                    x2nb[:, qslot * 512:(qslot + 1) * 512])
                bld.wait("vector", "mT", vt)
                for sub in range(4):
                    tb = (gt * 4 + sub) * 130
                    ev_v = bld.v("tensor_copy",
                                 bass.AP(vaug, tb, [[130 * 128, 128], [65, 2], [1, 64]]),
                                 bass.AP(psB, sub * 128, [[2048, 128], [64, 2], [1, 64]]),
                                 inc=("mV" if sub == 3 else None))
                if gt < 30:
                    xg_load(gt + 2)

            if dbg == "att_xg":
                bld.barrier()
                dvx = bld.dma("qg", zscr[:, :],
                              bass.AP(ag4o, 0, [[NCH * TL, 128], [1, NCH * TL]]))
                bld.wait("vector", "qg", dvx)
                bld.barrier()
            if dbg in ("att_k", "att_q", "att_v"):
                bld.barrier()
                src = {"att_k": kT[:, :], "att_q": None, "att_v": None}[dbg] \
                    if dbg == "att_k" else None
                if dbg == "att_k":
                    dvk = bld.dma("qg", zscr[:, :], kT[:, :])
                elif dbg == "att_q":
                    dvk = bld.dma("qg", zscr[:, :], bass.AP(qscr, 0, [[T, 128], [1, T]]))
                else:
                    dvk = bld.dma("qg", zscr[:, :], vaug[:, 0:T])
                bld.wait("vector", "qg", dvk)
                bld.barrier()

            qld = {}

            def q_load(qt):
                qld[qt] = bld.dma("ql", qt2[:, (qt % 2) * 512:(qt % 2 + 1) * 512],
                                  qscr[:, qt * 512:(qt + 1) * 512])

            osb_v = bld.cnt["mV"]
            bld.wait("sync", "qg", qst_v[0])
            q_load(0)
            bld.wait("sync", "qg", qst_v[1])
            q_load(1)
            exp_v = {}
            qk_done = {}
            ost_v = {}
            exp_last = {0: 0, 1: 0, 2: 0}
            pt_guard = {0: 0, 1: 0}
            REG = {0: (psA, 0), 1: (psA, 1024), 2: (psB, 1024)}
            for qt in range(32):
                nkt = 4 * qt + 4
                qs = qt % 2
                bld.wait("tensor", "ql", qld[qt])
                pend_pv = []
                pv_done = {}
                ngrp = (nkt + 2) // 3
                for m in range(ngrp + 1):
                    if m < ngrp:
                        grp = [kt for kt in range(3 * m, min(3 * m + 3, nkt))]
                        par = m % 2
                        ev = {}
                        for kt in grp:
                            r = kt % 3
                            ps_, off_ = REG[r]
                            bld.wait("tensor", "mS", exp_last[r])
                            if kt < 3 and qt > 0 and r != 2:
                                bld.wait("tensor", "mV", osb_v)
                            bld.mm(ps_[:, off_:off_ + 512],
                                   kT[0:64, kt * 128:(kt + 1) * 128],
                                   qt2[0:64, qs * 512:(qs + 1) * 512], True, True)
                            qkv_ = bld.mm(ps_[:, off_ + 512:off_ + 1024],
                                          kT[64:128, kt * 128:(kt + 1) * 128],
                                          qt2[64:128, qs * 512:(qs + 1) * 512], True, True,
                                          inc="mT")
                            if kt % 3 == 1 or kt == nkt - 1 or kt % 3 == 2:
                                pass
                            ev[kt] = qkv_
                        amem = [kt for kt in grp if kt % 3 != 2]
                        bmem = [kt for kt in grp if kt % 3 == 2]
                        if amem:
                            bld.wait("scalar", "mT", max(ev[amem[-1]], pt_guard[par]))
                            fd = 1024 * len(amem)
                            e_ = bld.act(pTA[:, par * 2048:par * 2048 + fd],
                                         psA[:, 0:fd], ACT.Exp,
                                         bias=cst[:, 3:4], scale=0.125, inc="mS")
                            for kt in amem:
                                exp_v[(qt, kt)] = e_
                                exp_last[kt % 3] = e_
                        if bmem:
                            kt = bmem[0]
                            bld.wait("scalar", "mT", max(ev[kt], pt_guard[par]))
                            e_ = bld.act(pTB[:, par * 1024:(par + 1) * 1024],
                                         psB[:, 1024:2048], ACT.Exp,
                                         bias=cst[:, 3:4], scale=0.125, inc="mS")
                            exp_v[(qt, kt)] = e_
                            exp_last[2] = e_
                        mk = {}
                        for kt in grp:
                            if kt >= 4 * qt:
                                a_ = kt - 4 * qt
                                bld.wait("vector", "mS", exp_v[(qt, kt)])
                                if kt % 3 == 2:
                                    pslice = ap3(pTB, 2 * 1024, 512, 512,
                                                 off=par * 1024, nc_=2)
                                else:
                                    pslice = ap3(pTA, 2 * 2048, 512, 512,
                                                 off=par * 2048 + (kt % 3) * 1024, nc_=2)
                                mk[kt] = bld.v("tensor_tensor", pslice, pslice,
                                               bass.AP(masks, a_ * 512,
                                                       [[4 * 512, 128], [0, 2], [1, 512]]),
                                               ALU.mult, inc="mV")
                        for kt in grp:
                            if kt % 3 == 2:
                                p0 = pTB[:, par * 1024:par * 1024 + 512]
                                p1 = pTB[:, par * 1024 + 512:par * 1024 + 1024]
                            else:
                                b0 = par * 2048 + (kt % 3) * 1024
                                p0 = pTA[:, b0:b0 + 512]
                                p1 = pTA[:, b0 + 512:b0 + 1024]
                            pend_pv.append((m, kt, p0, p1, exp_v[(qt, kt)], mk.get(kt)))
                    flushn = len(pend_pv) if m == ngrp else max(0, len(pend_pv) - 3)
                    for _ in range(flushn):
                        gm, kt, p0, p1, e_, mkv = pend_pv.pop(0)
                        bld.wait("tensor", "mS", e_)
                        if mkv:
                            bld.wait("tensor", "mV", mkv)
                        if kt == 0:
                            bld.wait("tensor", "mV", osb_v)
                        bld.mm(psB[0:65, 0:512], vaug[:, kt * 130:kt * 130 + 65],
                               p0, kt == 0, kt == nkt - 1)
                        pv_t = bld.mm(psB[0:65, 512:1024],
                                      vaug[:, kt * 130 + 65:kt * 130 + 130],
                                      p1, kt == 0, kt == nkt - 1, inc="mT")
                        pt_guard[(kt // 3) % 2] = pv_t
                qk_done[qt] = bld.cnt["mT"]
                if qt < 31:
                    if qt >= 1:
                        bld.wait("sync", "mT", qk_done[qt])
                    bld.wait("sync", "qg", qst_v[qt + 1])
                    q_load(qt + 1)
                bld.wait("vector", "mT", pv_t)
                bld.v("reciprocal", srow[0:1, 0:512], psB[64:65, 0:512])
                bld.v("reciprocal", srow[0:1, 512:1024], psB[64:65, 512:1024])
                bld.v("tensor_copy", browbf[0:1, 0:1024], srow[0:1, 0:1024])
                bld.v("tensor_copy", srow[0:1, 1024:2048], browbf[0:1, 0:1024])
                bld.v("tensor_sub", srow[0:1, 1024:2048], srow[0:1, 0:1024],
                      srow[0:1, 1024:2048])
                rc_v = bld.v("tensor_copy", browbf[0:1, 1024:2048], srow[0:1, 1024:2048],
                             inc="mV")
                bld.wait("tensor", "mV", rc_v)
                bld.wait("tensor", "mS")
                bld.mm(psA[0:64, 0:512], onesrow[0:1, 0:64], browbf[0:1, 0:512],
                       True, False)
                bld.mm(psA[0:64, 512:1024], onesrow[0:1, 0:64], browbf[0:1, 512:1024],
                       True, False)
                bld.mm(psA[0:64, 0:512], onesrow[0:1, 0:64], browbf[0:1, 1024:1536],
                       False, True)
                bc_t = bld.mm(psA[0:64, 512:1024], onesrow[0:1, 0:64],
                              browbf[0:1, 1536:2048], False, True, inc="mT")
                bld.wait("vector", "mT", bc_t)
                if qt >= 2:
                    bld.wait("vector", "qg", ost_v[qt - 2])
                bld.v("tensor_copy", osb[0:64, :], psB[0:64, 0:1024])
                osb_v = bld.v("tensor_mul", oTb[0:64, qs * 1024:(qs + 1) * 1024],
                              osb[0:64, :], psA[0:64, 0:1024], inc="mV")
                bld.wait("gpsimd", "mV", osb_v)
                blk, off = qt // 4, (qt % 4) * 512
                bld.dma("qg", a2ai[blk, 0:64, off:off + 512],
                        oTb[0:64, qs * 1024:qs * 1024 + 512])
                ost_v[qt] = bld.dma("qg", a2ai[blk, 64:128, off:off + 512],
                                    oTb[0:64, qs * 1024 + 512:qs * 1024 + 1024])

            bld.wait("gpsimd", "qg")
            if dbg == "att_o":
                bld.barrier()
                dvo = bld.dma("qg", zscr[:, :],
                              bass.AP(a2ai, 0, [[T, 128], [1, T]]))
                bld.wait("vector", "qg", dvo)
                bld.barrier()

            def _a2a(e, i_=a2ai, o_=a2ao, rg_=rg):
                with allow_ici_all_to_all():
                    return e.collective_compute(
                        "AllToAll", ALU.bypass, replica_groups=rg_,
                        ins=[i_.ap().opt()], outs=[o_.ap().opt()])

            bld.pend["gpsimd"].append(_a2a)
            bld._wrap_inc("gpsimd", "cc")
            bld.wait("sync", "cc")
            bld.flush()
            s3.close()

            s3b = ExitStack()
            woT = s3b.enter_context(nc.sbuf_tensor("s3_woT", [128, NCH * D], BF16))
            obin = s3b.enter_context(nc.sbuf_tensor("s3_obin", [128, NCH * 512], BF16))
            obuf = s3b.enter_context(nc.sbuf_tensor("s3_obuf", [128, 2048], F32))
            bo3b = s3b.enter_context(nc.sbuf_tensor("s3_bo3b", [128, NCH], F32))
            bld.dma("ql", bo3b[:, :], bod[:, :])
            for ci in range(NCH):
                wo_v = bld.dma("qg", woT[:, ci * D:(ci + 1) * D], wod[ci, :, :])
            bld.wait("tensor", "qg", wo_v)
            ob_v = 0
            for tt4 in range(4):
                inap = bass.AP(a2ao, tt4 * 512, [[TL, 128], [128 * TL, NCH], [1, 512]])
                bld.wait("sync", "mT")
                ov = bld.dma("ql", ap3(obin, NCH * 512, 512, 512), inap)
                bld.wait("tensor", "ql", ov)
                if tt4 >= 1:
                    bld.wait("tensor", "mV", ob_v)
                for do in range(NCH):
                    ps = psA if do < 4 else psB
                    col = (do % 4) * 512
                    for oc in range(NCH):
                        op_t = bld.mm(ps[:, col:col + 512],
                                      woT[:, oc * D + do * 128:oc * D + (do + 1) * 128],
                                      obin[:, oc * 512:(oc + 1) * 512], oc == 0,
                                      oc == NCH - 1,
                                      inc=("mT" if oc == NCH - 1 else None))
                bld.wait("scalar", "mT", op_t)
                for half in range(2):
                    bld.wait("scalar", "mV", ob_v)
                    for d4 in range(4):
                        do = half * 4 + d4
                        ps = psA if half == 0 else psB
                        oev = bld.act(obuf[:, d4 * 512:(d4 + 1) * 512],
                                      ps[:, d4 * 512:(d4 + 1) * 512], ACT.Identity,
                                      bias=bo3b[:, do:do + 1],
                                      inc=("mS" if d4 == 3 else None))
                    bld.wait("vector", "mS", oev)
                    xh = bass.AP(xres, half * 4 * TL + tt4 * 512,
                                 [[NCH * TL, 128], [TL, 4], [1, 512]])
                    ob_v = bld.v("tensor_tensor", xh, xh,
                                 ap3(obuf, 2048, 512, 512, nc_=4), ALU.add, inc="mV")
            bld.flush()
            s3b.close()

        if upto >= 4:
            s4 = ExitStack()
            sb4 = lambda name, shape, dt: s4.enter_context(
                nc.sbuf_tensor("s4_" + name, shape, dt))
            bF = sb4("bF", [128, 2048], F32)
            bI = sb4("bI", [128, 2048], F32)
            bS = sb4("bS", [128, 2048], F32)
            bT2 = sb4("bT2", [128, 2080], F32)
            xb_ = sb4("xb", [128, 2048], BF16)
            sqn = sb4("sqn", [128, 2048], BF16)
            gbf = sb4("gbf", [128, 2048], BF16)
            ffw = sb4("ffw", [128, 3 * NCH * D], BF16)
            a3row = sb4("a3row", [1, TL], F32)
            b3row = sb4("b3row", [1, TL], F32)
            for m in range(3):
                for ci in range(NCH):
                    wv4 = bld.dma("qg", ffw[:, m * NCH * D + ci * D:m * NCH * D + (ci + 1) * D],
                                  wfd[m, ci, :, :])
            bld.wait("tensor", "qg", wv4)
            bld.wait("vector", "qg", wv4)
            for tt in range(NT1):
                c0 = tt * TT1
                ln_stats(lambda ci: xres[:, ci * TL + c0: ci * TL + c0 + TT1], xb_, sqn)
                bld.v("tensor_copy", a3row[0:1, c0:c0 + TT1], srow[0:1, 0:TT1])
                bld.v("tensor_copy", b3row[0:1, c0:c0 + TT1], srow[0:1, 256:256 + TT1])
                bld.barrier()
            halo_ag(a3row, b3row, ag3i, ag3o, haloc)
            ffn_stage(a3row, b3row, haloc, ffw, out_final=True)
            bld.wait("gpsimd", "qg")
            bld.flush()
            s4.close()

        if upto < 4:
            bld.barrier()
            if dbg is not None:
                bld.dma("qg", outd[:, :], zscr[:, :])
            else:
                for tt in range(NT1):
                    xt = bass.AP(outd, tt * TT1, [[NCH * TL, 128], [TL, NCH], [1, TT1]])
                    bld.dma("qg", xt, xres_t(tt))
            bld.wait("gpsimd", "qg")
            bld.flush()
            if upto < 3:
                st1.close()
        perm.close()
    return nc


def prep_inputs(core, inputs):
    c = core
    x = np.asarray(inputs["L0_attn_a_x"]).reshape(T, D)
    d = {}
    d["x0"] = fm_pack(np.asarray(x[c * TL:(c + 1) * TL], np.float32))
    w1 = np.stack([np.asarray(inputs["L0_attn_a_Wi"]), np.asarray(inputs["L0_attn_a_Wf"]),
                   np.asarray(inputs["L0_attn_a_Wo"]), np.asarray(inputs["L0_attn_a_Ws"])])
    d["w1"] = np.ascontiguousarray(w1.reshape(4, NCH, 128, D), np.float32)
    d["b1"] = np.ascontiguousarray(np.concatenate(
        [np.asarray(inputs["L0_attn_a_bi"]), np.asarray(inputs["L0_attn_a_bf"]),
         np.asarray(inputs["L0_attn_a_bo"]), np.asarray(inputs["L0_attn_a_bs"])])[None, :],
        np.float32)
    pp = lambda v: np.ascontiguousarray(np.asarray(v).reshape(NCH, 128).T, np.float32)
    d["gam"] = pp(inputs["L0_attn_a_gamma"])
    d["bet"] = pp(inputs["L0_attn_a_beta"])
    d["mur"] = pp(inputs["L0_ffn_mu_r"])
    d["muk"] = pp(inputs["L0_ffn_mu_k"])
    wf = np.stack([np.asarray(inputs["L0_ffn_W_r"]).T,
                   np.asarray(inputs["L0_ffn_W_k"]).T,
                   np.asarray(inputs["L0_ffn_W_v"]).T])
    d["wf"] = np.ascontiguousarray(wf.reshape(3, NCH, 128, D), np.float32)
    r0, r1 = 128 * c, 128 * (c + 1)
    wq = np.asarray(inputs["L0_attn_b_Wq"]).T[:, r0:r1]
    wk = np.asarray(inputs["L0_attn_b_Wk"]).T[:, r0:r1]
    wv = np.asarray(inputs["L0_attn_b_Wv"]).T[:, r0:r1]
    d["wqkv"] = np.ascontiguousarray(np.stack([wq, wk, wv]).reshape(3, NCH, 128, 128),
                                     np.float32)
    d["bqk"] = np.ascontiguousarray(
        np.stack([np.asarray(inputs["L0_attn_b_bq"])[r0:r1],
                  np.asarray(inputs["L0_attn_b_bk"])[r0:r1]], axis=1), np.float32)
    d["bvrow"] = np.ascontiguousarray(np.asarray(inputs["L0_attn_b_bv"])[None, r0:r1],
                                      np.float32)
    d["wo"] = np.ascontiguousarray(
        np.asarray(inputs["L0_attn_b_Wo"]).T.reshape(NCH, 128, D), np.float32)
    d["bo"] = pp(inputs["L0_attn_b_bo"])
    mk = np.zeros((128, 4, 512), np.float32)
    q = np.arange(512)
    for a in range(4):
        for k in range(128):
            mk[k, a] = (q >= 128 * a + k)
    d["mask"] = np.ascontiguousarray(mk.reshape(128, 4 * 512))
    sel = np.zeros((128, 8), np.float32)
    if c > 0:
        sel[:, c - 1] = 1.0
    d["sel"] = sel
    return d


def kernel(**inputs):
    nc = build(_UPTO, _DBG)
    in_maps = [prep_inputs(c, inputs) for c in range(NCORE)]
    res = run_bass_kernel_spmd(nc, in_maps, core_ids=list(range(NCORE)))
    if _DBG is not None:
        return [res.results[c]["out"] for c in range(NCORE)]
    shards = [fm_unpack(res.results[c]["out"], TL) for c in range(NCORE)]
    return np.concatenate(shards, axis=0).reshape(1, T, D).astype(np.float32)


# version 04
# speedup vs baseline: 1.6185x; 1.1341x over previous
import numpy as np
import ml_dtypes
import concourse.bass as bass
import concourse.mybir as mybir
from concourse.bass_utils import run_bass_kernel_spmd
from concourse.aot_env import allow_ici_all_to_all
from contextlib import ExitStack

F32 = mybir.dt.float32
BF16 = mybir.dt.bfloat16
ALU = mybir.AluOpType
ACT = mybir.ActivationFunctionType
AXX = mybir.AxisListType.X

NCORE = 8
D = 1024
T = 16384
TL = T // NCORE
NCH = 8
TT1 = 256
NT1 = TL // TT1
LN_EPS = 1e-5

_UPTO = 4
_DBG = None


class Bld:
    ENGS = ("sync", "gpsimd", "tensor", "scalar", "vector")

    def __init__(self, nc, block, sems):
        self.nc, self.block = nc, block
        self.sems = sems
        self.cnt = {k: 0 for k in sems}
        self.pend = {e: [] for e in self.ENGS}

    def flush(self):
        for e in self.ENGS:
            lst = self.pend[e]
            if not lst:
                continue
            self.pend[e] = []
            getattr(self.block, e)(lambda eng, lst=lst: [f(eng) for f in lst])

    def _p(self, eng, fn):
        self.pend[eng].append(fn)

    def _wrap_inc(self, eng, sem, n=1):
        self.cnt[sem] += n
        h = self.sems[sem]
        f = self.pend[eng][-1]
        self.pend[eng][-1] = lambda e, f=f, h=h, n=n: f(e).then_inc(h, n)
        return self.cnt[sem]

    def wait(self, eng, sem, val=None):
        if val is None:
            val = self.cnt[sem]
        if val <= 0:
            return
        assert val <= self.cnt[sem], (sem, val, self.cnt[sem])
        h = self.sems[sem]
        self._p(eng, lambda e, h=h, v=val: e.wait_ge(h, v))

    def dma(self, q, out, in_):
        eng = "sync" if q == "ql" else "gpsimd"
        self._p(eng, lambda e, o=out, i=in_: e.dma_start(out=o, in_=i))
        return self._wrap_inc(eng, q, 16)

    def mm(self, out, lhsT, rhs, start, stop, inc=None):
        self._p("tensor", lambda e, o=out, l=lhsT, r=rhs, s0=start, s1=stop:
                e.matmul(o, l, r, start=s0, stop=s1))
        if inc:
            return self._wrap_inc("tensor", inc)

    def act(self, out, in_, func, bias=0.0, scale=1.0, inc=None):
        self._p("scalar", lambda e, o=out, i=in_, f=func, b=bias, s=scale:
                e.activation(out=o, in_=i, func=f, bias=b, scale=s))
        if inc:
            return self._wrap_inc("scalar", inc)

    def v(self, meth, *args, inc=None, **kw):
        self._p("vector", lambda e, m=meth, a=args, k=kw: getattr(e, m)(*a, **k))
        try:
            fs = args[0].free_size() if hasattr(args[0], "free_size") else 999
        except Exception:
            fs = 999
        r = None
        if inc:
            r = self._wrap_inc("vector", inc)
        if fs <= 64 and meth != "memset":
            for _ in range(4):
                self._p("vector", lambda e: e.engine_nop())
        return r

    def g(self, meth, *args, inc=None, **kw):
        self._p("gpsimd", lambda e, m=meth, a=args, k=kw: getattr(e, m)(*a, **k))
        if inc:
            return self._wrap_inc("gpsimd", inc)

    def barrier(self):
        engs = self.ENGS
        vals = {}
        for e in engs:
            sem = "b_" + e
            h = self.sems[sem]
            self.cnt[sem] += 1
            vals[e] = self.cnt[sem]
            self._p(e, lambda eng, h=h: eng.nop().then_inc(h, 1))
        for e in engs:
            for o in engs:
                if o != e:
                    self.wait(e, "b_" + o, vals[o])


def fm_pack(a):
    tn = a.shape[0]
    return np.ascontiguousarray(
        a.T.reshape(NCH, 128, tn).transpose(1, 0, 2).reshape(128, NCH * tn))


def fm_unpack(a, tn):
    return np.ascontiguousarray(
        a.reshape(128, NCH, tn).transpose(1, 0, 2).reshape(D, tn).T)


def build(upto=4, dbg=None):
    nc = bass.Bass("TRN2", target_bir_lowering=False, debug=False)
    DT = nc.dram_tensor

    x0d = DT("x0", [128, NCH * TL], F32, kind="ExternalInput")
    w1d = DT("w1", [4, NCH, 128, D], F32, kind="ExternalInput")
    b1d = DT("b1", [1, 4 * D], F32, kind="ExternalInput")
    gamd = DT("gam", [128, NCH], F32, kind="ExternalInput")
    betd = DT("bet", [128, NCH], F32, kind="ExternalInput")
    wfd = DT("wf", [3, NCH, 128, D], F32, kind="ExternalInput")
    murd = DT("mur", [128, NCH], F32, kind="ExternalInput")
    mukd = DT("muk", [128, NCH], F32, kind="ExternalInput")
    wqkvd = DT("wqkv", [3, NCH, 128, 128], F32, kind="ExternalInput")
    bqkd = DT("bqk", [128, 2], F32, kind="ExternalInput")
    bvrowd = DT("bvrow", [1, 128], F32, kind="ExternalInput")
    wod = DT("wo", [NCH, 128, D], F32, kind="ExternalInput")
    bod = DT("bo", [128, NCH], F32, kind="ExternalInput")
    maskd = DT("mask", [128, 4 * 512], F32, kind="ExternalInput")
    seld = DT("sel", [128, 8], F32, kind="ExternalInput")
    outd = DT("out", [128, NCH * TL], F32, kind="ExternalOutput")

    zscr = DT("zscr", [128, NCH * TL], F32)
    gscr = DT("gscr", [128, NCH * TL], BF16)
    ag1i = DT("ag1i", [128, 16], F32)
    ag1o = DT("ag1o", [NCORE * 128, 16], F32)
    ag2i = DT("ag2i", [128, 8], F32)
    ag2o = DT("ag2o", [NCORE * 128, 8], F32)
    ag3i = DT("ag3i", [128, 8], F32)
    ag3o = DT("ag3o", [NCORE * 128, 8], F32)
    ag4i = DT("ag4i", [128, NCH * TL], BF16)
    ag4o = DT("ag4o", [NCORE * 128, NCH * TL], BF16)
    qscr = DT("qscr", [128, T], BF16)
    a2ai = DT("a2ai", [NCORE, 128, TL], BF16)
    a2ao = DT("a2ao", [NCORE, 128, TL], BF16)
    rg = [list(range(NCORE))]

    ctx0 = ExitStack()
    sems = {}
    for s in ("ql", "qg", "cc", "mT", "mS", "mV",
              "b_sync", "b_gpsimd", "b_tensor", "b_scalar", "b_vector"):
        sems[s] = ctx0.enter_context(nc.semaphore(s))

    with ctx0, nc.Block() as block:
        bld = Bld(nc, block, sems)
        perm = ExitStack()
        sb = lambda name, shape, dt: perm.enter_context(nc.sbuf_tensor("s_" + name, shape, dt))
        xres = sb("xres", [128, NCH * TL], F32)
        onescol = sb("onescol", [128, 1], BF16)
        onesrow = sb("onesrow", [1, 512], BF16)
        ones2k = sb("ones2k", [128, TT1], F32)
        cst = sb("cst", [128, 8], F32)
        b1row = sb("b1row", [1, 4 * D], BF16)
        gam = sb("gam", [128, NCH], F32)
        bet = sb("bet", [128, NCH], F32)
        mur = sb("mur", [128, NCH], F32)
        muk = sb("muk", [128, NCH], F32)
        selm = sb("selm", [128, 8], F32)
        hcar = sb("hcar", [128, 8], F32)
        fcar = sb("fcar", [128, 8], F32)
        h_in = sb("h_in", [128, 8], F32)
        halocar = sb("halocar", [128, 8], F32)
        haloc = sb("haloc", [128, 8], F32)
        carrs = sb("carrs", [128, 8, 16], F32)
        ctmp = sb("ctmp", [128, 24], F32)
        srow = sb("srow", [1, 3072], F32)
        browbf = sb("browbf", [1, 2048], BF16)
        psA = perm.enter_context(nc.psum_tensor("psA", [128, 2048], F32))
        psB = perm.enter_context(nc.psum_tensor("psB", [128, 2048], F32))

        bld.dma("ql", xres[:, :], x0d[:, :])
        bld.dma("ql", gam[:, :], gamd[:, :])
        bld.dma("ql", bet[:, :], betd[:, :])
        bld.dma("ql", mur[:, :], murd[:, :])
        bld.dma("ql", muk[:, :], mukd[:, :])
        bld.dma("ql", selm[:, :], seld[:, :])
        b1v = bld.dma("qg", b1row[:, :], b1d[:, :])
        ld0 = bld.cnt["ql"]
        bld.v("memset", onescol[:, :], 1.0)
        bld.v("memset", onesrow[:, :], 1.0)
        bld.v("memset", ones2k[:, :], 1.0)
        bld.v("memset", hcar[:, :], 0.0)
        bld.v("memset", fcar[:, :], 1.0)
        bld.v("memset", cst[:, 0:1], LN_EPS)
        bld.v("memset", cst[:, 1:2], 1.0 / D)
        bld.v("memset", cst[:, 2:3], -1.0)
        bld.v("memset", cst[:, 3:4], -4.0)
        bld.v("memset", cst[:, 4:5], 0.125)
        bld.v("memset", cst[:, 5:6], 0.0)
        bld.v("memset", cst[:, 6:7], -0.5)
        bld.wait("vector", "ql", ld0)
        bld.wait("tensor", "qg", b1v)

        def ap3(buf, pstep, cstep, n, off=0, nc_=NCH):
            return bass.AP(buf, off, [[pstep, 128], [cstep, nc_], [1, n]])

        def xres_t(tt, n=TT1):
            return ap3(xres, NCH * TL, TL, n, off=tt * n)

        def bc3(col_off, n=TT1):
            return bass.AP(psB, col_off, [[2048, 128], [0, NCH], [1, n]])

        def v3(buf, n=TT1, pstep=2048):
            return ap3(buf, pstep, n, n)

        Abc_off, Bbc_off = 1024, 1280

        def ln_stats(src_of_ci, xb, sq, n=TT1):
            for ci in range(NCH):
                bld.v("tensor_copy", xb[:, ci * n:(ci + 1) * n], src_of_ci(ci))
            for ci in range(NCH):
                vv = bld.v("tensor_mul", sq[:, ci * n:(ci + 1) * n],
                           xb[:, ci * n:(ci + 1) * n], xb[:, ci * n:(ci + 1) * n],
                           inc=("mV" if ci == NCH - 1 else None))
            bld.wait("tensor", "mV", vv)
            for ci in range(NCH):
                bld.mm(psB[0:1, 1536:1536 + n], onescol[:, :], xb[:, ci * n:(ci + 1) * n],
                       ci == 0, ci == NCH - 1)
            for ci in range(NCH):
                tv = bld.mm(psB[0:1, 1792:1792 + n], onescol[:, :], sq[:, ci * n:(ci + 1) * n],
                            ci == 0, ci == NCH - 1,
                            inc=("mT" if ci == NCH - 1 else None))
            bld.wait("vector", "mT", tv)
            m_ = srow[0:1, 2048:2048 + n]
            msq = srow[0:1, 2304:2304 + n]
            bld.v("tensor_scalar", m_, psB[0:1, 1536:1536 + n], cst[0:1, 1:2], None, ALU.mult)
            bld.v("tensor_scalar", msq, psB[0:1, 1792:1792 + n], cst[0:1, 1:2], None, ALU.mult)
            var = srow[0:1, 2560:2560 + n]
            bld.v("tensor_mul", var, m_, m_)
            bld.v("tensor_sub", var, msq, var)
            bld.v("tensor_scalar", var, var, cst[0:1, 0:1], None, ALU.add)
            inv = srow[0:1, 2816:2816 + n]
            sv = bld.v("reciprocal", inv, var, inc="mV")
            bld.wait("scalar", "mV", sv)
            arow = srow[0:1, 0:n]
            av = bld.act(arow, inv, ACT.Sqrt, inc="mS")
            bld.wait("vector", "mS", av)
            t1 = srow[0:1, 2304:2304 + n]
            bld.v("tensor_mul", t1, arow, arow)
            bld.v("tensor_mul", t1, t1, var)
            bld.v("tensor_scalar", t1, t1, cst[0:1, 6:7], 1.5, ALU.mult, ALU.add)
            bld.v("tensor_mul", arow, arow, t1)
            brow = srow[0:1, 256:256 + n]
            bld.v("tensor_mul", brow, m_, arow)
            bld.v("tensor_scalar", brow, brow, cst[0:1, 2:3], None, ALU.mult)
            bcast_rows(arow, brow, n)

        def bcast_rows(arow, brow, n=TT1):
            bld.v("tensor_copy", browbf[0:1, 0:n], arow)
            ahif = srow[0:1, 2560:2560 + n]
            bld.v("tensor_copy", ahif, browbf[0:1, 0:n])
            alo = srow[0:1, 2816:2816 + n]
            bld.v("tensor_sub", alo, arow, ahif)
            bld.v("tensor_copy", browbf[0:1, 256:256 + n], alo)
            vv = bld.v("tensor_copy", browbf[0:1, 512:512 + n], brow, inc="mV")
            bld.wait("tensor", "mV", vv)
            bld.mm(psB[:, Abc_off:Abc_off + n], onesrow[0:1, 0:128], browbf[0:1, 0:n],
                   True, False)
            bld.mm(psB[:, Abc_off:Abc_off + n], onesrow[0:1, 0:128], browbf[0:1, 256:256 + n],
                   False, True)
            tv = bld.mm(psB[:, Bbc_off:Bbc_off + n], onesrow[0:1, 0:128],
                        browbf[0:1, 512:512 + n], True, True, inc="mT")
            bld.wait("vector", "mT", tv)

        st1 = ExitStack()
        sb1 = lambda name, shape, dt: st1.enter_context(nc.sbuf_tensor("s_" + name, shape, dt))
        bF = sb1("bF", [128, 2048], F32)
        bI = sb1("bI", [128, 2048], F32)
        bS = sb1("bS", [128, 2048], F32)
        bT2 = sb1("bT2", [128, 2080], F32)
        xb_ = sb1("xb", [128, 2048], BF16)
        sqn = sb1("sqn", [128, 2048], BF16)
        gbf = sb1("gbf", [128, 2048], BF16)
        a3row = sb1("a3row", [1, TL], F32)
        b3row = sb1("b3row", [1, TL], F32)
        st1b = ExitStack()
        wbuf = st1b.enter_context(nc.sbuf_tensor("s_wbuf", [128, 2, D], BF16))
        bO = st1b.enter_context(nc.sbuf_tensor("s_bO", [128, 2048], F32))

        gate_dst = [psA, psB, psA, psB]
        gate_fun = [ACT.Sigmoid, ACT.Sigmoid, ACT.Sigmoid, ACT.Tanh]
        gate_buf = [bI, bF, bO, bS]

        worder = [(tt, g, ci) for tt in range(NT1) for g in range(4) for ci in range(NCH)]
        evac_v = {}
        wld_v = {}
        wuse_v = {}
        wload_ptr = [0]

        def w1_issue_next():
            i = wload_ptr[0]
            if i >= len(worder):
                return
            wload_ptr[0] += 1
            tt, g, ci = worder[i]
            slot = i % 2
            if i >= 2:
                bld.wait("gpsimd", "mT", wuse_v[i - 2])
            wld_v[i] = bld.dma("qg", wbuf[:, slot, :], w1d[g, ci, :, :])

        w1_issue_next()
        w1_issue_next()

        gates_done_prev = 0
        for tt in range(NT1):
            c0 = tt * TT1
            if tt > 0:
                bld.wait("vector", "mT", gates_done_prev)
                bld.wait("tensor", "mS")
            ln_stats(lambda ci: xres[:, ci * TL + c0: ci * TL + c0 + TT1], xb_, sqn)
            bld.v("tensor_tensor", v3(bT2, pstep=2080), xres_t(tt), bc3(Abc_off), ALU.mult)
            bld.v("tensor_tensor", v3(bT2, pstep=2080), v3(bT2, pstep=2080),
                  bc3(Bbc_off), ALU.add)
            xnv = bld.v("tensor_copy", xb_[:, :], bT2[:, 0:2048], inc="mV")
            bld.wait("tensor", "mV", xnv)
            if dbg == "xn":
                bld.wait("gpsimd", "mV", xnv)
                bld.dma("qg", zscr[:, tt * 2048:(tt + 1) * 2048], bT2[:, 0:2048])

            for g in range(4):
                dst = gate_dst[g]
                for ci in range(NCH):
                    i = worder.index((tt, g, ci))
                    bld.wait("tensor", "qg", wld_v[i])
                    if ci == 0 and g >= 2:
                        bld.wait("tensor", "mS", evac_v[g - 2])
                    elif ci == 0 and tt > 0:
                        bld.wait("tensor", "mS", evac_prev[g + 2])
                    slot = i % 2
                    for e in range(NCH):
                        tv = bld.mm(dst[:, e * TT1:(e + 1) * TT1],
                                    wbuf[:, slot, e * 128:(e + 1) * 128],
                                    xb_[:, ci * TT1:(ci + 1) * TT1],
                                    ci == 0 and e % 2 == 0, False,
                                    inc=("mT" if e == NCH - 1 else None))
                    wuse_v[i] = tv
                    w1_issue_next()
                for e in range(NCH):
                    tv = bld.mm(dst[:, e * TT1:(e + 1) * TT1],
                                b1row[0:1, g * D + e * 128: g * D + (e + 1) * 128],
                                onesrow[0:1, 0:TT1], False, True,
                                inc=("mT" if e == NCH - 1 else None))
                bld.wait("scalar", "mT", tv)
                evac_v[g] = bld.act(gate_buf[g][:, :], dst[:, :], gate_fun[g], inc="mS")
            bld.wait("vector", "mS", evac_v[3])
            evac_prev = dict(evac_v)

            if dbg == "gatef":
                fdv = bld.v("tensor_copy", bT2[:, 0:2048], bF[:, :], inc="mV")
                bld.wait("gpsimd", "mV", fdv)
                bld.dma("qg", zscr[:, tt * 2048:(tt + 1) * 2048], bT2[:, 0:2048])
            bld.v("tensor_mul", bI[:, :], bI[:, :], bS[:, :])
            for ci in range(NCH):
                bld.v("tensor_tensor_scan", bS[:, ci * TT1:(ci + 1) * TT1],
                      bF[:, ci * TT1:(ci + 1) * TT1], bI[:, ci * TT1:(ci + 1) * TT1],
                      hcar[:, ci:ci + 1], ALU.mult, ALU.add)
            for ci in range(NCH):
                bld.v("tensor_tensor_scan", bI[:, ci * TT1:(ci + 1) * TT1],
                      bF[:, ci * TT1:(ci + 1) * TT1], ones2k[:, 0:TT1],
                      fcar[:, ci:ci + 1], ALU.mult, ALU.mult)
            hS = bass.AP(bS, TT1 - 1, [[2048, 128], [TT1, NCH]])
            hI = bass.AP(bI, TT1 - 1, [[2048, 128], [TT1, NCH]])
            bld.v("tensor_copy", hcar[:, :], hS)
            bld.v("tensor_copy", fcar[:, :], hI)
            if dbg == "scanh":
                sdv = bld.v("tensor_copy", bT2[:, 0:2048], bS[:, :], inc="mV")
                bld.wait("gpsimd", "mV", sdv)
                bld.dma("qg", zscr[:, tt * 2048:(tt + 1) * 2048], bT2[:, 0:2048])
            bld.v("tensor_mul", gbf[:, :], bO[:, :], bI[:, :])
            bld.v("tensor_mul", bO[:, :], bO[:, :], bS[:, :])
            zv = bld.v("tensor_add", bT2[:, 0:2048], bT2[:, 0:2048], bO[:, :], inc="mV")
            gates_done_prev = bld.cnt["mT"]
            bld.wait("gpsimd", "mV", zv)
            bld.dma("qg", zscr[:, tt * 2048:(tt + 1) * 2048], bT2[:, 0:2048])
            stv = bld.dma("qg", gscr[:, tt * 2048:(tt + 1) * 2048], gbf[:, :])
            bld.wait("vector", "qg", stv)

        bld.v("tensor_copy", ctmp[:, 0:8], hcar[:, :])
        cv = bld.v("tensor_copy", ctmp[:, 8:16], fcar[:, :], inc="mV")
        bld.wait("gpsimd", "mV", cv)
        bld.dma("qg", ag1i[:, :], ctmp[:, 0:16])
        bld.wait("gpsimd", "qg")
        bld.g("collective_compute", "AllGather", ALU.bypass,
              replica_groups=rg, ins=[ag1i.ap().opt()], outs=[ag1o.ap().opt()], inc="cc")
        bld.wait("sync", "cc")
        bld.dma("ql", carrs[:, :, :], bass.AP(ag1o, 0, [[16, 128], [128 * 16, 8], [1, 16]]))
        bld.wait("vector", "ql")
        for ci in range(NCH):
            bld.v("tensor_copy", ctmp[:, 0:8], carrs[:, :, 8 + ci])
            bld.v("tensor_copy", ctmp[:, 8:16], carrs[:, :, ci])
            bld.v("tensor_tensor_scan", ctmp[:, 16:24], ctmp[:, 0:8], ctmp[:, 8:16],
                  0.0, ALU.mult, ALU.add)
            bld.v("tensor_mul", ctmp[:, 16:24], ctmp[:, 16:24], selm[:, :])
            bld.v("reduce_sum", h_in[:, ci:ci + 1], ctmp[:, 16:24], axis=AXX)
        bld.barrier()
        if dbg == "hin":
            bld.v("tensor_copy", bT2[:, 0:8], h_in[:, :])
            bld.v("tensor_copy", bT2[:, 8:16], hcar[:, :])
            bld.v("tensor_copy", bT2[:, 16:24], fcar[:, :])
            hv_ = bld.v("tensor_copy", bT2[:, 24:152], carrs[:, 0:8, :], inc="mV")
            bld.wait("gpsimd", "mV", hv_)
            dvv = bld.dma("qg", zscr[:, 0:2048], bT2[:, 0:2048])
            bld.wait("sync", "qg", dvv)
            bld.barrier()

        for tt in range(NT1):
            c0 = tt * TT1
            bld.wait("sync", "mV")
            bld.dma("ql", bT2[:, 0:2048], zscr[:, tt * 2048:(tt + 1) * 2048])
            gv = bld.dma("ql", gbf[:, :], gscr[:, tt * 2048:(tt + 1) * 2048])
            bld.wait("vector", "ql", gv)
            for ci in range(NCH):
                bld.v("scalar_tensor_tensor", bT2[:, ci * TT1:(ci + 1) * TT1],
                      gbf[:, ci * TT1:(ci + 1) * TT1], h_in[:, ci:ci + 1],
                      bT2[:, ci * TT1:(ci + 1) * TT1], ALU.mult, ALU.add)
            ln_stats(lambda ci: bT2[:, ci * TT1:(ci + 1) * TT1], xb_, sqn)
            bld.v("tensor_tensor", v3(bF), v3(bT2, pstep=2080), bc3(Abc_off), ALU.mult)
            bld.v("tensor_tensor", v3(bF), v3(bF), bc3(Bbc_off), ALU.add)
            for ci in range(NCH):
                bld.v("tensor_scalar", bF[:, ci * TT1:(ci + 1) * TT1],
                      bF[:, ci * TT1:(ci + 1) * TT1], gam[:, ci:ci + 1], bet[:, ci:ci + 1],
                      ALU.mult, ALU.add)
            bld.v("tensor_tensor", xres_t(tt), xres_t(tt), v3(bF), ALU.add)
            ln_stats(lambda ci: xres[:, ci * TL + c0: ci * TL + c0 + TT1], xb_, sqn)
            bld.v("tensor_copy", a3row[0:1, c0:c0 + TT1], srow[0:1, 0:TT1])
            bld.v("tensor_copy", b3row[0:1, c0:c0 + TT1], srow[0:1, 256:256 + TT1])

        def halo_ag(arow_, brow_, agi, ago, dest):
            vv = bld.v("tensor_copy", browbf[0:1, 0:1], arow_[0:1, TL - 1:TL], inc=None)
            vv = bld.v("tensor_copy", browbf[0:1, 1:2], brow_[0:1, TL - 1:TL], inc="mV")
            bld.wait("tensor", "mV", vv)
            tv = bld.mm(psB[:, 1024:1026], onesrow[0:1, 0:128], browbf[0:1, 0:2],
                        True, True, inc="mT")
            bld.wait("vector", "mT", tv)
            xlast = bass.AP(xres, TL - 1, [[NCH * TL, 128], [TL, NCH]])
            bld.v("tensor_tensor", ctmp[:, 0:8], xlast,
                  bass.AP(psB, 1025, [[2048, 128], [0, NCH]]), ALU.add)
            vv = bld.v("tensor_tensor", ctmp[:, 0:8], ctmp[:, 0:8],
                       bass.AP(psB, 1024, [[2048, 128], [0, NCH]]), ALU.mult, inc="mV")
            bld.wait("gpsimd", "mV", vv)
            bld.dma("qg", agi[:, :], ctmp[:, 0:8])
            bld.wait("gpsimd", "qg")
            bld.g("collective_compute", "AllGather", ALU.bypass,
                  replica_groups=rg, ins=[agi.ap().opt()], outs=[ago.ap().opt()], inc="cc")
            bld.wait("sync", "cc")
            bld.dma("ql", carrs[:, :, 0:8], bass.AP(ago, 0, [[8, 128], [128 * 8, 8], [1, 8]]))
            bld.wait("vector", "ql")
            for ci in range(NCH):
                bld.v("tensor_mul", ctmp[:, 0:8], carrs[:, :, ci], selm[:, :])
                bld.v("reduce_sum", dest[:, ci:ci + 1], ctmp[:, 0:8], axis=AXX)
            bld.barrier()

        if dbg == "a3row":
            bld.v("tensor_copy", bF[0:1, 0:TL], a3row[0:1, :])
            a3v = bld.v("tensor_copy", bF[0:1, TL:2 * TL] if False else bI[0:1, 0:TL],
                        b3row[0:1, :], inc="mV")
            bld.wait("gpsimd", "mV", a3v)
            dv1 = bld.dma("qg", zscr[:, 0:TL], bF[:, 0:TL])
            dv2 = bld.dma("qg", zscr[:, TL:2 * TL], bI[:, 0:TL])
            bld.wait("vector", "qg", dv2)
            bld.barrier()
        halo_ag(a3row, b3row, ag2i, ag2o, haloc)

        def ffn_dbg(buf, tt, n=2048):
            bld.barrier()
            dv = bld.dma("qg", zscr[:, tt * 2048:tt * 2048 + n], buf)
            bld.wait("vector", "qg", dv)
            bld.barrier()

        def ffn_stage(arow_, brow_, halosel, ffw, out_final=False):
            for tt in range(NT1):
                c0 = tt * TT1
                av = srow[0:1, 512:512 + TT1]
                bv = srow[0:1, 768:768 + TT1]
                bld.v("tensor_copy", av, arow_[0:1, c0:c0 + TT1])
                bld.v("tensor_copy", bv, brow_[0:1, c0:c0 + TT1])
                bcast_rows(av, bv)
                if dbg == "ffnbc":
                    bld.v("tensor_copy", bF[:, 0:256], psB[:, 1024:1280])
                    bcv = bld.v("tensor_copy", bF[:, 256:512], psB[:, 1280:1536], inc="mV")
                    bld.wait("gpsimd", "mV", bcv)
                    dv_ = bld.dma("qg", zscr[:, tt * 512:(tt + 1) * 512], bF[:, 0:512])
                    bld.wait("vector", "qg", dv_)
                    bld.barrier()
                xnp = ap3(bT2, 2080, 257, TT1, off=1)
                bld.v("tensor_tensor", xnp, xres_t(tt), bc3(Abc_off), ALU.mult)
                apv = bld.v("tensor_tensor", xnp, xnp, bc3(Bbc_off), ALU.add, inc="mV")
                if dbg == "ffnxn0":
                    bld.wait("gpsimd", "mV", apv)
                    dv0 = bld.dma("qg", zscr[:, tt * 2048:(tt + 1) * 2048], bT2[:, 0:2048])
                    bld.wait("vector", "qg", dv0)
                    bld.barrier()
                hc = bass.AP(bT2, 0, [[2080, 128], [257, NCH]])
                if tt == 0:
                    bld.v("tensor_copy", hc, halosel[:, :])
                else:
                    bld.v("tensor_copy", hc, halocar[:, :])
                lc = bass.AP(bT2, TT1, [[2080, 128], [257, NCH]])
                bld.v("tensor_copy", halocar[:, :], lc)
                xall = ap3(bT2, 2080, 257, TT1, off=1)
                xprev = ap3(bT2, 2080, 257, TT1, off=0)
                if dbg in ("ffnxn", "ffnxn0"):
                    ffn_dbg(bT2[:, 0:2048], tt)
                bld.v("tensor_tensor", v3(bF), xall, xprev, ALU.subtract)
                for ci in range(NCH):
                    xp_ci = bass.AP(bT2, ci * 257, [[2080, 128], [1, TT1]])
                    bld.v("scalar_tensor_tensor", xb_[:, ci * TT1:(ci + 1) * TT1],
                          bF[:, ci * TT1:(ci + 1) * TT1], mur[:, ci:ci + 1], xp_ci,
                          ALU.mult, ALU.add)
                for ci in range(NCH):
                    xp_ci = bass.AP(bT2, ci * 257, [[2080, 128], [1, TT1]])
                    vv = bld.v("scalar_tensor_tensor", sqn[:, ci * TT1:(ci + 1) * TT1],
                               bF[:, ci * TT1:(ci + 1) * TT1], muk[:, ci:ci + 1], xp_ci,
                               ALU.mult, ALU.add, inc=("mV" if ci == NCH - 1 else None))
                bld.wait("tensor", "mV", vv)
                for ci in range(NCH):
                    for e in range(NCH):
                        tv = bld.mm(psA[:, e * TT1:(e + 1) * TT1],
                                    ffw[:, 0 * NCH * D + ci * D + e * 128: 0 * NCH * D + ci * D + (e + 1) * 128],
                                    xb_[:, ci * TT1:(ci + 1) * TT1],
                                    ci == 0 and e % 2 == 0, ci == NCH - 1,
                                    inc=("mT" if (ci == NCH - 1 and e == NCH - 1) else None))
                bld.wait("scalar", "mT", tv)
                rv = bld.act(bI[:, :], psA[:, :], ACT.Sigmoid, inc="mS")
                if dbg == "ffnr":
                    bld.wait("vector", "mS", rv)
                    ffn_dbg(bI[:, :], tt)
                for ci in range(NCH):
                    for e in range(NCH):
                        tv = bld.mm(psB[:, e * TT1:(e + 1) * TT1],
                                    ffw[:, 1 * NCH * D + ci * D + e * 128: 1 * NCH * D + ci * D + (e + 1) * 128],
                                    sqn[:, ci * TT1:(ci + 1) * TT1],
                                    ci == 0 and e % 2 == 0, ci == NCH - 1,
                                    inc=("mT" if (ci == NCH - 1 and e == NCH - 1) else None))
                bld.wait("vector", "mT", tv)
                bld.v("tensor_scalar", bS[:, :], psB[:, 0:2048], cst[:, 5:6], None, ALU.max)
                if dbg == "ffnk":
                    ffn_dbg(bS[:, :], tt)
                kav = bld.v("tensor_mul", sqn[:, :], bS[:, :], bS[:, :], inc="mV")
                bld.wait("tensor", "mV", kav)
                bld.wait("tensor", "mS", rv)
                for ci in range(NCH):
                    for e in range(NCH):
                        tv = bld.mm(psA[:, e * TT1:(e + 1) * TT1],
                                    ffw[:, 2 * NCH * D + ci * D + e * 128: 2 * NCH * D + ci * D + (e + 1) * 128],
                                    sqn[:, ci * TT1:(ci + 1) * TT1],
                                    ci == 0 and e % 2 == 0, ci == NCH - 1,
                                    inc=("mT" if (ci == NCH - 1 and e == NCH - 1) else None))
                bld.wait("vector", "mT", tv)
                bld.wait("vector", "mS", rv)
                bld.v("tensor_tensor", bF[:, :], bI[:, :], psA[:, :], ALU.mult, inc="mV")
                if dbg == "ffnout":
                    ffn_dbg(bF[:, :], tt)
                fv = bld.v("tensor_tensor", xres_t(tt), xres_t(tt), v3(bF), ALU.add, inc="mV")
                if out_final:
                    bld.wait("gpsimd", "mV", fv)
                    xt = bass.AP(outd, tt * TT1, [[NCH * TL, 128], [TL, NCH], [1, TT1]])
                    bld.dma("qg", xt, xres_t(tt))
            bld.barrier()

        bld.flush()
        st1b.close()
        if upto >= 2:
            ffw = sb1("ffw", [128, 3 * NCH * D], BF16)
            for m in range(3):
                for ci in range(NCH):
                    wv = bld.dma("qg", ffw[:, m * NCH * D + ci * D:m * NCH * D + (ci + 1) * D], wfd[m, ci, :, :])
            bld.wait("tensor", "qg", wv)
            bld.wait("vector", "qg", wv)
            ffn_stage(a3row, b3row, haloc, ffw, out_final=False)

        if upto >= 3:
            bld.flush()
            st1.close()
            s3 = ExitStack()
            sb3 = lambda name, shape, dt: s3.enter_context(
                nc.sbuf_tensor("s3_" + name, shape, dt))
            kT = sb3("kT", [128, T], BF16)
            vaug = sb3("vaug", [128, 130 * 128], BF16)
            wqkv = sb3("wqkv", [128, 3 * NCH * 128], BF16)
            masks = sb3("masks", [128, 4 * 512], BF16)
            bqk = sb3("bqk", [128, 2], F32)
            x2nb = sb3("x2nb", [128, 2048], BF16)
            s3pre = ExitStack()
            xb3 = s3pre.enter_context(nc.sbuf_tensor("s3_xb3", [128, 2048], BF16))
            sq3 = s3pre.enter_context(nc.sbuf_tensor("s3_sq3", [128, 2048], BF16))
            t3 = s3pre.enter_context(nc.sbuf_tensor("s3_t3", [128, 2048], F32))

            mv_ = bld.dma("ql", t3[:, :], maskd[:, :])
            bld.dma("ql", bqk[:, :], bqkd[:, :])
            lv = bld.dma("ql", srow[0:1, 0:128], bvrowd[:, :])
            bld.wait("vector", "ql", lv)
            bld.v("tensor_copy", masks[:, :], t3[:, :])
            bvv = bld.v("tensor_copy", browbf[0:1, 1920:2048], srow[0:1, 0:128], inc="mV")
            for m in range(3):
                for ci in range(NCH):
                    wv3 = bld.dma("qg", wqkv[:, m * 1024 + ci * 128:m * 1024 + (ci + 1) * 128],
                                  wqkvd[m, ci, :, :])
            bld.v("memset", bass.AP(vaug, 64, [[130 * 128, 128], [130, 128], [65, 2]]), 1.0)

            for tt in range(NT1):
                c0 = tt * TT1
                ln_stats(lambda ci: xres[:, ci * TL + c0: ci * TL + c0 + TT1], xb3, sq3)
                bld.v("tensor_tensor", v3(t3), xres_t(tt), bc3(Abc_off), ALU.mult)
                xnv3 = bld.v("tensor_tensor", v3(x2nb), v3(t3), bc3(Bbc_off), ALU.add,
                             inc="mV")
                bld.wait("gpsimd", "mV", xnv3)
                agp = bass.AP(ag4i, tt * TT1, [[NCH * TL, 128], [TL, NCH], [1, TT1]])
                agv = bld.dma("qg", agp, v3(x2nb))
                bld.wait("vector", "qg", agv)
            bld.wait("gpsimd", "qg")
            bld.g("collective_compute", "AllGather", ALU.bypass,
                  replica_groups=rg, ins=[ag4i.ap().opt()], outs=[ag4o.ap().opt()],
                  inc="cc")
            bld.wait("sync", "cc")
            bld.wait("tensor", "qg", wv3)
            bld.wait("tensor", "mV", bvv)
            bld.barrier()
            bld.flush()
            s3pre.close()
            xg = sb3("xg", [128, 2 * NCH * 512], BF16)
            pTA = sb3("pTA", [128, 2 * 2048], BF16)
            pTB = sb3("pTB", [128, 2 * 1024], BF16)
            qt2 = sb3("qt2", [128, 2 * 512], BF16)
            osb = sb3("osb", [64, 1024], F32)
            oTb = sb3("oTb", [64, 2 * 1024], BF16)

            xg_v = {}

            def xg_load(gt):
                r_, off = gt // 4, (gt % 4) * 512
                slot = gt % 2
                inap = bass.AP(ag4o, (r_ * 128) * (NCH * TL) + off,
                               [[NCH * TL, 128], [TL, NCH], [1, 512]])
                bld.wait("sync", "mT")
                xg_v[gt] = bld.dma("ql", ap3(xg, 2 * NCH * 512, 512, 512,
                                             off=slot * NCH * 512), inap)

            xg_load(0)
            xg_load(1)
            ev_q = ev_k = ev_v = 0
            for gt in range(32):
                slot = gt % 2
                xgs = lambda ci: xg[:, slot * NCH * 512 + ci * 512:
                                    slot * NCH * 512 + (ci + 1) * 512]
                bld.wait("tensor", "ql", xg_v[gt])
                if gt >= 1:
                    bld.wait("tensor", "mS", ev_k)
                    bld.wait("tensor", "mV", ev_v)
                for ci in range(NCH):
                    bld.mm(psA[:, 0:512], wqkv[:, 0 * 1024 + ci * 128:0 * 1024 + (ci + 1) * 128],
                           xgs(ci), ci == 0, ci == NCH - 1)
                for ci in range(NCH):
                    qkv_t = bld.mm(psA[:, 512:1024],
                                   wqkv[:, 1 * 1024 + ci * 128:1 * 1024 + (ci + 1) * 128],
                                   xgs(ci), ci == 0, ci == NCH - 1,
                                   inc=("mT" if ci == NCH - 1 else None))
                for sub in range(4):
                    for ci in range(NCH):
                        bld.mm(psB[:, sub * 128:(sub + 1) * 128],
                               bass.AP(xg, slot * NCH * 512 + ci * 512 + sub * 128,
                                       [[2 * NCH * 512, 128], [1, 128]]),
                               wqkv[:, 2 * 1024 + ci * 128:2 * 1024 + (ci + 1) * 128],
                               sub == 0 and ci == 0, False)
                    vt = bld.mm(psB[:, sub * 128:(sub + 1) * 128],
                                onesrow[0:1, 0:128], browbf[0:1, 1920:2048],
                                False, True, inc=("mT" if sub == 3 else None))
                bld.wait("scalar", "mT", qkv_t)
                qslot = gt % 2
                if gt >= 2:
                    bld.wait("scalar", "qg", qst_v[gt - 2])
                bld.act(x2nb[:, qslot * 512:(qslot + 1) * 512], psA[:, 0:512],
                        ACT.Identity, bias=bqk[:, 0:1])
                ev_k = bld.act(kT[:, gt * 512:(gt + 1) * 512], psA[:, 512:1024],
                               ACT.Identity, bias=bqk[:, 1:2], inc="mS")
                bld.wait("gpsimd", "mS", ev_k)
                qst_v = qst_v if gt else {}
                qst_v[gt] = bld.dma("qg", qscr[:, gt * 512:(gt + 1) * 512],
                                    x2nb[:, qslot * 512:(qslot + 1) * 512])
                bld.wait("vector", "mT", vt)
                for sub in range(4):
                    tb = (gt * 4 + sub) * 130
                    ev_v = bld.v("tensor_copy",
                                 bass.AP(vaug, tb, [[130 * 128, 128], [65, 2], [1, 64]]),
                                 bass.AP(psB, sub * 128, [[2048, 128], [64, 2], [1, 64]]),
                                 inc=("mV" if sub == 3 else None))
                if gt < 30:
                    xg_load(gt + 2)

            if dbg == "att_xg":
                bld.barrier()
                dvx = bld.dma("qg", zscr[:, :],
                              bass.AP(ag4o, 0, [[NCH * TL, 128], [1, NCH * TL]]))
                bld.wait("vector", "qg", dvx)
                bld.barrier()
            if dbg in ("att_k", "att_q", "att_v"):
                bld.barrier()
                src = {"att_k": kT[:, :], "att_q": None, "att_v": None}[dbg] \
                    if dbg == "att_k" else None
                if dbg == "att_k":
                    dvk = bld.dma("qg", zscr[:, :], kT[:, :])
                elif dbg == "att_q":
                    dvk = bld.dma("qg", zscr[:, :], bass.AP(qscr, 0, [[T, 128], [1, T]]))
                else:
                    dvk = bld.dma("qg", zscr[:, :], vaug[:, 0:T])
                bld.wait("vector", "qg", dvk)
                bld.barrier()

            qld = {}

            def q_load(qt):
                qld[qt] = bld.dma("ql", qt2[:, (qt % 2) * 512:(qt % 2 + 1) * 512],
                                  qscr[:, qt * 512:(qt + 1) * 512])

            osb_v = bld.cnt["mV"]
            bld.wait("sync", "qg", qst_v[0])
            q_load(0)
            bld.wait("sync", "qg", qst_v[1])
            q_load(1)
            exp_v = {}
            qk_done = {}
            ost_v = {}
            exp_last = {0: 0, 1: 0, 2: 0}
            pt_guard = {0: 0, 1: 0}
            REG = {0: (psA, 0), 1: (psA, 1024), 2: (psB, 1024)}
            for qt in range(32):
                nkt = 4 * qt + 4
                qs = qt % 2
                bld.wait("tensor", "ql", qld[qt])
                pend_pv = []
                pv_done = {}
                ngrp = (nkt + 2) // 3
                for m in range(ngrp + 1):
                    if m < ngrp:
                        grp = [kt for kt in range(3 * m, min(3 * m + 3, nkt))]
                        par = m % 2
                        ev = {}
                        for kt in grp:
                            r = kt % 3
                            ps_, off_ = REG[r]
                            bld.wait("tensor", "mS", exp_last[r])
                            if kt < 3 and qt > 0 and r != 2:
                                bld.wait("tensor", "mV", osb_v)
                            bld.mm(ps_[:, off_:off_ + 512],
                                   kT[0:64, kt * 128:(kt + 1) * 128],
                                   qt2[0:64, qs * 512:(qs + 1) * 512], True, True)
                            qkv_ = bld.mm(ps_[:, off_ + 512:off_ + 1024],
                                          kT[64:128, kt * 128:(kt + 1) * 128],
                                          qt2[64:128, qs * 512:(qs + 1) * 512], True, True,
                                          inc="mT")
                            if kt % 3 == 1 or kt == nkt - 1 or kt % 3 == 2:
                                pass
                            ev[kt] = qkv_
                        amem = [kt for kt in grp if kt % 3 != 2]
                        bmem = [kt for kt in grp if kt % 3 == 2]
                        if amem:
                            bld.wait("scalar", "mT", max(ev[amem[-1]], pt_guard[par]))
                            fd = 1024 * len(amem)
                            e_ = bld.act(pTA[:, par * 2048:par * 2048 + fd],
                                         psA[:, 0:fd], ACT.Exp,
                                         bias=cst[:, 3:4], scale=0.125, inc="mS")
                            for kt in amem:
                                exp_v[(qt, kt)] = e_
                                exp_last[kt % 3] = e_
                        if bmem:
                            kt = bmem[0]
                            bld.wait("scalar", "mT", max(ev[kt], pt_guard[par]))
                            e_ = bld.act(pTB[:, par * 1024:(par + 1) * 1024],
                                         psB[:, 1024:2048], ACT.Exp,
                                         bias=cst[:, 3:4], scale=0.125, inc="mS")
                            exp_v[(qt, kt)] = e_
                            exp_last[2] = e_
                        mk = {}
                        for kt in grp:
                            if kt >= 4 * qt:
                                a_ = kt - 4 * qt
                                bld.wait("vector", "mS", exp_v[(qt, kt)])
                                if kt % 3 == 2:
                                    pslice = ap3(pTB, 2 * 1024, 512, 512,
                                                 off=par * 1024, nc_=2)
                                else:
                                    pslice = ap3(pTA, 2 * 2048, 512, 512,
                                                 off=par * 2048 + (kt % 3) * 1024, nc_=2)
                                mk[kt] = bld.v("tensor_tensor", pslice, pslice,
                                               bass.AP(masks, a_ * 512,
                                                       [[4 * 512, 128], [0, 2], [1, 512]]),
                                               ALU.mult, inc="mV")
                        for kt in grp:
                            if kt % 3 == 2:
                                p0 = pTB[:, par * 1024:par * 1024 + 512]
                                p1 = pTB[:, par * 1024 + 512:par * 1024 + 1024]
                            else:
                                b0 = par * 2048 + (kt % 3) * 1024
                                p0 = pTA[:, b0:b0 + 512]
                                p1 = pTA[:, b0 + 512:b0 + 1024]
                            pend_pv.append((m, kt, p0, p1, exp_v[(qt, kt)], mk.get(kt)))
                    flushn = len(pend_pv) if m == ngrp else max(0, len(pend_pv) - 3)
                    for _ in range(flushn):
                        gm, kt, p0, p1, e_, mkv = pend_pv.pop(0)
                        bld.wait("tensor", "mS", e_)
                        if mkv:
                            bld.wait("tensor", "mV", mkv)
                        if kt == 0:
                            bld.wait("tensor", "mV", osb_v)
                        bld.mm(psB[0:65, 0:512], vaug[:, kt * 130:kt * 130 + 65],
                               p0, kt == 0, kt == nkt - 1)
                        pv_t = bld.mm(psB[0:65, 512:1024],
                                      vaug[:, kt * 130 + 65:kt * 130 + 130],
                                      p1, kt == 0, kt == nkt - 1, inc="mT")
                        pt_guard[(kt // 3) % 2] = pv_t
                qk_done[qt] = bld.cnt["mT"]
                if qt < 31:
                    if qt >= 1:
                        bld.wait("sync", "mT", qk_done[qt])
                    bld.wait("sync", "qg", qst_v[qt + 1])
                    q_load(qt + 1)
                bld.wait("vector", "mT", pv_t)
                bld.v("reciprocal", srow[0:1, 0:512], psB[64:65, 0:512])
                bld.v("reciprocal", srow[0:1, 512:1024], psB[64:65, 512:1024])
                bld.v("tensor_copy", browbf[0:1, 0:1024], srow[0:1, 0:1024])
                bld.v("tensor_copy", srow[0:1, 1024:2048], browbf[0:1, 0:1024])
                bld.v("tensor_sub", srow[0:1, 1024:2048], srow[0:1, 0:1024],
                      srow[0:1, 1024:2048])
                rc_v = bld.v("tensor_copy", browbf[0:1, 1024:2048], srow[0:1, 1024:2048],
                             inc="mV")
                bld.wait("tensor", "mV", rc_v)
                bld.wait("tensor", "mS")
                bld.mm(psA[0:64, 0:512], onesrow[0:1, 0:64], browbf[0:1, 0:512],
                       True, False)
                bld.mm(psA[0:64, 512:1024], onesrow[0:1, 0:64], browbf[0:1, 512:1024],
                       True, False)
                bld.mm(psA[0:64, 0:512], onesrow[0:1, 0:64], browbf[0:1, 1024:1536],
                       False, True)
                bc_t = bld.mm(psA[0:64, 512:1024], onesrow[0:1, 0:64],
                              browbf[0:1, 1536:2048], False, True, inc="mT")
                bld.wait("vector", "mT", bc_t)
                if qt >= 2:
                    bld.wait("vector", "qg", ost_v[qt - 2])
                bld.v("tensor_copy", osb[0:64, :], psB[0:64, 0:1024])
                osb_v = bld.v("tensor_mul", oTb[0:64, qs * 1024:(qs + 1) * 1024],
                              osb[0:64, :], psA[0:64, 0:1024], inc="mV")
                bld.wait("gpsimd", "mV", osb_v)
                blk, off = qt // 4, (qt % 4) * 512
                bld.dma("qg", a2ai[blk, 0:64, off:off + 512],
                        oTb[0:64, qs * 1024:qs * 1024 + 512])
                ost_v[qt] = bld.dma("qg", a2ai[blk, 64:128, off:off + 512],
                                    oTb[0:64, qs * 1024 + 512:qs * 1024 + 1024])

            bld.wait("gpsimd", "qg")
            if dbg == "att_o":
                bld.barrier()
                dvo = bld.dma("qg", zscr[:, :],
                              bass.AP(a2ai, 0, [[T, 128], [1, T]]))
                bld.wait("vector", "qg", dvo)
                bld.barrier()

            def _a2a(e, i_=a2ai, o_=a2ao, rg_=rg):
                with allow_ici_all_to_all():
                    return e.collective_compute(
                        "AllToAll", ALU.bypass, replica_groups=rg_,
                        ins=[i_.ap().opt()], outs=[o_.ap().opt()])

            bld.pend["gpsimd"].append(_a2a)
            bld._wrap_inc("gpsimd", "cc")
            bld.wait("sync", "cc")
            bld.barrier()
            bld.flush()
            s3.close()

            s3b = ExitStack()
            woT = s3b.enter_context(nc.sbuf_tensor("s3_woT", [128, NCH * D], BF16))
            obin = s3b.enter_context(nc.sbuf_tensor("s3_obin", [128, NCH * 512], BF16))
            obuf = s3b.enter_context(nc.sbuf_tensor("s3_obuf", [128, 2048], F32))
            bo3b = s3b.enter_context(nc.sbuf_tensor("s3_bo3b", [128, NCH], F32))
            bld.dma("ql", bo3b[:, :], bod[:, :])
            for ci in range(NCH):
                wo_v = bld.dma("qg", woT[:, ci * D:(ci + 1) * D], wod[ci, :, :])
            bld.wait("tensor", "qg", wo_v)
            ob_v = 0
            for tt4 in range(4):
                inap = bass.AP(a2ao, tt4 * 512, [[TL, 128], [128 * TL, NCH], [1, 512]])
                bld.wait("sync", "mT")
                ov = bld.dma("ql", ap3(obin, NCH * 512, 512, 512), inap)
                bld.wait("tensor", "ql", ov)
                if tt4 >= 1:
                    bld.wait("tensor", "mV", ob_v)
                for do in range(NCH):
                    ps = psA if do < 4 else psB
                    col = (do % 4) * 512
                    for oc in range(NCH):
                        op_t = bld.mm(ps[:, col:col + 512],
                                      woT[:, oc * D + do * 128:oc * D + (do + 1) * 128],
                                      obin[:, oc * 512:(oc + 1) * 512], oc == 0,
                                      oc == NCH - 1,
                                      inc=("mT" if oc == NCH - 1 else None))
                bld.wait("scalar", "mT", op_t)
                for half in range(2):
                    bld.wait("scalar", "mV", ob_v)
                    for d4 in range(4):
                        do = half * 4 + d4
                        ps = psA if half == 0 else psB
                        oev = bld.act(obuf[:, d4 * 512:(d4 + 1) * 512],
                                      ps[:, d4 * 512:(d4 + 1) * 512], ACT.Identity,
                                      bias=bo3b[:, do:do + 1],
                                      inc=("mS" if d4 == 3 else None))
                    bld.wait("vector", "mS", oev)
                    xh = bass.AP(xres, half * 4 * TL + tt4 * 512,
                                 [[NCH * TL, 128], [TL, 4], [1, 512]])
                    ob_v = bld.v("tensor_tensor", xh, xh,
                                 ap3(obuf, 2048, 512, 512, nc_=4), ALU.add, inc="mV")
            bld.barrier()
            bld.flush()
            s3b.close()

        if upto >= 4:
            s4 = ExitStack()
            sb4 = lambda name, shape, dt: s4.enter_context(
                nc.sbuf_tensor("s4_" + name, shape, dt))
            bF = sb4("bF", [128, 2048], F32)
            bI = sb4("bI", [128, 2048], F32)
            bS = sb4("bS", [128, 2048], F32)
            bT2 = sb4("bT2", [128, 2080], F32)
            xb_ = sb4("xb", [128, 2048], BF16)
            sqn = sb4("sqn", [128, 2048], BF16)
            gbf = sb4("gbf", [128, 2048], BF16)
            ffw = sb4("ffw", [128, 3 * NCH * D], BF16)
            a3row = sb4("a3row", [1, TL], F32)
            b3row = sb4("b3row", [1, TL], F32)
            for m in range(3):
                for ci in range(NCH):
                    wv4 = bld.dma("qg", ffw[:, m * NCH * D + ci * D:m * NCH * D + (ci + 1) * D],
                                  wfd[m, ci, :, :])
            bld.wait("tensor", "qg", wv4)
            bld.wait("vector", "qg", wv4)
            for tt in range(NT1):
                c0 = tt * TT1
                if tt > 0:
                    bld.wait("vector", "mT")
                ln_stats(lambda ci: xres[:, ci * TL + c0: ci * TL + c0 + TT1], xb_, sqn)
                bld.v("tensor_copy", a3row[0:1, c0:c0 + TT1], srow[0:1, 0:TT1])
                bld.v("tensor_copy", b3row[0:1, c0:c0 + TT1], srow[0:1, 256:256 + TT1])
            halo_ag(a3row, b3row, ag3i, ag3o, haloc)
            ffn_stage(a3row, b3row, haloc, ffw, out_final=True)
            bld.wait("gpsimd", "qg")
            bld.flush()
            s4.close()

        if upto < 4:
            bld.barrier()
            if dbg is not None:
                bld.dma("qg", outd[:, :], zscr[:, :])
            else:
                for tt in range(NT1):
                    xt = bass.AP(outd, tt * TT1, [[NCH * TL, 128], [TL, NCH], [1, TT1]])
                    bld.dma("qg", xt, xres_t(tt))
            bld.wait("gpsimd", "qg")
            bld.flush()
            if upto < 3:
                st1.close()
        perm.close()
    return nc


def prep_inputs(core, inputs):
    c = core
    x = np.asarray(inputs["L0_attn_a_x"]).reshape(T, D)
    d = {}
    d["x0"] = fm_pack(np.asarray(x[c * TL:(c + 1) * TL], np.float32))
    w1 = np.stack([np.asarray(inputs["L0_attn_a_Wi"]), np.asarray(inputs["L0_attn_a_Wf"]),
                   np.asarray(inputs["L0_attn_a_Wo"]), np.asarray(inputs["L0_attn_a_Ws"])])
    d["w1"] = np.ascontiguousarray(w1.reshape(4, NCH, 128, D), np.float32)
    d["b1"] = np.ascontiguousarray(np.concatenate(
        [np.asarray(inputs["L0_attn_a_bi"]), np.asarray(inputs["L0_attn_a_bf"]),
         np.asarray(inputs["L0_attn_a_bo"]), np.asarray(inputs["L0_attn_a_bs"])])[None, :],
        np.float32)
    pp = lambda v: np.ascontiguousarray(np.asarray(v).reshape(NCH, 128).T, np.float32)
    d["gam"] = pp(inputs["L0_attn_a_gamma"])
    d["bet"] = pp(inputs["L0_attn_a_beta"])
    d["mur"] = pp(inputs["L0_ffn_mu_r"])
    d["muk"] = pp(inputs["L0_ffn_mu_k"])
    wf = np.stack([np.asarray(inputs["L0_ffn_W_r"]).T,
                   np.asarray(inputs["L0_ffn_W_k"]).T,
                   np.asarray(inputs["L0_ffn_W_v"]).T])
    d["wf"] = np.ascontiguousarray(wf.reshape(3, NCH, 128, D), np.float32)
    r0, r1 = 128 * c, 128 * (c + 1)
    wq = np.asarray(inputs["L0_attn_b_Wq"]).T[:, r0:r1]
    wk = np.asarray(inputs["L0_attn_b_Wk"]).T[:, r0:r1]
    wv = np.asarray(inputs["L0_attn_b_Wv"]).T[:, r0:r1]
    d["wqkv"] = np.ascontiguousarray(np.stack([wq, wk, wv]).reshape(3, NCH, 128, 128),
                                     np.float32)
    d["bqk"] = np.ascontiguousarray(
        np.stack([np.asarray(inputs["L0_attn_b_bq"])[r0:r1],
                  np.asarray(inputs["L0_attn_b_bk"])[r0:r1]], axis=1), np.float32)
    d["bvrow"] = np.ascontiguousarray(np.asarray(inputs["L0_attn_b_bv"])[None, r0:r1],
                                      np.float32)
    d["wo"] = np.ascontiguousarray(
        np.asarray(inputs["L0_attn_b_Wo"]).T.reshape(NCH, 128, D), np.float32)
    d["bo"] = pp(inputs["L0_attn_b_bo"])
    mk = np.zeros((128, 4, 512), np.float32)
    q = np.arange(512)
    for a in range(4):
        for k in range(128):
            mk[k, a] = (q >= 128 * a + k)
    d["mask"] = np.ascontiguousarray(mk.reshape(128, 4 * 512))
    sel = np.zeros((128, 8), np.float32)
    if c > 0:
        sel[:, c - 1] = 1.0
    d["sel"] = sel
    return d


def kernel(**inputs):
    nc = build(_UPTO, _DBG)
    in_maps = [prep_inputs(c, inputs) for c in range(NCORE)]
    res = run_bass_kernel_spmd(nc, in_maps, core_ids=list(range(NCORE)))
    if _DBG is not None:
        return [res.results[c]["out"] for c in range(NCORE)]
    shards = [fm_unpack(res.results[c]["out"], TL) for c in range(NCORE)]
    return np.concatenate(shards, axis=0).reshape(1, T, D).astype(np.float32)


# version 06
# speedup vs baseline: 1.8221x; 1.1258x over previous
import numpy as np
import ml_dtypes
import concourse.bass as bass
import concourse.mybir as mybir
from concourse.bass_utils import run_bass_kernel_spmd
from concourse.aot_env import allow_ici_all_to_all
from contextlib import ExitStack

F32 = mybir.dt.float32
BF16 = mybir.dt.bfloat16
ALU = mybir.AluOpType
ACT = mybir.ActivationFunctionType
AXX = mybir.AxisListType.X

NCORE = 8
D = 1024
T = 16384
TL = T // NCORE
NCH = 8
TT1 = 256
NT1 = TL // TT1
LN_EPS = 1e-5

_UPTO = 4
_DBG = None


class Bld:
    ENGS = ("sync", "gpsimd", "tensor", "scalar", "vector")

    def __init__(self, nc, block, sems):
        self.nc, self.block = nc, block
        self.sems = sems
        self.cnt = {k: 0 for k in sems}
        self.pend = {e: [] for e in self.ENGS}

    def flush(self):
        for e in self.ENGS:
            lst = self.pend[e]
            if not lst:
                continue
            self.pend[e] = []
            getattr(self.block, e)(lambda eng, lst=lst: [f(eng) for f in lst])

    def _p(self, eng, fn):
        self.pend[eng].append(fn)

    def _wrap_inc(self, eng, sem, n=1):
        self.cnt[sem] += n
        h = self.sems[sem]
        f = self.pend[eng][-1]
        self.pend[eng][-1] = lambda e, f=f, h=h, n=n: f(e).then_inc(h, n)
        return self.cnt[sem]

    def wait(self, eng, sem, val=None):
        if val is None:
            val = self.cnt[sem]
        if val <= 0:
            return
        assert val <= self.cnt[sem], (sem, val, self.cnt[sem])
        h = self.sems[sem]
        self._p(eng, lambda e, h=h, v=val: e.wait_ge(h, v))

    def dma(self, q, out, in_):
        eng = "sync" if q == "ql" else "gpsimd"
        self._p(eng, lambda e, o=out, i=in_: e.dma_start(out=o, in_=i))
        return self._wrap_inc(eng, q, 16)

    def mm(self, out, lhsT, rhs, start, stop, inc=None):
        self._p("tensor", lambda e, o=out, l=lhsT, r=rhs, s0=start, s1=stop:
                e.matmul(o, l, r, start=s0, stop=s1))
        if inc:
            return self._wrap_inc("tensor", inc)

    def act(self, out, in_, func, bias=0.0, scale=1.0, inc=None):
        self._p("scalar", lambda e, o=out, i=in_, f=func, b=bias, s=scale:
                e.activation(out=o, in_=i, func=f, bias=b, scale=s))
        if inc:
            return self._wrap_inc("scalar", inc)

    def v(self, meth, *args, inc=None, **kw):
        self._p("vector", lambda e, m=meth, a=args, k=kw: getattr(e, m)(*a, **k))
        try:
            fs = args[0].free_size() if hasattr(args[0], "free_size") else 999
        except Exception:
            fs = 999
        r = None
        if inc:
            r = self._wrap_inc("vector", inc)
        if fs <= 64 and meth != "memset":
            for _ in range(4):
                self._p("vector", lambda e: e.engine_nop())
        return r

    def g(self, meth, *args, inc=None, **kw):
        self._p("gpsimd", lambda e, m=meth, a=args, k=kw: getattr(e, m)(*a, **k))
        if inc:
            return self._wrap_inc("gpsimd", inc)

    def barrier(self):
        engs = self.ENGS
        vals = {}
        for e in engs:
            sem = "b_" + e
            h = self.sems[sem]
            self.cnt[sem] += 1
            vals[e] = self.cnt[sem]
            self._p(e, lambda eng, h=h: eng.nop().then_inc(h, 1))
        for e in engs:
            for o in engs:
                if o != e:
                    self.wait(e, "b_" + o, vals[o])


def fm_pack(a):
    tn = a.shape[0]
    return np.ascontiguousarray(
        a.T.reshape(NCH, 128, tn).transpose(1, 0, 2).reshape(128, NCH * tn))


def fm_unpack(a, tn):
    return np.ascontiguousarray(
        a.reshape(128, NCH, tn).transpose(1, 0, 2).reshape(D, tn).T)


def build(upto=4, dbg=None):
    nc = bass.Bass("TRN2", target_bir_lowering=False, debug=False)
    DT = nc.dram_tensor

    x0d = DT("x0", [128, NCH * TL], F32, kind="ExternalInput")
    w1d = DT("w1", [4, NCH, 128, D], F32, kind="ExternalInput")
    b1d = DT("b1", [1, 4 * D], F32, kind="ExternalInput")
    gamd = DT("gam", [128, NCH], F32, kind="ExternalInput")
    betd = DT("bet", [128, NCH], F32, kind="ExternalInput")
    wfd = DT("wf", [3, NCH, 128, D], F32, kind="ExternalInput")
    murd = DT("mur", [128, NCH], F32, kind="ExternalInput")
    mukd = DT("muk", [128, NCH], F32, kind="ExternalInput")
    wqkvd = DT("wqkv", [3, NCH, 128, 128], F32, kind="ExternalInput")
    bqkd = DT("bqk", [128, 2], F32, kind="ExternalInput")
    bvrowd = DT("bvrow", [1, 128], F32, kind="ExternalInput")
    wod = DT("wo", [NCH, 128, D], F32, kind="ExternalInput")
    bod = DT("bo", [128, NCH], F32, kind="ExternalInput")
    maskd = DT("mask", [128, 4 * 512], F32, kind="ExternalInput")
    seld = DT("sel", [128, 8], F32, kind="ExternalInput")
    outd = DT("out", [128, NCH * TL], F32, kind="ExternalOutput")

    zscr = DT("zscr", [128, NCH * TL], F32)
    w1b = DT("w1b", [4, NCH, 128, D], BF16)
    wfb = DT("wfb", [3, NCH, 128, D], BF16)
    gscr = DT("gscr", [128, NCH * TL], BF16)
    ag1i = DT("ag1i", [128, 16], F32)
    ag1o = DT("ag1o", [NCORE * 128, 16], F32)
    ag2i = DT("ag2i", [128, 8], F32)
    ag2o = DT("ag2o", [NCORE * 128, 8], F32)
    ag3i = DT("ag3i", [128, 8], F32)
    ag3o = DT("ag3o", [NCORE * 128, 8], F32)
    ag4i = DT("ag4i", [128, NCH * TL], BF16)
    ag4o = DT("ag4o", [NCORE * 128, NCH * TL], BF16)
    qscr = DT("qscr", [128, T], BF16)
    a2ai = DT("a2ai", [NCORE, 128, TL], BF16)
    a2ao = DT("a2ao", [NCORE, 128, TL], BF16)
    rg = [list(range(NCORE))]

    ctx0 = ExitStack()
    sems = {}
    for s in ("ql", "qg", "cc", "mT", "mS", "mV",
              "b_sync", "b_gpsimd", "b_tensor", "b_scalar", "b_vector"):
        sems[s] = ctx0.enter_context(nc.semaphore(s))

    with ctx0, nc.Block() as block:
        bld = Bld(nc, block, sems)
        perm = ExitStack()
        sb = lambda name, shape, dt: perm.enter_context(nc.sbuf_tensor("s_" + name, shape, dt))
        xres = sb("xres", [128, NCH * TL], F32)
        onescol = sb("onescol", [128, 1], BF16)
        onesrow = sb("onesrow", [1, 512], BF16)
        ones2k = sb("ones2k", [128, TT1], F32)
        cst = sb("cst", [128, 8], F32)
        b1row = sb("b1row", [1, 4 * D], BF16)
        gam = sb("gam", [128, NCH], F32)
        bet = sb("bet", [128, NCH], F32)
        mur = sb("mur", [128, NCH], F32)
        muk = sb("muk", [128, NCH], F32)
        selm = sb("selm", [128, 8], F32)
        hcar = sb("hcar", [128, 8], F32)
        fcar = sb("fcar", [128, 8], F32)
        h_in = sb("h_in", [128, 8], F32)
        halocar = sb("halocar", [128, 8], F32)
        haloc = sb("haloc", [128, 8], F32)
        carrs = sb("carrs", [128, 8, 16], F32)
        ctmp = sb("ctmp", [128, 24], F32)
        srow = sb("srow", [1, 3072], F32)
        browbf = sb("browbf", [1, 2048], BF16)
        psA = perm.enter_context(nc.psum_tensor("psA", [128, 2048], F32))
        psB = perm.enter_context(nc.psum_tensor("psB", [128, 2048], F32))

        bld.dma("ql", xres[:, :], x0d[:, :])
        bld.dma("ql", gam[:, :], gamd[:, :])
        bld.dma("ql", bet[:, :], betd[:, :])
        bld.dma("ql", mur[:, :], murd[:, :])
        bld.dma("ql", muk[:, :], mukd[:, :])
        bld.dma("ql", selm[:, :], seld[:, :])
        b1v = bld.dma("qg", b1row[:, :], b1d[:, :])
        ld0 = bld.cnt["ql"]
        bld.v("memset", onescol[:, :], 1.0)
        bld.v("memset", onesrow[:, :], 1.0)
        bld.v("memset", ones2k[:, :], 1.0)
        bld.v("memset", hcar[:, :], 0.0)
        bld.v("memset", fcar[:, :], 1.0)
        bld.v("memset", cst[:, 0:1], LN_EPS)
        bld.v("memset", cst[:, 1:2], 1.0 / D)
        bld.v("memset", cst[:, 2:3], -1.0)
        bld.v("memset", cst[:, 3:4], -4.0)
        bld.v("memset", cst[:, 4:5], 0.125)
        bld.v("memset", cst[:, 5:6], 0.0)
        bld.v("memset", cst[:, 6:7], -0.5)
        bld.wait("vector", "ql", ld0)
        bld.wait("tensor", "qg", b1v)

        def ap3(buf, pstep, cstep, n, off=0, nc_=NCH):
            return bass.AP(buf, off, [[pstep, 128], [cstep, nc_], [1, n]])

        def xres_t(tt, n=TT1):
            return ap3(xres, NCH * TL, TL, n, off=tt * n)

        def bc3(col_off, n=TT1):
            return bass.AP(psB, col_off, [[2048, 128], [0, NCH], [1, n]])

        def v3(buf, n=TT1, pstep=2048):
            return ap3(buf, pstep, n, n)

        Abc_off, Bbc_off = 1024, 1280

        def ln_stats(src_of_ci, xb, sq, n=TT1):
            for ci in range(NCH):
                bld.v("tensor_copy", xb[:, ci * n:(ci + 1) * n], src_of_ci(ci))
            for ci in range(NCH):
                vv = bld.v("tensor_mul", sq[:, ci * n:(ci + 1) * n],
                           xb[:, ci * n:(ci + 1) * n], xb[:, ci * n:(ci + 1) * n],
                           inc=("mV" if ci == NCH - 1 else None))
            bld.wait("tensor", "mV", vv)
            for ci in range(NCH):
                bld.mm(psB[0:1, 1536:1536 + n], onescol[:, :], xb[:, ci * n:(ci + 1) * n],
                       ci == 0, ci == NCH - 1)
            for ci in range(NCH):
                tv = bld.mm(psB[0:1, 1792:1792 + n], onescol[:, :], sq[:, ci * n:(ci + 1) * n],
                            ci == 0, ci == NCH - 1,
                            inc=("mT" if ci == NCH - 1 else None))
            bld.wait("vector", "mT", tv)
            m_ = srow[0:1, 2048:2048 + n]
            msq = srow[0:1, 2304:2304 + n]
            bld.v("tensor_scalar", m_, psB[0:1, 1536:1536 + n], cst[0:1, 1:2], None, ALU.mult)
            bld.v("tensor_scalar", msq, psB[0:1, 1792:1792 + n], cst[0:1, 1:2], None, ALU.mult)
            var = srow[0:1, 2560:2560 + n]
            bld.v("tensor_mul", var, m_, m_)
            bld.v("tensor_sub", var, msq, var)
            bld.v("tensor_scalar", var, var, cst[0:1, 0:1], None, ALU.add)
            inv = srow[0:1, 2816:2816 + n]
            sv = bld.v("reciprocal", inv, var, inc="mV")
            bld.wait("scalar", "mV", sv)
            arow = srow[0:1, 0:n]
            av = bld.act(arow, inv, ACT.Sqrt, inc="mS")
            bld.wait("vector", "mS", av)
            t1 = srow[0:1, 2304:2304 + n]
            bld.v("tensor_mul", t1, arow, arow)
            bld.v("tensor_mul", t1, t1, var)
            bld.v("tensor_scalar", t1, t1, cst[0:1, 6:7], 1.5, ALU.mult, ALU.add)
            bld.v("tensor_mul", arow, arow, t1)
            brow = srow[0:1, 256:256 + n]
            bld.v("tensor_mul", brow, m_, arow)
            bld.v("tensor_scalar", brow, brow, cst[0:1, 2:3], None, ALU.mult)
            bcast_rows(arow, brow, n)

        def bcast_rows(arow, brow, n=TT1):
            bld.v("tensor_copy", browbf[0:1, 0:n], arow)
            ahif = srow[0:1, 2560:2560 + n]
            bld.v("tensor_copy", ahif, browbf[0:1, 0:n])
            alo = srow[0:1, 2816:2816 + n]
            bld.v("tensor_sub", alo, arow, ahif)
            bld.v("tensor_copy", browbf[0:1, 256:256 + n], alo)
            vv = bld.v("tensor_copy", browbf[0:1, 512:512 + n], brow, inc="mV")
            bld.wait("tensor", "mV", vv)
            bld.mm(psB[:, Abc_off:Abc_off + n], onesrow[0:1, 0:128], browbf[0:1, 0:n],
                   True, False)
            bld.mm(psB[:, Abc_off:Abc_off + n], onesrow[0:1, 0:128], browbf[0:1, 256:256 + n],
                   False, True)
            tv = bld.mm(psB[:, Bbc_off:Bbc_off + n], onesrow[0:1, 0:128],
                        browbf[0:1, 512:512 + n], True, True, inc="mT")
            bld.wait("vector", "mT", tv)

        st1 = ExitStack()
        sb1 = lambda name, shape, dt: st1.enter_context(nc.sbuf_tensor("s_" + name, shape, dt))
        bF = sb1("bF", [128, 2048], F32)
        bI = sb1("bI", [128, 2048], F32)
        bS = sb1("bS", [128, 2048], F32)
        bT2 = sb1("bT2", [128, 2080], F32)
        bT2b = sb1("bT2b", [128, 2080], F32)
        xb_ = sb1("xb", [128, 2048], BF16)
        xb2_ = sb1("xb2", [128, 2048], BF16)
        sqn = sb1("sqn", [128, 2048], BF16)
        gbf = sb1("gbf", [128, 2048], BF16)
        a3hl = sb1("a3hl", [1, 2 * TL], BF16)
        b3b = sb1("b3b", [1, TL], BF16)
        st1b = ExitStack()
        wbuf = st1b.enter_context(nc.sbuf_tensor("s_wbuf", [128, 4, 2 * D], BF16))
        bO = st1b.enter_context(nc.sbuf_tensor("s_bO", [128, 2048], F32))
        xbpre = st1b.enter_context(nc.sbuf_tensor("s_xbpre", [128, 2048], BF16))

        gate_dst = [psA, psB, psA, psB]
        gate_fun = [ACT.Sigmoid, ACT.Sigmoid, ACT.Sigmoid, ACT.Tanh]
        gate_buf = [bI, bF, bO, bS]

        worder = [(tt, g, cp) for tt in range(NT1) for g in range(4) for cp in range(4)]
        wld_v = {}
        wuse_v = {}
        wload_ptr = [0]

        def w1_issue_next():
            j = wload_ptr[0]
            if j >= len(worder):
                return
            wload_ptr[0] += 1
            tt, g, cp = worder[j]
            slot = j % 4
            if j >= 4:
                bld.wait("sync", "mT", wuse_v[j - 4])
            inap = bass.AP(w1b, (g * 8 + 2 * cp) * 128 * D,
                           [[D, 128], [128 * D, 2], [1, D]])
            outap = bass.AP(wbuf, slot * 2 * D, [[4 * 2 * D, 128], [D, 2], [1, D]])
            wld_v[j] = bld.dma("ql", outap, inap)

        pcst = {}
        for gci in range(32):
            g_, ci_ = gci // 8, gci % 8
            if gci >= 4:
                bld.wait("gpsimd", "ql", pcst[gci - 4])
            cv_ = bld.dma("qg", wbuf[:, gci % 4, 0:D], w1d[g_, ci_, :, :])
            bld.wait("sync", "qg", cv_)
            pcst[gci] = bld.dma("ql", w1b[g_, ci_, :, :], wbuf[:, gci % 4, 0:D])

        BT2 = [bT2, bT2b]
        XNB = [xb_, xb2_]
        vA = {}; vB = {}; vC1 = {}; vSq = {}; vC2 = {}; vD = {}; vE = {}
        vH = {-1: 0, -2: 0}; vSt = {-1: 0, -2: 0}
        vFg = {}; vG = {}

        def sl(t):
            return (t % 2) * 1536

        def slb(t):
            return (t % 2) * 768

        def p1_A(t):
            c0 = t * TT1
            for ci in range(NCH):
                if ci == 0:
                    bld.wait("vector", "mT", vB.get(t - 1, 0))
                bld.v("tensor_copy", xbpre[:, ci * TT1:(ci + 1) * TT1],
                      xres[:, ci * TL + c0: ci * TL + c0 + TT1])
            for ci in range(NCH):
                vA[t] = bld.v("tensor_mul", sqn[:, ci * TT1:(ci + 1) * TT1],
                              xbpre[:, ci * TT1:(ci + 1) * TT1],
                              xbpre[:, ci * TT1:(ci + 1) * TT1],
                              inc=("mV" if ci == NCH - 1 else None))

        def p1_B(t):
            bld.wait("tensor", "mV", vA[t])
            bld.wait("tensor", "mV", vC1.get(t - 1, 0))
            bld.wait("tensor", "mS", vG.get((t - 2, 3), 0))
            for ci in range(NCH):
                bld.mm(psB[0:1, 1536:1536 + TT1], onescol[:, :],
                       xbpre[:, ci * TT1:(ci + 1) * TT1], ci == 0, ci == NCH - 1)
            for ci in range(NCH):
                vB[t] = bld.mm(psB[0:1, 1792:1792 + TT1], onescol[:, :],
                               sqn[:, ci * TT1:(ci + 1) * TT1], ci == 0, ci == NCH - 1,
                               inc=("mT" if ci == NCH - 1 else None))

        def p1_C1(t):
            b = sl(t)
            m_ = srow[0:1, b + 512:b + 768]
            msq = srow[0:1, b + 768:b + 1024]
            var = srow[0:1, b + 1024:b + 1280]
            inv = srow[0:1, b + 1280:b + 1536]
            bld.wait("vector", "mT", vB[t])
            bld.v("tensor_scalar", m_, psB[0:1, 1536:1536 + TT1], cst[0:1, 1:2], None, ALU.mult)
            bld.v("tensor_scalar", msq, psB[0:1, 1792:1792 + TT1], cst[0:1, 1:2], None, ALU.mult)
            bld.v("tensor_mul", var, m_, m_)
            bld.v("tensor_sub", var, msq, var)
            bld.v("tensor_scalar", var, var, cst[0:1, 0:1], None, ALU.add)
            vC1[t] = bld.v("reciprocal", inv, var, inc="mV")

        def p1_sqrt(t):
            b = sl(t)
            bld.wait("scalar", "mV", vC1[t])
            vSq[t] = bld.act(srow[0:1, b:b + 256], srow[0:1, b + 1280:b + 1536],
                             ACT.Sqrt, inc="mS")

        def p1_C2(t):
            b = sl(t)
            bb = slb(t)
            arow = srow[0:1, b:b + 256]
            brow = srow[0:1, b + 256:b + 512]
            m_ = srow[0:1, b + 512:b + 768]
            var = srow[0:1, b + 1024:b + 1280]
            t1 = srow[0:1, b + 768:b + 1024]
            bld.wait("vector", "mS", vSq[t])
            bld.v("tensor_mul", t1, arow, arow)
            bld.v("tensor_mul", t1, t1, var)
            bld.v("tensor_scalar", t1, t1, cst[0:1, 6:7], 1.5, ALU.mult, ALU.add)
            bld.v("tensor_mul", arow, arow, t1)
            bld.v("tensor_mul", brow, m_, arow)
            bld.v("tensor_scalar", brow, brow, cst[0:1, 2:3], None, ALU.mult)
            bld.v("tensor_copy", browbf[0:1, bb:bb + 256], arow)
            ahif = srow[0:1, b + 1024:b + 1280]
            bld.v("tensor_copy", ahif, browbf[0:1, bb:bb + 256])
            alo = srow[0:1, b + 1280:b + 1536]
            bld.v("tensor_sub", alo, arow, ahif)
            bld.v("tensor_copy", browbf[0:1, bb + 256:bb + 512], alo)
            vC2[t] = bld.v("tensor_copy", browbf[0:1, bb + 512:bb + 768], brow, inc="mV")

        def p1_D(t):
            bb = slb(t)
            bld.wait("tensor", "mV", vC2[t])
            bld.wait("tensor", "mV", vE.get(t - 1, 0))
            bld.wait("tensor", "mS", vG.get((t - 2, 3), 0))
            bld.mm(psB[:, Abc_off:Abc_off + TT1], onesrow[0:1, 0:128],
                   browbf[0:1, bb:bb + 256], True, False)
            bld.mm(psB[:, Abc_off:Abc_off + TT1], onesrow[0:1, 0:128],
                   browbf[0:1, bb + 256:bb + 512], False, True)
            vD[t] = bld.mm(psB[:, Bbc_off:Bbc_off + TT1], onesrow[0:1, 0:128],
                           browbf[0:1, bb + 512:bb + 768], True, True, inc="mT")

        def p1_E(t):
            p = t % 2
            bld.wait("vector", "mT", vD[t])
            bld.wait("vector", "qg", vSt[t - 2])
            bld.wait("vector", "mT", vFg.get((t - 2, 3), 0))
            bld.v("tensor_tensor", v3(BT2[p], pstep=2080), xres_t(t), bc3(Abc_off), ALU.mult)
            bld.v("tensor_tensor", v3(BT2[p], pstep=2080), v3(BT2[p], pstep=2080),
                  bc3(Bbc_off), ALU.add)
            vE[t] = bld.v("tensor_copy", XNB[p][:, :], BT2[p][:, 0:2048], inc="mV")

        def p1_F(t):
            p = t % 2
            bld.wait("tensor", "mV", vE[t])
            for g in range(4):
                dst = gate_dst[g]
                for ci in range(NCH):
                    j = t * 16 + g * 4 + ci // 2
                    if ci % 2 == 0:
                        bld.wait("tensor", "ql", wld_v[j])
                    if ci == 0:
                        if g >= 2:
                            bld.wait("tensor", "mS", vG[(t, g - 2)])
                        elif t > 0:
                            bld.wait("tensor", "mS", vG[(t - 1, g + 2)])
                    for e in range(NCH):
                        tv = bld.mm(dst[:, e * TT1:(e + 1) * TT1],
                                    wbuf[:, j % 4, (ci % 2) * D + e * 128:
                                         (ci % 2) * D + (e + 1) * 128],
                                    XNB[p][:, ci * TT1:(ci + 1) * TT1],
                                    ci == 0 and e % 2 == 0, False,
                                    inc=("mT" if e == NCH - 1 else None))
                    if ci % 2 == 1:
                        wuse_v[j] = tv
                        w1_issue_next()
                for e in range(NCH):
                    tv = bld.mm(dst[:, e * TT1:(e + 1) * TT1],
                                b1row[0:1, g * D + e * 128: g * D + (e + 1) * 128],
                                onesrow[0:1, 0:TT1], False, True,
                                inc=("mT" if e == NCH - 1 else None))
                vFg[(t, g)] = tv
                bld.wait("scalar", "mT", vFg[(t, g)])
                if g == 0:
                    bld.wait("scalar", "mV", vH[t - 1])
                vG[(t, g)] = bld.act(gate_buf[g][:, :], gate_dst[g][:, :],
                                     gate_fun[g], inc="mS")

        def p1_H(t):
            p = t % 2
            bld.wait("vector", "mS", vG[(t, 3)])
            bld.v("tensor_mul", bI[:, :], bI[:, :], bS[:, :])
            for ci in range(NCH):
                bld.v("tensor_tensor_scan", bS[:, ci * TT1:(ci + 1) * TT1],
                      bF[:, ci * TT1:(ci + 1) * TT1], bI[:, ci * TT1:(ci + 1) * TT1],
                      hcar[:, ci:ci + 1], ALU.mult, ALU.add)
            for ci in range(NCH):
                bld.v("tensor_tensor_scan", bI[:, ci * TT1:(ci + 1) * TT1],
                      bF[:, ci * TT1:(ci + 1) * TT1], ones2k[:, 0:TT1],
                      fcar[:, ci:ci + 1], ALU.mult, ALU.mult)
            hS = bass.AP(bS, TT1 - 1, [[2048, 128], [TT1, NCH]])
            hI = bass.AP(bI, TT1 - 1, [[2048, 128], [TT1, NCH]])
            bld.v("tensor_copy", hcar[:, :], hS)
            bld.v("tensor_copy", fcar[:, :], hI)
            bld.wait("vector", "qg", vSt[t - 1])
            bld.v("tensor_mul", gbf[:, :], bO[:, :], bI[:, :])
            bld.v("tensor_mul", bO[:, :], bO[:, :], bS[:, :])
            vH[t] = bld.v("tensor_add", BT2[p][:, 0:2048], BT2[p][:, 0:2048],
                          bO[:, :], inc="mV")
            bld.wait("gpsimd", "mV", vH[t])
            bld.dma("qg", zscr[:, t * 2048:(t + 1) * 2048], BT2[p][:, 0:2048])
            vSt[t] = bld.dma("qg", gscr[:, t * 2048:(t + 1) * 2048], gbf[:, :])

        w1_issue_next()
        w1_issue_next()
        p1_A(0)
        p1_B(0)
        p1_C1(0)
        p1_sqrt(0)
        for it in range(NT1 + 2):
            if it < NT1:
                p1_C2(it)
            if it + 1 < NT1:
                p1_A(it + 1)
            if it < NT1:
                p1_D(it)
            if 0 <= it - 1 < NT1:
                p1_F(it - 1)
            if it + 1 < NT1:
                p1_B(it + 1)
            if it < NT1:
                p1_E(it)
            if 0 <= it - 1 < NT1:
                p1_H(it - 1)
            if it + 1 < NT1:
                p1_C1(it + 1)
                p1_sqrt(it + 1)

        bld.v("tensor_copy", ctmp[:, 0:8], hcar[:, :])
        cv = bld.v("tensor_copy", ctmp[:, 8:16], fcar[:, :], inc="mV")
        bld.wait("gpsimd", "mV", cv)
        bld.dma("qg", ag1i[:, :], ctmp[:, 0:16])
        bld.wait("gpsimd", "qg")
        bld.g("collective_compute", "AllGather", ALU.bypass,
              replica_groups=rg, ins=[ag1i.ap().opt()], outs=[ag1o.ap().opt()], inc="cc")
        bld.wait("sync", "cc")
        bld.dma("ql", carrs[:, :, :], bass.AP(ag1o, 0, [[16, 128], [128 * 16, 8], [1, 16]]))
        bld.wait("vector", "ql")
        for ci in range(NCH):
            bld.v("tensor_copy", ctmp[:, 0:8], carrs[:, :, 8 + ci])
            bld.v("tensor_copy", ctmp[:, 8:16], carrs[:, :, ci])
            bld.v("tensor_tensor_scan", ctmp[:, 16:24], ctmp[:, 0:8], ctmp[:, 8:16],
                  0.0, ALU.mult, ALU.add)
            bld.v("tensor_mul", ctmp[:, 16:24], ctmp[:, 16:24], selm[:, :])
            bld.v("reduce_sum", h_in[:, ci:ci + 1], ctmp[:, 16:24], axis=AXX)
        bld.barrier()
        if dbg == "hin":
            bld.v("tensor_copy", bT2[:, 0:8], h_in[:, :])
            bld.v("tensor_copy", bT2[:, 8:16], hcar[:, :])
            bld.v("tensor_copy", bT2[:, 16:24], fcar[:, :])
            hv_ = bld.v("tensor_copy", bT2[:, 24:152], carrs[:, 0:8, :], inc="mV")
            bld.wait("gpsimd", "mV", hv_)
            dvv = bld.dma("qg", zscr[:, 0:2048], bT2[:, 0:2048])
            bld.wait("sync", "qg", dvv)
            bld.barrier()

        for tt in range(NT1):
            c0 = tt * TT1
            bld.wait("sync", "mV")
            bld.dma("ql", bT2[:, 0:2048], zscr[:, tt * 2048:(tt + 1) * 2048])
            gv = bld.dma("ql", gbf[:, :], gscr[:, tt * 2048:(tt + 1) * 2048])
            bld.wait("vector", "ql", gv)
            for ci in range(NCH):
                bld.v("scalar_tensor_tensor", bT2[:, ci * TT1:(ci + 1) * TT1],
                      gbf[:, ci * TT1:(ci + 1) * TT1], h_in[:, ci:ci + 1],
                      bT2[:, ci * TT1:(ci + 1) * TT1], ALU.mult, ALU.add)
            ln_stats(lambda ci: bT2[:, ci * TT1:(ci + 1) * TT1], xb_, sqn)
            bld.v("tensor_tensor", v3(bF), v3(bT2, pstep=2080), bc3(Abc_off), ALU.mult)
            bld.v("tensor_tensor", v3(bF), v3(bF), bc3(Bbc_off), ALU.add)
            for ci in range(NCH):
                bld.v("tensor_scalar", bF[:, ci * TT1:(ci + 1) * TT1],
                      bF[:, ci * TT1:(ci + 1) * TT1], gam[:, ci:ci + 1], bet[:, ci:ci + 1],
                      ALU.mult, ALU.add)
            bld.v("tensor_tensor", xres_t(tt), xres_t(tt), v3(bF), ALU.add)
            ln_stats(lambda ci: xres[:, ci * TL + c0: ci * TL + c0 + TT1], xb_, sqn)
            bld.v("tensor_copy", a3hl[0:1, c0:c0 + TT1], browbf[0:1, 0:TT1])
            bld.v("tensor_copy", a3hl[0:1, TL + c0:TL + c0 + TT1], browbf[0:1, 256:256 + TT1])
            bld.v("tensor_copy", b3b[0:1, c0:c0 + TT1], browbf[0:1, 512:512 + TT1])

        def halo_ag(ahl_, bb_, agi, ago, dest):
            bld.v("tensor_copy", srow[0:1, 2048:2049], ahl_[0:1, TL - 1:TL])
            bld.v("tensor_copy", srow[0:1, 2049:2050], ahl_[0:1, 2 * TL - 1:2 * TL])
            bld.v("tensor_add", srow[0:1, 2048:2049], srow[0:1, 2048:2049],
                  srow[0:1, 2049:2050])
            vv = bld.v("tensor_copy", browbf[0:1, 0:1], srow[0:1, 2048:2049], inc=None)
            vv = bld.v("tensor_copy", browbf[0:1, 1:2], bb_[0:1, TL - 1:TL], inc="mV")
            bld.wait("tensor", "mV", vv)
            tv = bld.mm(psB[:, 1024:1026], onesrow[0:1, 0:128], browbf[0:1, 0:2],
                        True, True, inc="mT")
            bld.wait("vector", "mT", tv)
            xlast = bass.AP(xres, TL - 1, [[NCH * TL, 128], [TL, NCH]])
            bld.v("tensor_tensor", ctmp[:, 0:8], xlast,
                  bass.AP(psB, 1025, [[2048, 128], [0, NCH]]), ALU.add)
            vv = bld.v("tensor_tensor", ctmp[:, 0:8], ctmp[:, 0:8],
                       bass.AP(psB, 1024, [[2048, 128], [0, NCH]]), ALU.mult, inc="mV")
            bld.wait("gpsimd", "mV", vv)
            bld.dma("qg", agi[:, :], ctmp[:, 0:8])
            bld.wait("gpsimd", "qg")
            bld.g("collective_compute", "AllGather", ALU.bypass,
                  replica_groups=rg, ins=[agi.ap().opt()], outs=[ago.ap().opt()], inc="cc")
            bld.wait("sync", "cc")
            bld.dma("ql", carrs[:, :, 0:8], bass.AP(ago, 0, [[8, 128], [128 * 8, 8], [1, 8]]))
            bld.wait("vector", "ql")
            for ci in range(NCH):
                bld.v("tensor_mul", ctmp[:, 0:8], carrs[:, :, ci], selm[:, :])
                bld.v("reduce_sum", dest[:, ci:ci + 1], ctmp[:, 0:8], axis=AXX)
            bld.barrier()

        halo_ag(a3hl, b3b, ag2i, ag2o, haloc)

        def ffn_dbg(buf, tt, n=2048):
            bld.barrier()
            dv = bld.dma("qg", zscr[:, tt * 2048:tt * 2048 + n], buf)
            bld.wait("vector", "qg", dv)
            bld.barrier()

        def ffn_stage(arow_, brow_, halosel, ffw, out_final=False):
            for tt in range(NT1):
                c0 = tt * TT1
                av = srow[0:1, 512:512 + TT1]
                bv = srow[0:1, 768:768 + TT1]
                bld.v("tensor_copy", av, arow_[0:1, c0:c0 + TT1])
                bld.v("tensor_copy", bv, brow_[0:1, c0:c0 + TT1])
                bcast_rows(av, bv)
                if dbg == "ffnbc":
                    bld.v("tensor_copy", bF[:, 0:256], psB[:, 1024:1280])
                    bcv = bld.v("tensor_copy", bF[:, 256:512], psB[:, 1280:1536], inc="mV")
                    bld.wait("gpsimd", "mV", bcv)
                    dv_ = bld.dma("qg", zscr[:, tt * 512:(tt + 1) * 512], bF[:, 0:512])
                    bld.wait("vector", "qg", dv_)
                    bld.barrier()
                xnp = ap3(bT2, 2080, 257, TT1, off=1)
                bld.v("tensor_tensor", xnp, xres_t(tt), bc3(Abc_off), ALU.mult)
                apv = bld.v("tensor_tensor", xnp, xnp, bc3(Bbc_off), ALU.add, inc="mV")
                if dbg == "ffnxn0":
                    bld.wait("gpsimd", "mV", apv)
                    dv0 = bld.dma("qg", zscr[:, tt * 2048:(tt + 1) * 2048], bT2[:, 0:2048])
                    bld.wait("vector", "qg", dv0)
                    bld.barrier()
                hc = bass.AP(bT2, 0, [[2080, 128], [257, NCH]])
                if tt == 0:
                    bld.v("tensor_copy", hc, halosel[:, :])
                else:
                    bld.v("tensor_copy", hc, halocar[:, :])
                lc = bass.AP(bT2, TT1, [[2080, 128], [257, NCH]])
                bld.v("tensor_copy", halocar[:, :], lc)
                xall = ap3(bT2, 2080, 257, TT1, off=1)
                xprev = ap3(bT2, 2080, 257, TT1, off=0)
                if dbg in ("ffnxn", "ffnxn0"):
                    ffn_dbg(bT2[:, 0:2048], tt)
                bld.v("tensor_tensor", v3(bF), xall, xprev, ALU.subtract)
                for ci in range(NCH):
                    xp_ci = bass.AP(bT2, ci * 257, [[2080, 128], [1, TT1]])
                    bld.v("scalar_tensor_tensor", xb_[:, ci * TT1:(ci + 1) * TT1],
                          bF[:, ci * TT1:(ci + 1) * TT1], mur[:, ci:ci + 1], xp_ci,
                          ALU.mult, ALU.add)
                for ci in range(NCH):
                    xp_ci = bass.AP(bT2, ci * 257, [[2080, 128], [1, TT1]])
                    vv = bld.v("scalar_tensor_tensor", sqn[:, ci * TT1:(ci + 1) * TT1],
                               bF[:, ci * TT1:(ci + 1) * TT1], muk[:, ci:ci + 1], xp_ci,
                               ALU.mult, ALU.add, inc=("mV" if ci == NCH - 1 else None))
                bld.wait("tensor", "mV", vv)
                for ci in range(NCH):
                    for e in range(NCH):
                        tv = bld.mm(psA[:, e * TT1:(e + 1) * TT1],
                                    ffw[:, 0 * NCH * D + ci * D + e * 128: 0 * NCH * D + ci * D + (e + 1) * 128],
                                    xb_[:, ci * TT1:(ci + 1) * TT1],
                                    ci == 0 and e % 2 == 0, ci == NCH - 1,
                                    inc=("mT" if (ci == NCH - 1 and e == NCH - 1) else None))
                bld.wait("scalar", "mT", tv)
                rv = bld.act(bI[:, :], psA[:, :], ACT.Sigmoid, inc="mS")
                if dbg == "ffnr":
                    bld.wait("vector", "mS", rv)
                    ffn_dbg(bI[:, :], tt)
                for ci in range(NCH):
                    for e in range(NCH):
                        tv = bld.mm(psB[:, e * TT1:(e + 1) * TT1],
                                    ffw[:, 1 * NCH * D + ci * D + e * 128: 1 * NCH * D + ci * D + (e + 1) * 128],
                                    sqn[:, ci * TT1:(ci + 1) * TT1],
                                    ci == 0 and e % 2 == 0, ci == NCH - 1,
                                    inc=("mT" if (ci == NCH - 1 and e == NCH - 1) else None))
                bld.wait("vector", "mT", tv)
                bld.v("tensor_scalar", bS[:, :], psB[:, 0:2048], cst[:, 5:6], None, ALU.max)
                if dbg == "ffnk":
                    ffn_dbg(bS[:, :], tt)
                kav = bld.v("tensor_mul", sqn[:, :], bS[:, :], bS[:, :], inc="mV")
                bld.wait("tensor", "mV", kav)
                bld.wait("tensor", "mS", rv)
                for ci in range(NCH):
                    for e in range(NCH):
                        tv = bld.mm(psA[:, e * TT1:(e + 1) * TT1],
                                    ffw[:, 2 * NCH * D + ci * D + e * 128: 2 * NCH * D + ci * D + (e + 1) * 128],
                                    sqn[:, ci * TT1:(ci + 1) * TT1],
                                    ci == 0 and e % 2 == 0, ci == NCH - 1,
                                    inc=("mT" if (ci == NCH - 1 and e == NCH - 1) else None))
                bld.wait("vector", "mT", tv)
                bld.wait("vector", "mS", rv)
                bld.v("tensor_tensor", bF[:, :], bI[:, :], psA[:, :], ALU.mult, inc="mV")
                if dbg == "ffnout":
                    ffn_dbg(bF[:, :], tt)
                fv = bld.v("tensor_tensor", xres_t(tt), xres_t(tt), v3(bF), ALU.add, inc="mV")
                if out_final:
                    bld.wait("gpsimd", "mV", fv)
                    xt = bass.AP(outd, tt * TT1, [[NCH * TL, 128], [TL, NCH], [1, TT1]])
                    bld.dma("qg", xt, xres_t(tt))
            bld.barrier()

        bld.flush()
        st1b.close()
        if upto >= 2:
            ffw = sb1("ffw", [128, 3 * NCH * D], BF16)
            for m in range(3):
                for ci in range(NCH):
                    wv = bld.dma("qg", ffw[:, m * NCH * D + ci * D:m * NCH * D + (ci + 1) * D], wfd[m, ci, :, :])
            bld.wait("tensor", "qg", wv)
            bld.wait("vector", "qg", wv)
            bld.wait("sync", "qg", wv)
            for m in range(3):
                for ci in range(NCH):
                    bld.dma("ql", wfb[m, ci, :, :],
                            ffw[:, m * NCH * D + ci * D:m * NCH * D + (ci + 1) * D])
            ffn_stage(a3hl, b3b, haloc, ffw, out_final=False)

        if upto >= 3:
            bld.flush()
            st1.close()
            s3 = ExitStack()
            sb3 = lambda name, shape, dt: s3.enter_context(
                nc.sbuf_tensor("s3_" + name, shape, dt))
            kT = sb3("kT", [128, T], BF16)
            vaug = sb3("vaug", [128, 130 * 128], BF16)
            wqkv = sb3("wqkv", [128, 3 * NCH * 128], BF16)
            masks = sb3("masks", [128, 4 * 512], BF16)
            bqk = sb3("bqk", [128, 2], F32)
            x2nb = sb3("x2nb", [128, 2048], BF16)
            s3pre = ExitStack()
            xb3 = s3pre.enter_context(nc.sbuf_tensor("s3_xb3", [128, 2048], BF16))
            sq3 = s3pre.enter_context(nc.sbuf_tensor("s3_sq3", [128, 2048], BF16))
            t3 = s3pre.enter_context(nc.sbuf_tensor("s3_t3", [128, 2048], F32))

            mv_ = bld.dma("ql", t3[:, :], maskd[:, :])
            bld.dma("ql", bqk[:, :], bqkd[:, :])
            lv = bld.dma("ql", srow[0:1, 0:128], bvrowd[:, :])
            bld.wait("vector", "ql", lv)
            bld.v("tensor_copy", masks[:, :], t3[:, :])
            bvv = bld.v("tensor_copy", browbf[0:1, 1920:2048], srow[0:1, 0:128], inc="mV")
            for m in range(3):
                for ci in range(NCH):
                    wv3 = bld.dma("qg", wqkv[:, m * 1024 + ci * 128:m * 1024 + (ci + 1) * 128],
                                  wqkvd[m, ci, :, :])
            bld.v("memset", bass.AP(vaug, 64, [[130 * 128, 128], [130, 128], [65, 2]]), 1.0)

            for tt in range(NT1):
                c0 = tt * TT1
                ln_stats(lambda ci: xres[:, ci * TL + c0: ci * TL + c0 + TT1], xb3, sq3)
                bld.v("tensor_tensor", v3(t3), xres_t(tt), bc3(Abc_off), ALU.mult)
                xnv3 = bld.v("tensor_tensor", v3(x2nb), v3(t3), bc3(Bbc_off), ALU.add,
                             inc="mV")
                bld.wait("gpsimd", "mV", xnv3)
                agp = bass.AP(ag4i, tt * TT1, [[NCH * TL, 128], [TL, NCH], [1, TT1]])
                agv = bld.dma("qg", agp, v3(x2nb))
                bld.wait("vector", "qg", agv)
            bld.wait("gpsimd", "qg")
            bld.g("collective_compute", "AllGather", ALU.bypass,
                  replica_groups=rg, ins=[ag4i.ap().opt()], outs=[ag4o.ap().opt()],
                  inc="cc")
            bld.wait("sync", "cc")
            bld.wait("tensor", "qg", wv3)
            bld.wait("tensor", "mV", bvv)
            bld.barrier()
            bld.flush()
            s3pre.close()
            xg = sb3("xg", [128, 2 * NCH * 512], BF16)
            pTA = sb3("pTA", [128, 2 * 2048], BF16)
            pTB = sb3("pTB", [128, 2 * 1024], BF16)
            qt2 = sb3("qt2", [128, 2 * 512], BF16)
            osb = sb3("osb", [64, 1024], F32)
            oTb = sb3("oTb", [64, 2 * 1024], BF16)

            xg_v = {}

            def xg_load(gt):
                r_, off = gt // 4, (gt % 4) * 512
                slot = gt % 2
                inap = bass.AP(ag4o, (r_ * 128) * (NCH * TL) + off,
                               [[NCH * TL, 128], [TL, NCH], [1, 512]])
                bld.wait("sync", "mT")
                xg_v[gt] = bld.dma("ql", ap3(xg, 2 * NCH * 512, 512, 512,
                                             off=slot * NCH * 512), inap)

            xg_load(0)
            xg_load(1)
            ev_q = ev_k = ev_v = 0
            for gt in range(32):
                slot = gt % 2
                xgs = lambda ci: xg[:, slot * NCH * 512 + ci * 512:
                                    slot * NCH * 512 + (ci + 1) * 512]
                bld.wait("tensor", "ql", xg_v[gt])
                if gt >= 1:
                    bld.wait("tensor", "mS", ev_k)
                    bld.wait("tensor", "mV", ev_v)
                for ci in range(NCH):
                    bld.mm(psA[:, 0:512], wqkv[:, 0 * 1024 + ci * 128:0 * 1024 + (ci + 1) * 128],
                           xgs(ci), ci == 0, ci == NCH - 1)
                for ci in range(NCH):
                    qkv_t = bld.mm(psA[:, 512:1024],
                                   wqkv[:, 1 * 1024 + ci * 128:1 * 1024 + (ci + 1) * 128],
                                   xgs(ci), ci == 0, ci == NCH - 1,
                                   inc=("mT" if ci == NCH - 1 else None))
                for sub in range(4):
                    for ci in range(NCH):
                        bld.mm(psB[:, sub * 128:(sub + 1) * 128],
                               bass.AP(xg, slot * NCH * 512 + ci * 512 + sub * 128,
                                       [[2 * NCH * 512, 128], [1, 128]]),
                               wqkv[:, 2 * 1024 + ci * 128:2 * 1024 + (ci + 1) * 128],
                               sub == 0 and ci == 0, False)
                    vt = bld.mm(psB[:, sub * 128:(sub + 1) * 128],
                                onesrow[0:1, 0:128], browbf[0:1, 1920:2048],
                                False, True, inc=("mT" if sub == 3 else None))
                bld.wait("scalar", "mT", qkv_t)
                qslot = gt % 2
                if gt >= 2:
                    bld.wait("scalar", "qg", qst_v[gt - 2])
                bld.act(x2nb[:, qslot * 512:(qslot + 1) * 512], psA[:, 0:512],
                        ACT.Identity, bias=bqk[:, 0:1])
                ev_k = bld.act(kT[:, gt * 512:(gt + 1) * 512], psA[:, 512:1024],
                               ACT.Identity, bias=bqk[:, 1:2], inc="mS")
                bld.wait("gpsimd", "mS", ev_k)
                qst_v = qst_v if gt else {}
                qst_v[gt] = bld.dma("qg", qscr[:, gt * 512:(gt + 1) * 512],
                                    x2nb[:, qslot * 512:(qslot + 1) * 512])
                bld.wait("vector", "mT", vt)
                for sub in range(4):
                    tb = (gt * 4 + sub) * 130
                    ev_v = bld.v("tensor_copy",
                                 bass.AP(vaug, tb, [[130 * 128, 128], [65, 2], [1, 64]]),
                                 bass.AP(psB, sub * 128, [[2048, 128], [64, 2], [1, 64]]),
                                 inc=("mV" if sub == 3 else None))
                if gt < 30:
                    xg_load(gt + 2)

            if dbg == "att_xg":
                bld.barrier()
                dvx = bld.dma("qg", zscr[:, :],
                              bass.AP(ag4o, 0, [[NCH * TL, 128], [1, NCH * TL]]))
                bld.wait("vector", "qg", dvx)
                bld.barrier()
            if dbg in ("att_k", "att_q", "att_v"):
                bld.barrier()
                src = {"att_k": kT[:, :], "att_q": None, "att_v": None}[dbg] \
                    if dbg == "att_k" else None
                if dbg == "att_k":
                    dvk = bld.dma("qg", zscr[:, :], kT[:, :])
                elif dbg == "att_q":
                    dvk = bld.dma("qg", zscr[:, :], bass.AP(qscr, 0, [[T, 128], [1, T]]))
                else:
                    dvk = bld.dma("qg", zscr[:, :], vaug[:, 0:T])
                bld.wait("vector", "qg", dvk)
                bld.barrier()

            qld = {}

            def q_load(qt):
                qld[qt] = bld.dma("ql", qt2[:, (qt % 2) * 512:(qt % 2 + 1) * 512],
                                  qscr[:, qt * 512:(qt + 1) * 512])

            osb_v = bld.cnt["mV"]
            bld.wait("sync", "qg", qst_v[0])
            q_load(0)
            bld.wait("sync", "qg", qst_v[1])
            q_load(1)
            exp_v = {}
            qk_done = {}
            ost_v = {}
            exp_last = {0: 0, 1: 0, 2: 0}
            pt_guard = {0: 0, 1: 0}
            REG = {0: (psA, 0), 1: (psA, 1024), 2: (psB, 1024)}
            for qt in range(32):
                nkt = 4 * qt + 4
                qs = qt % 2
                bld.wait("tensor", "ql", qld[qt])
                pend_pv = []
                pv_done = {}
                ngrp = (nkt + 2) // 3
                for m in range(ngrp + 1):
                    if m < ngrp:
                        grp = [kt for kt in range(3 * m, min(3 * m + 3, nkt))]
                        par = m % 2
                        ev = {}
                        for kt in grp:
                            r = kt % 3
                            ps_, off_ = REG[r]
                            bld.wait("tensor", "mS", exp_last[r])
                            if kt < 3 and qt > 0 and r != 2:
                                bld.wait("tensor", "mV", osb_v)
                            bld.mm(ps_[:, off_:off_ + 512],
                                   kT[0:64, kt * 128:(kt + 1) * 128],
                                   qt2[0:64, qs * 512:(qs + 1) * 512], True, True)
                            qkv_ = bld.mm(ps_[:, off_ + 512:off_ + 1024],
                                          kT[64:128, kt * 128:(kt + 1) * 128],
                                          qt2[64:128, qs * 512:(qs + 1) * 512], True, True,
                                          inc="mT")
                            if kt % 3 == 1 or kt == nkt - 1 or kt % 3 == 2:
                                pass
                            ev[kt] = qkv_
                        amem = [kt for kt in grp if kt % 3 != 2]
                        bmem = [kt for kt in grp if kt % 3 == 2]
                        if amem:
                            bld.wait("scalar", "mT", max(ev[amem[-1]], pt_guard[par]))
                            fd = 1024 * len(amem)
                            e_ = bld.act(pTA[:, par * 2048:par * 2048 + fd],
                                         psA[:, 0:fd], ACT.Exp,
                                         bias=cst[:, 3:4], scale=0.125, inc="mS")
                            for kt in amem:
                                exp_v[(qt, kt)] = e_
                                exp_last[kt % 3] = e_
                        if bmem:
                            kt = bmem[0]
                            bld.wait("scalar", "mT", max(ev[kt], pt_guard[par]))
                            e_ = bld.act(pTB[:, par * 1024:(par + 1) * 1024],
                                         psB[:, 1024:2048], ACT.Exp,
                                         bias=cst[:, 3:4], scale=0.125, inc="mS")
                            exp_v[(qt, kt)] = e_
                            exp_last[2] = e_
                        mk = {}
                        for kt in grp:
                            if kt >= 4 * qt:
                                a_ = kt - 4 * qt
                                bld.wait("vector", "mS", exp_v[(qt, kt)])
                                if kt % 3 == 2:
                                    pslice = ap3(pTB, 2 * 1024, 512, 512,
                                                 off=par * 1024, nc_=2)
                                else:
                                    pslice = ap3(pTA, 2 * 2048, 512, 512,
                                                 off=par * 2048 + (kt % 3) * 1024, nc_=2)
                                mk[kt] = bld.v("tensor_tensor", pslice, pslice,
                                               bass.AP(masks, a_ * 512,
                                                       [[4 * 512, 128], [0, 2], [1, 512]]),
                                               ALU.mult, inc="mV")
                        for kt in grp:
                            if kt % 3 == 2:
                                p0 = pTB[:, par * 1024:par * 1024 + 512]
                                p1 = pTB[:, par * 1024 + 512:par * 1024 + 1024]
                            else:
                                b0 = par * 2048 + (kt % 3) * 1024
                                p0 = pTA[:, b0:b0 + 512]
                                p1 = pTA[:, b0 + 512:b0 + 1024]
                            pend_pv.append((m, kt, p0, p1, exp_v[(qt, kt)], mk.get(kt)))
                    flushn = len(pend_pv) if m == ngrp else max(0, len(pend_pv) - 3)
                    for _ in range(flushn):
                        gm, kt, p0, p1, e_, mkv = pend_pv.pop(0)
                        bld.wait("tensor", "mS", e_)
                        if mkv:
                            bld.wait("tensor", "mV", mkv)
                        if kt == 0:
                            bld.wait("tensor", "mV", osb_v)
                        bld.mm(psB[0:65, 0:512], vaug[:, kt * 130:kt * 130 + 65],
                               p0, kt == 0, kt == nkt - 1)
                        pv_t = bld.mm(psB[0:65, 512:1024],
                                      vaug[:, kt * 130 + 65:kt * 130 + 130],
                                      p1, kt == 0, kt == nkt - 1, inc="mT")
                        pt_guard[(kt // 3) % 2] = pv_t
                qk_done[qt] = bld.cnt["mT"]
                if qt < 31:
                    if qt >= 1:
                        bld.wait("sync", "mT", qk_done[qt])
                    bld.wait("sync", "qg", qst_v[qt + 1])
                    q_load(qt + 1)
                bld.wait("vector", "mT", pv_t)
                bld.v("reciprocal", srow[0:1, 0:512], psB[64:65, 0:512])
                bld.v("reciprocal", srow[0:1, 512:1024], psB[64:65, 512:1024])
                bld.v("tensor_copy", browbf[0:1, 0:1024], srow[0:1, 0:1024])
                bld.v("tensor_copy", srow[0:1, 1024:2048], browbf[0:1, 0:1024])
                bld.v("tensor_sub", srow[0:1, 1024:2048], srow[0:1, 0:1024],
                      srow[0:1, 1024:2048])
                rc_v = bld.v("tensor_copy", browbf[0:1, 1024:2048], srow[0:1, 1024:2048],
                             inc="mV")
                bld.wait("tensor", "mV", rc_v)
                bld.wait("tensor", "mS")
                bld.mm(psA[0:64, 0:512], onesrow[0:1, 0:64], browbf[0:1, 0:512],
                       True, False)
                bld.mm(psA[0:64, 512:1024], onesrow[0:1, 0:64], browbf[0:1, 512:1024],
                       True, False)
                bld.mm(psA[0:64, 0:512], onesrow[0:1, 0:64], browbf[0:1, 1024:1536],
                       False, True)
                bc_t = bld.mm(psA[0:64, 512:1024], onesrow[0:1, 0:64],
                              browbf[0:1, 1536:2048], False, True, inc="mT")
                bld.wait("vector", "mT", bc_t)
                if qt >= 2:
                    bld.wait("vector", "qg", ost_v[qt - 2])
                bld.v("tensor_copy", osb[0:64, :], psB[0:64, 0:1024])
                osb_v = bld.v("tensor_mul", oTb[0:64, qs * 1024:(qs + 1) * 1024],
                              osb[0:64, :], psA[0:64, 0:1024], inc="mV")
                bld.wait("gpsimd", "mV", osb_v)
                blk, off = qt // 4, (qt % 4) * 512
                bld.dma("qg", a2ai[blk, 0:64, off:off + 512],
                        oTb[0:64, qs * 1024:qs * 1024 + 512])
                ost_v[qt] = bld.dma("qg", a2ai[blk, 64:128, off:off + 512],
                                    oTb[0:64, qs * 1024 + 512:qs * 1024 + 1024])

            bld.wait("gpsimd", "qg")
            if dbg == "att_o":
                bld.barrier()
                dvo = bld.dma("qg", zscr[:, :],
                              bass.AP(a2ai, 0, [[T, 128], [1, T]]))
                bld.wait("vector", "qg", dvo)
                bld.barrier()

            def _a2a(e, i_=a2ai, o_=a2ao, rg_=rg):
                with allow_ici_all_to_all():
                    return e.collective_compute(
                        "AllToAll", ALU.bypass, replica_groups=rg_,
                        ins=[i_.ap().opt()], outs=[o_.ap().opt()])

            bld.pend["gpsimd"].append(_a2a)
            bld._wrap_inc("gpsimd", "cc")
            bld.wait("sync", "cc")
            bld.barrier()
            bld.flush()
            s3.close()

            s3b = ExitStack()
            woT = s3b.enter_context(nc.sbuf_tensor("s3_woT", [128, NCH * D], BF16))
            obin = s3b.enter_context(nc.sbuf_tensor("s3_obin", [128, NCH * 512], BF16))
            obuf = s3b.enter_context(nc.sbuf_tensor("s3_obuf", [128, 2048], F32))
            bo3b = s3b.enter_context(nc.sbuf_tensor("s3_bo3b", [128, NCH], F32))
            bld.dma("ql", bo3b[:, :], bod[:, :])
            for ci in range(NCH):
                wo_v = bld.dma("qg", woT[:, ci * D:(ci + 1) * D], wod[ci, :, :])
            bld.wait("tensor", "qg", wo_v)
            ob_v = 0
            for tt4 in range(4):
                inap = bass.AP(a2ao, tt4 * 512, [[TL, 128], [128 * TL, NCH], [1, 512]])
                bld.wait("sync", "mT")
                ov = bld.dma("ql", ap3(obin, NCH * 512, 512, 512), inap)
                bld.wait("tensor", "ql", ov)
                if tt4 >= 1:
                    bld.wait("tensor", "mV", ob_v)
                for do in range(NCH):
                    ps = psA if do < 4 else psB
                    col = (do % 4) * 512
                    for oc in range(NCH):
                        op_t = bld.mm(ps[:, col:col + 512],
                                      woT[:, oc * D + do * 128:oc * D + (do + 1) * 128],
                                      obin[:, oc * 512:(oc + 1) * 512], oc == 0,
                                      oc == NCH - 1,
                                      inc=("mT" if oc == NCH - 1 else None))
                bld.wait("scalar", "mT", op_t)
                for half in range(2):
                    bld.wait("scalar", "mV", ob_v)
                    for d4 in range(4):
                        do = half * 4 + d4
                        ps = psA if half == 0 else psB
                        oev = bld.act(obuf[:, d4 * 512:(d4 + 1) * 512],
                                      ps[:, d4 * 512:(d4 + 1) * 512], ACT.Identity,
                                      bias=bo3b[:, do:do + 1],
                                      inc=("mS" if d4 == 3 else None))
                    bld.wait("vector", "mS", oev)
                    xh = bass.AP(xres, half * 4 * TL + tt4 * 512,
                                 [[NCH * TL, 128], [TL, 4], [1, 512]])
                    ob_v = bld.v("tensor_tensor", xh, xh,
                                 ap3(obuf, 2048, 512, 512, nc_=4), ALU.add, inc="mV")
            bld.barrier()
            bld.flush()
            s3b.close()

        if upto >= 4:
            s4 = ExitStack()
            sb4 = lambda name, shape, dt: s4.enter_context(
                nc.sbuf_tensor("s4_" + name, shape, dt))
            bF = sb4("bF", [128, 2048], F32)
            bI = sb4("bI", [128, 2048], F32)
            bS = sb4("bS", [128, 2048], F32)
            bT2 = sb4("bT2", [128, 2080], F32)
            xb_ = sb4("xb", [128, 2048], BF16)
            sqn = sb4("sqn", [128, 2048], BF16)
            gbf = sb4("gbf", [128, 2048], BF16)
            ffw = sb4("ffw", [128, 3 * NCH * D], BF16)
            a3hl = sb4("a3hl", [1, 2 * TL], BF16)
            b3b = sb4("b3b", [1, TL], BF16)
            for m in range(3):
                for ci in range(NCH):
                    wv4 = bld.dma("ql", ffw[:, m * NCH * D + ci * D:m * NCH * D + (ci + 1) * D],
                                  wfb[m, ci, :, :])
            bld.wait("tensor", "ql", wv4)
            bld.wait("vector", "ql", wv4)
            for tt in range(NT1):
                c0 = tt * TT1
                if tt > 0:
                    bld.wait("vector", "mT")
                ln_stats(lambda ci: xres[:, ci * TL + c0: ci * TL + c0 + TT1], xb_, sqn)
                bld.v("tensor_copy", a3hl[0:1, c0:c0 + TT1], browbf[0:1, 0:TT1])
                bld.v("tensor_copy", a3hl[0:1, TL + c0:TL + c0 + TT1], browbf[0:1, 256:256 + TT1])
                bld.v("tensor_copy", b3b[0:1, c0:c0 + TT1], browbf[0:1, 512:512 + TT1])
            halo_ag(a3hl, b3b, ag3i, ag3o, haloc)
            ffn_stage(a3hl, b3b, haloc, ffw, out_final=True)
            bld.wait("gpsimd", "qg")
            bld.flush()
            s4.close()

        if upto < 4:
            bld.barrier()
            if dbg is not None:
                bld.dma("qg", outd[:, :], zscr[:, :])
            else:
                for tt in range(NT1):
                    xt = bass.AP(outd, tt * TT1, [[NCH * TL, 128], [TL, NCH], [1, TT1]])
                    bld.dma("qg", xt, xres_t(tt))
            bld.wait("gpsimd", "qg")
            bld.flush()
            if upto < 3:
                st1.close()
        perm.close()
    return nc


def prep_inputs(core, inputs):
    c = core
    x = np.asarray(inputs["L0_attn_a_x"]).reshape(T, D)
    d = {}
    d["x0"] = fm_pack(np.asarray(x[c * TL:(c + 1) * TL], np.float32))
    w1 = np.stack([np.asarray(inputs["L0_attn_a_Wi"]), np.asarray(inputs["L0_attn_a_Wf"]),
                   np.asarray(inputs["L0_attn_a_Wo"]), np.asarray(inputs["L0_attn_a_Ws"])])
    d["w1"] = np.ascontiguousarray(w1.reshape(4, NCH, 128, D), np.float32)
    d["b1"] = np.ascontiguousarray(np.concatenate(
        [np.asarray(inputs["L0_attn_a_bi"]), np.asarray(inputs["L0_attn_a_bf"]),
         np.asarray(inputs["L0_attn_a_bo"]), np.asarray(inputs["L0_attn_a_bs"])])[None, :],
        np.float32)
    pp = lambda v: np.ascontiguousarray(np.asarray(v).reshape(NCH, 128).T, np.float32)
    d["gam"] = pp(inputs["L0_attn_a_gamma"])
    d["bet"] = pp(inputs["L0_attn_a_beta"])
    d["mur"] = pp(inputs["L0_ffn_mu_r"])
    d["muk"] = pp(inputs["L0_ffn_mu_k"])
    wf = np.stack([np.asarray(inputs["L0_ffn_W_r"]).T,
                   np.asarray(inputs["L0_ffn_W_k"]).T,
                   np.asarray(inputs["L0_ffn_W_v"]).T])
    d["wf"] = np.ascontiguousarray(wf.reshape(3, NCH, 128, D), np.float32)
    r0, r1 = 128 * c, 128 * (c + 1)
    wq = np.asarray(inputs["L0_attn_b_Wq"]).T[:, r0:r1]
    wk = np.asarray(inputs["L0_attn_b_Wk"]).T[:, r0:r1]
    wv = np.asarray(inputs["L0_attn_b_Wv"]).T[:, r0:r1]
    d["wqkv"] = np.ascontiguousarray(np.stack([wq, wk, wv]).reshape(3, NCH, 128, 128),
                                     np.float32)
    d["bqk"] = np.ascontiguousarray(
        np.stack([np.asarray(inputs["L0_attn_b_bq"])[r0:r1],
                  np.asarray(inputs["L0_attn_b_bk"])[r0:r1]], axis=1), np.float32)
    d["bvrow"] = np.ascontiguousarray(np.asarray(inputs["L0_attn_b_bv"])[None, r0:r1],
                                      np.float32)
    d["wo"] = np.ascontiguousarray(
        np.asarray(inputs["L0_attn_b_Wo"]).T.reshape(NCH, 128, D), np.float32)
    d["bo"] = pp(inputs["L0_attn_b_bo"])
    mk = np.zeros((128, 4, 512), np.float32)
    q = np.arange(512)
    for a in range(4):
        for k in range(128):
            mk[k, a] = (q >= 128 * a + k)
    d["mask"] = np.ascontiguousarray(mk.reshape(128, 4 * 512))
    sel = np.zeros((128, 8), np.float32)
    if c > 0:
        sel[:, c - 1] = 1.0
    d["sel"] = sel
    return d


def kernel(**inputs):
    nc = build(_UPTO, _DBG)
    in_maps = [prep_inputs(c, inputs) for c in range(NCORE)]
    res = run_bass_kernel_spmd(nc, in_maps, core_ids=list(range(NCORE)))
    if _DBG is not None:
        return [res.results[c]["out"] for c in range(NCORE)]
    shards = [fm_unpack(res.results[c]["out"], TL) for c in range(NCORE)]
    return np.concatenate(shards, axis=0).reshape(1, T, D).astype(np.float32)
